# version 07
# baseline (speedup 1.0000x reference)
import numpy as np
import ml_dtypes
import concourse.bass as bass
import concourse.mybir as mybir
from concourse.bass_utils import run_bass_kernel_spmd

F32 = mybir.dt.float32
BF16 = mybir.dt.bfloat16
AF = mybir.ActivationFunctionType
ALU = mybir.AluOpType
AX = mybir.AxisListType

T = 4096
D = 1024
H, LHH, SHH, DHH = 16, 4, 12, 64
NB, BW = 64, 64
NT = T // 128
NE, HID = 8, 2048
EPS = 1e-5

ENGINES = ("sync", "tensor", "vector", "scalar", "gpsimd")


class Sem:
    def __init__(self, handle):
        self.h = handle
        self.v = 0

    def bump(self, k):
        self.v += k
        return self.v


class Prog:
    def __init__(self):
        self.ops = {e: [] for e in ENGINES}

    def emit(self, eng, fn):
        self.ops[eng].append(fn)

    def op(self, eng, build, waits=(), inc=None, inc_n=1):
        waits = tuple((s, v) for (s, v) in waits if v > 0)
        if inc is not None:
            inc.bump(inc_n)

        def f(e, build=build, waits=waits, inc=inc, inc_n=inc_n):
            for s, v in waits:
                e.wait_ge(s.h, v)
            ins = build(e)
            if inc is not None:
                ins.then_inc(inc.h, inc_n)

        self.emit(eng, f)

    def dma(self, qeng, out_ap, in_ap, sem, waits=()):
        waits = tuple((s, v) for (s, v) in waits if v > 0)
        sem.bump(16)

        def f(e, out_ap=out_ap, in_ap=in_ap, waits=waits, sem=sem):
            for s, w in waits:
                e.wait_ge(s.h, w)
            e.dma_start(out=out_ap, in_=in_ap).then_inc(sem.h, 16)

        self.emit(qeng, f)
        return sem.v

    def wait(self, eng, sem, v):
        if v > 0:
            self.emit(eng, lambda e, sem=sem, v=v: e.wait_ge(sem.h, v))


def build_nc(debug=(), parts=("A", "Bl", "Bs", "C", "D"), sink_stop=99):
    nc = bass.Bass("TRN2", target_bir_lowering=False)

    xT = nc.dram_tensor("xT", [D, T], F32, kind="ExternalInput").ap()
    x_in = nc.dram_tensor("x_in", [T, D], F32, kind="ExternalInput").ap()
    wq = nc.dram_tensor("wq", [D, D], F32, kind="ExternalInput").ap()
    wkv = nc.dram_tensor("wkv", [D, 2 * D], F32, kind="ExternalInput").ap()
    wo = nc.dram_tensor("wo", [D, D], F32, kind="ExternalInput").ap()
    bo = nc.dram_tensor("bo", [D], F32, kind="ExternalInput").ap()
    sortw = nc.dram_tensor("sortw", [SHH, 2 * DHH, NB], F32, kind="ExternalInput").ap()
    gwT = nc.dram_tensor("gwT", [D, NE], F32, kind="ExternalInput").ap()
    ebias = nc.dram_tensor("ebias", [128, NE], F32, kind="ExternalInput").ap()
    wuT = nc.dram_tensor("wuT", [NE, D, HID], F32, kind="ExternalInput").ap()
    wdT = nc.dram_tensor("wdT", [NE, HID, D], F32, kind="ExternalInput").ap()
    out = nc.dram_tensor("out", [T, D], F32, kind="ExternalOutput").ap()

    def scratch(name, shape, dtype):
        kind = "ExternalOutput" if name in debug else "Internal"
        return nc.dram_tensor(name, shape, dtype, kind=kind).ap()

    scr_q = scratch("scr_q", [8, 128, T], BF16)
    scr_kl = scratch("scr_kl", [2, 128, T], BF16)
    scr_kv = scratch("scr_kv", [SHH, 128, T], BF16)
    scr_vt = scratch("scr_vt", [8, NT, 128, 128], BF16)
    scr_x1f = scratch("scr_x1f", [8, 128, T], BF16)
    scr_x1t = scratch("scr_x1t", [T, D], F32)
    scr_wub = scratch("scr_wub", [NE, D, HID], BF16)
    scr_wdb = scratch("scr_wdb", [NE, HID, D], BF16)
    dbg_ao = scratch("dbg_ao", [T, D], BF16) if "dbg_ao" in debug else None
    dbg_w2 = scratch("dbg_w2", [T, NE], F32) if "dbg_w2" in debug else None

    ident_h = nc.inline_tensor(np.eye(128, dtype=ml_dtypes.bfloat16), "ident128").ap()
    ones_h = nc.inline_tensor(np.ones((1, 128), dtype=ml_dtypes.bfloat16), "onesrow").ap()
    NEG = -1.5e38
    _ml = np.zeros((128, 256), dtype=np.float32)
    _ml[0:64, 192:256] = NEG
    _ml[64:128, 0:64] = NEG
    maskl_h = nc.inline_tensor(_ml.astype(ml_dtypes.bfloat16), "maskl").ap()
    _ms = np.zeros((128, 512), dtype=np.float32)
    for _half in range(2):
        _c0 = _half * 256
        _ms[0:64, _c0 + 64:_c0 + 128] = NEG
        _ms[64:128, _c0:_c0 + 64] = NEG
        _ms[0:64, _c0 + 192:_c0 + 256] = NEG
        _ms[64:128, _c0 + 128:_c0 + 192] = NEG
    masks_h = nc.inline_tensor(_ms.astype(ml_dtypes.bfloat16), "masks").ap()
    negrow_h = nc.inline_tensor(np.full((1, 64), NEG, dtype=ml_dtypes.bfloat16), "negrow").ap()
    epscol_h = nc.inline_tensor(np.full((128, 1), EPS, dtype=np.float32), "epscol").ap()

    ar1 = nc.alloc_sbuf_tensor("arena1", [128, 65536], mybir.dt.uint8).ap()
    ar2 = nc.alloc_sbuf_tensor("arena2", [128, 65536], mybir.dt.uint8).ap()
    ar3 = nc.alloc_sbuf_tensor("arena3", [128, 32768], mybir.dt.uint8).ap()
    ar4 = nc.alloc_sbuf_tensor("arena4", [128, 32768], mybir.dt.uint8).ap()
    arS = nc.alloc_sbuf_tensor("arenaS", [128, 15680], mybir.dt.uint8).ap()

    def sb(arena, off, dtype, cols, rows=128):
        es = mybir.dt.size(dtype)
        assert off % 32 == 0 and off + cols * es <= arena.shape[1] * 1
        return arena.bitcast(dtype)[0:rows, off // es: off // es + cols]

    sbXF = sb(ar1, 0, BF16, 8 * T)
    sbAO = sb(ar1, 0, BF16, NT * 1024)
    sbWU = [sb(ar1, i * 32768, BF16, 8 * HID) for i in range(2)]

    sbWQ = sb(ar2, 0, BF16, 8 * 1024)
    sbWKV = sb(ar2, 16384, BF16, 8 * 2048)
    sbQH = [sb(ar2, i * 8192, BF16, T) for i in range(2)]
    sbKH = [sb(ar2, 16384 + i * 8192, BF16, T) for i in range(2)]
    sbVT = [sb(ar2, 32768 + i * 8192, BF16, NT * 128) for i in range(2)]
    sbWO = sb(ar2, 49152, BF16, 8 * 1024)
    sbWD = [sb(ar2, i * 32768, BF16, 16 * 1024) for i in range(2)]
    sbHR = [sb(arS, 13632 + i * 1024, BF16, 512) for i in range(2)]

    sbSTG = sb(ar3, 0, BF16, 4 * 1024)
    sbKVB = sb(ar3, 0, BF16, BW * 128)
    sbRKV = sb(ar3, 16384, BF16, T)
    sbACC = sb(ar3, 0, F32, 8 * 1024)
    sbCz = sb(ar3, 0, F32, 1024)
    sbCsq = sb(ar3, 4096, F32, 1024)
    sbCx = sb(ar3, 8192, F32, 1024)
    sbCa = sb(ar3, 12288, BF16, 1024)
    sbCx1 = sb(ar3, 14336, BF16, 1024)
    sbCxf = sb(ar3, 16384, BF16, 1024)

    sbX1Q = sb(ar4, 0, BF16, 8 * 1024)
    sbHE = sb(ar4, 16384, BF16, 16 * 512)
    sbPP = sb(ar4, 0, BF16, 512)
    sbP2 = sb(ar4, 1024, BF16, 8 * 128)
    sbP2T = sb(ar4, 3072, BF16, 8 * 64)
    sbBVR = sb(ar4, 4096, BF16, 8 * 64)
    sbFE = sb(ar4, 5120, BF16, 64)
    sbRT = sb(ar4, 5248, BF16, 64)
    sbRS = sb(ar4, 5376, F32, 64, rows=64)
    sbRE = sb(ar4, 5632, F32, 64, rows=64)
    sbST1 = sb(ar4, 5888, F32, 16)
    sbST2 = sb(ar4, 5952, F32, 16)
    sbST3 = sb(ar4, 6016, F32, 16)
    sbFE32 = sb(ar4, 6144, F32, 64)
    sbWX = sb(ar4, 8192, BF16, 8 * 512)

    sbID = sb(arS, 0, BF16, 128)
    sbSORT = sb(arS, 256, BF16, SHH * NB)
    sbGW = sb(arS, 1792, BF16, 64)
    sbBO = sb(arS, 1920, BF16, 1024, rows=1)
    sbONE = sb(arS, 3968, BF16, 128, rows=1)
    sbEB = sb(arS, 4224, F32, NE)
    sbW2 = sb(arS, 4480, F32, 8 * NE)
    sbGS = sb(arS, 4736, F32, 8 * NE)
    sbGB = sb(arS, 4992, F32, 8 * NE)
    sbTOP = sb(arS, 5248, F32, 8)
    sbLN = sb(arS, 5312, F32, 16)
    sbEPSC = sb(arS, 5376, F32, 1)
    sbMKL = sb(ar4, 3072, BF16, 256)
    sbMKS = sb(ar4, 3584, BF16, 512)
    sbNEGR = sb(ar4, 4608, BF16, 64, rows=1)
    sbO = sb(arS, 5440, F32, 1024)
    sbX1T = sb(arS, 9536, F32, 1024)
    sbEPQ = [sb(ar4, 16384 + i * 4096, F32, 1024) for i in range(2)]

    pbank = [nc.alloc_psum_tensor(f"pb{i}", [128, 512], F32).ap() for i in range(6)]
    ptr = [nc.alloc_psum_tensor(f"pt{i}", [128, 1024], BF16).ap() for i in range(2)]

    import contextlib
    estack = contextlib.ExitStack()
    semnames = [
        "gld", "sld", "sst", "gst",
        "stg0", "stg1", "stg2", "stg3",
        "mmA", "evA",
        "mmB", "evB", "scB", "hld0", "hld1",
        "mmC", "evC", "scC", "xld", "x1t", "x1f", "gld2",
        "mmD", "evD", "scD", "wld0", "wld1", "pcl0", "pcs0", "pcl1", "pcs1", "xq", "xt2",
        "outd",
    ]
    S = {nm: Sem(estack.enter_context(nc.semaphore(nm))) for nm in semnames}

    prog = Prog()

    for c in range(8):
        prog.dma("gpsimd", sbXF[:, c * T:(c + 1) * T], xT[c * 128:(c + 1) * 128, :], S["gld"])
    for c in range(8):
        prog.dma("gpsimd", sbWQ[:, c * 1024:(c + 1) * 1024], wq[c * 128:(c + 1) * 128, :], S["gld"])
    for c in range(8):
        prog.dma("gpsimd", sbWKV[:, c * 2048:(c + 1) * 2048], wkv[c * 128:(c + 1) * 128, :], S["gld"])
    for hh in range(SHH):
        prog.dma("gpsimd", sbSORT[:, hh * NB:(hh + 1) * NB], sortw[hh], S["gld"])
    for c in range(8):
        prog.dma("gpsimd", sbGW[:, c * 8:(c + 1) * 8], gwT[c * 128:(c + 1) * 128, :], S["gld"])
    prog.dma("gpsimd", sbBO[:, :], bo.rearrange("(a b) -> a b", a=1), S["gld"])
    setup_g = S["gld"].v
    prog.dma("sync", sbID[:, :], ident_h, S["sld"])
    prog.dma("sync", sbONE[:, :], ones_h, S["sld"])
    prog.dma("sync", sbEB[:, :], ebias, S["sld"])
    prog.dma("sync", sbMKL[:, :], maskl_h, S["sld"])
    prog.dma("sync", sbMKS[:, :], masks_h, S["sld"])
    prog.dma("sync", sbNEGR[:, :], negrow_h, S["sld"])
    prog.dma("sync", sbEPSC[:, :], epscol_h, S["sld"])
    setup_s = S["sld"].v
    prog.wait("tensor", S["sld"], setup_s)
    prog.wait("vector", S["sld"], setup_s)
    prog.wait("scalar", S["sld"], setup_s)
    for c in range(8):
        prog.dma("gpsimd", sbWO[:, c * 1024:(c + 1) * 1024], wo[c * 128:(c + 1) * 128, :], S["gld2"])
    setup_g2 = S["gld2"].v

    sbPC = [sb(ar3, 24576 + i * 4096, BF16, 2048) for i in range(2)]
    pcl = [S["pcl0"], S["pcl1"]]
    pcs = [S["pcs0"], S["pcs1"]]
    pc_jobs = []
    for e_ in range(NE):
        for ch in range(8):
            pc_jobs.append((wuT[e_, ch * 128:(ch + 1) * 128, :],
                            scr_wub[e_, ch * 128:(ch + 1) * 128, :]))
    for e_ in range(NE):
        for ch in range(16):
            pc_jobs.append((wdT[e_, ch * 128:(ch + 1) * 128, :],
                            scr_wdb[e_, ch * 128:(ch + 1) * 128, :]))
    for j, (src, dst) in enumerate(pc_jobs):
        slot = j % 2
        ncols = src.shape[1]
        w = [(pcs[slot], pcs[slot].v)] if j >= 2 else []
        prog.dma("gpsimd", sbPC[slot][:, 0:ncols], src, pcl[slot], waits=w)
        prog.dma("gpsimd", dst, sbPC[slot][:, 0:ncols], pcs[slot],
                 waits=[(pcl[slot], pcl[slot].v)])
    precast_done = [(pcs[0], pcs[0].v), (pcs[1], pcs[1].v)]

    mmA, evA = S["mmA"], S["evA"]
    stgsem = [S["stg0"], S["stg1"], S["stg2"], S["stg3"]]
    unit = [0]

    def proj_unit(mk_mms, evacs, stores):
        k = unit[0]
        bank = pbank[k % 2]
        slot = stgsem[k % 4]
        prog.op("tensor", (lambda e, bank=bank: mk_mms(e, bank)),
                waits=[(S["gld"], setup_g), (evA, k - 1)], inc=mmA)
        mw = mmA.v

        evw = [(mmA, mw), (slot, slot.v)]

        def ev(e, bank=bank, evacs=evacs):
            ins = None
            for fn in evacs:
                ins = fn(e, bank)
            return ins
        prog.op("vector", ev, waits=evw, inc=evA)
        ew = evA.v
        for dst, src in stores:
            prog.dma("sync", dst, src, slot, waits=[(evA, ew)])
        unit[0] += 1

    def stg(k, r0, nr, cols=512):
        s0 = (k % 4) * 1024
        return sbSTG[r0:r0 + nr, s0:s0 + cols]

    for pc in range(8):
        for sl in range(8):
            k = unit[0]
            def mk(e, bank, pc=pc, sl=sl):
                ins = None
                for dc in range(8):
                    ins = e.matmul(bank[:, :],
                                   sbWQ[:, dc * 1024 + pc * 128: dc * 1024 + pc * 128 + 128],
                                   sbXF[:, dc * T + sl * 512: dc * T + sl * 512 + 512],
                                   start=(dc == 0), stop=(dc == 7))
                return ins
            def ev0(e, bank, k=k):
                return e.tensor_copy(out=stg(k, 0, 128), in_=bank[:, :])
            proj_unit(mk, [ev0],
                      [(scr_q[pc, :, sl * 512:(sl + 1) * 512], stg(k, 0, 128))])

    for pc in range(8):
        for sl in range(8):
            k = unit[0]
            def mk(e, bank, pc=pc, sl=sl):
                ins = None
                for dc in range(8):
                    ins = e.matmul(bank[:, :],
                                   sbWKV[:, dc * 2048 + pc * 128: dc * 2048 + pc * 128 + 128],
                                   sbXF[:, dc * T + sl * 512: dc * T + sl * 512 + 512],
                                   start=(dc == 0), stop=(dc == 7))
                return ins
            def ev0(e, bank, k=k):
                return e.tensor_copy(out=stg(k, 0, 128), in_=bank[:, :])
            if pc < 2:
                proj_unit(mk, [ev0],
                          [(scr_kl[pc, :, sl * 512:(sl + 1) * 512], stg(k, 0, 128))])
            else:
                hs0, hs1 = 2 * pc - 4, 2 * pc - 3
                full = stg(k, 0, 128)
                proj_unit(mk, [ev0],
                          [(scr_kv[hs0, 0:64, sl * 512:(sl + 1) * 512], full[0:64, :]),
                           (scr_kv[hs1, 0:64, sl * 512:(sl + 1) * 512], full[64:128, :])])

    for vc in range(2, 8):
        for sl in range(8):
            k = unit[0]
            def mk(e, bank, vc=vc, sl=sl):
                ins = None
                for dc in range(8):
                    ins = e.matmul(bank[:, :],
                                   sbWKV[:, dc * 2048 + 1024 + vc * 128: dc * 2048 + 1024 + vc * 128 + 128],
                                   sbXF[:, dc * T + sl * 512: dc * T + sl * 512 + 512],
                                   start=(dc == 0), stop=(dc == 7))
                return ins
            def ev0(e, bank, k=k):
                return e.tensor_copy(out=stg(k, 0, 128), in_=bank[:, :])
            hs0, hs1 = 2 * vc - 4, 2 * vc - 3
            full = stg(k, 0, 128)
            proj_unit(mk, [ev0],
                      [(scr_kv[hs0, 64:128, sl * 512:(sl + 1) * 512], full[0:64, :]),
                       (scr_kv[hs1, 64:128, sl * 512:(sl + 1) * 512], full[64:128, :])])

    for tt in range(NT):
        for vh in range(2):
            k = unit[0]
            def mk(e, bank, tt=tt, vh=vh):
                ins = None
                for dc in range(8):
                    ins = e.matmul(bank[:, :],
                                   sbXF[:, dc * T + tt * 128: dc * T + tt * 128 + 128],
                                   sbWKV[:, dc * 2048 + 1024 + vh * 512: dc * 2048 + 1024 + vh * 512 + 512],
                                   start=(dc == 0), stop=(dc == 7))
                return ins
            def ev0(e, bank, k=k):
                return e.tensor_copy(out=stg(k, 0, 128), in_=bank[:, :])
            stores = []
            for pp in range(4):
                p = 4 * vh + pp
                stores.append((scr_vt[p, tt], stg(k, 0, 128)[:, pp * 128:(pp + 1) * 128]))
            proj_unit(mk, [ev0], stores)

    phaseA_done = [(s, s.v) for s in stgsem]

    mmB, scB, evB = S["mmB"], S["scB"], S["evB"]
    hld = [S["hld0"], S["hld1"]]
    slot_free_at = [0, 0]

    def load_local_pair(p):
        s = p % 2
        w = list(phaseA_done) if p == 0 else []
        if slot_free_at[s] > 0:
            w.append((mmB, slot_free_at[s]))
        prog.dma("sync", sbQH[s][:, :], scr_q[p], hld[s], waits=w)
        prog.dma("sync", sbKH[s][:, :], scr_kl[p], hld[s], waits=w)
        for tc in range(4):
            prog.dma("sync",
                     sbVT[s].rearrange("p (t c) -> p t c", c=128)[:, tc * 8:(tc + 1) * 8, :],
                     scr_vt[p, tc * 8:(tc + 1) * 8].transpose([1, 0, 2]),
                     hld[s], waits=w)
        return hld[s].v

    def load_sink_head(hs):
        s = hs % 2
        w = []
        if slot_free_at[s] > 0:
            w.append((mmB, slot_free_at[s]))
        h = hs + 4
        pair = h // 2
        rb = (h % 2) * 64
        prog.dma("sync", sbQH[s][0:64, :], scr_q[pair][rb:rb + 64, :], hld[s], waits=w)
        prog.dma("sync", sbKH[s][:, :], scr_kv[hs], hld[s], waits=w)
        for tc in range(4):
            prog.dma("sync",
                     sbVT[s].rearrange("p (t c) -> p t c", c=128)[:, tc * 8:(tc + 1) * 8, :],
                     scr_vt[pair, tc * 8:(tc + 1) * 8].transpose([1, 0, 2]),
                     hld[s], waits=w)
        return hld[s].v

    lu = [0]
    l_tr_mm = []
    l_av_mm = []
    l_pt_ev = []
    l_av_ev = []
    l_exp = []

    def local_head_tile(s, h, tt, ldv):
        u = lu[0]
        rb = (h % 2) * 64
        hv = (h % 2) * 64
        sbank = pbank[u % 2]
        obank = pbank[2 + u % 2]
        pslot = (u % 2) * 256
        wslot = (u % 2) * 1024

        if tt == 0:
            c0, c1, p0 = 0, 192, 64
        elif tt == NT - 1:
            c0, c1, p0 = tt * 128 - 64, tt * 128 + 128, 0
        else:
            c0, c1, p0 = tt * 128 - 64, tt * 128 + 192, 0

        def mk_s(e, s=s, rb=rb, tt=tt, c0=c0, c1=c1, p0=p0, sbank=sbank):
            ins = e.matmul(sbank[:, p0:p0 + (c1 - c0)],
                           sbQH[s][rb:rb + 64, tt * 128:tt * 128 + 128],
                           sbKH[s][rb:rb + 64, c0:c1],
                           start=True, stop=True)
            if tt == 0:
                ins = e.matmul(sbank[:, 0:64], sbONE[:, :], sbNEGR[:, :],
                               start=True, stop=True)
            if tt == NT - 1:
                ins = e.matmul(sbank[:, 192:256], sbONE[:, :], sbNEGR[:, :],
                               start=True, stop=True)
            return ins
        sw = [(hld[s], ldv)]
        if u >= 2:
            sw.append((scB, l_exp[u - 2]))
        prog.op("tensor", mk_s, waits=sw, inc=mmB)
        smm = mmB.v

        prog.op("vector",
                lambda e, sbank=sbank:
                    e.tensor_tensor(sbank[:, 0:256], sbank[:, 0:256], sbMKL[:, :],
                                    ALU.add),
                waits=[(mmB, smm)], inc=evB)
        mzd = evB.v

        expw = [(evB, mzd)]
        if u >= 2:
            expw.append((mmB, l_tr_mm[u - 2]))

        prog.op("scalar",
                lambda e, sbank=sbank, pslot=pslot:
                    e.activation(out=sbPP[:, pslot:pslot + 256], in_=sbank[:, 0:256],
                                 func=AF.Exp, accum_out=sbST1[:, 0:1]),
                waits=expw, inc=scB)
        l_exp.append(scB.v)
        sexp = scB.v

        prog.op("vector",
                lambda e: e.reciprocal(sbST2[:, 0:1], sbST1[:, 0:1]),
                waits=[(scB, sexp)], inc=evB)
        maskd = evB.v

        pb0 = (u % 2) * 512
        trw = [(evB, maskd)]
        if u >= 2:
            trw.append((evB, l_pt_ev[u - 2]))

        def mk_tr(e, pslot=pslot, tt=tt, pb0=pb0):
            ins = None
            if tt > 0:
                ins = e.transpose(ptr[0][64:128, pb0:pb0 + 128], sbPP[:, pslot:pslot + 64],
                                  sbID[:, :])
            ins = e.transpose(ptr[0][:, pb0 + 128:pb0 + 256], sbPP[:, pslot + 64:pslot + 192],
                              sbID[:, :])
            if tt < NT - 1:
                ins = e.transpose(ptr[0][0:64, pb0 + 256:pb0 + 384], sbPP[:, pslot + 192:pslot + 256],
                                  sbID[:, :])
            return ins
        prog.op("tensor", mk_tr, waits=trw, inc=mmB)
        l_tr_mm.append(mmB.v)
        trd = mmB.v

        pw = [(mmB, trd)]
        if u >= 2:
            pw.append((mmB, l_av_mm[u - 2]))
        def ev_pt(e, wslot=wslot, pb0=pb0, tt=tt):
            if tt > 0:
                e.tensor_copy(out=sbWX[64:128, wslot:wslot + 128],
                              in_=ptr[0][64:128, pb0:pb0 + 128])
            ins = e.tensor_copy(out=sbWX[:, wslot + 128:wslot + 256],
                                in_=ptr[0][:, pb0 + 128:pb0 + 256])
            if tt < NT - 1:
                ins = e.tensor_copy(out=sbWX[0:64, wslot + 256:wslot + 384],
                                    in_=ptr[0][0:64, pb0 + 256:pb0 + 384])
            return ins
        prog.op("vector", ev_pt, waits=pw, inc=evB)
        l_pt_ev.append(evB.v)
        ptd = evB.v

        avw = [(evB, ptd)]
        if u >= 2:
            avw.append((evB, l_av_ev[u - 2]))

        def mk_av(e, s=s, hv=hv, tt=tt, wslot=wslot, obank=obank):
            first = True
            ins = None
            if tt > 0:
                ins = e.matmul(obank[:, 0:64], sbWX[64:128, wslot:wslot + 128],
                               sbVT[s][64:128, (tt - 1) * 128 + hv:(tt - 1) * 128 + hv + 64],
                               start=first, stop=False)
                first = False
            ins = e.matmul(obank[:, 0:64], sbWX[:, wslot + 128:wslot + 256],
                           sbVT[s][:, tt * 128 + hv:tt * 128 + hv + 64],
                           start=first, stop=(tt == NT - 1))
            if tt < NT - 1:
                ins = e.matmul(obank[:, 0:64], sbWX[0:64, wslot + 256:wslot + 384],
                               sbVT[s][0:64, (tt + 1) * 128 + hv:(tt + 1) * 128 + hv + 64],
                               start=False, stop=True)
            return ins
        prog.op("tensor", mk_av, waits=avw, inc=mmB)
        l_av_mm.append(mmB.v)
        avd = mmB.v

        prog.op("vector",
                lambda e, obank=obank, tt=tt, h=h:
                    e.tensor_scalar(out=sbAO[:, tt * 1024 + h * 64:tt * 1024 + h * 64 + 64],
                                    in0=obank[:, 0:64], scalar1=sbST2[:, 0:1],
                                    scalar2=None, op0=ALU.mult),
                waits=[(mmB, avd)], inc=evB)
        l_av_ev.append(evB.v)
        lu[0] += 1


    if "Bl" in parts:
        for p in range(2):
            ldv = load_local_pair(p)
            for hh in range(2):
                h = 2 * p + hh
                for tt in range(NT):
                    local_head_tile(p % 2, h, tt, ldv)
            slot_free_at[p % 2] = mmB.v

    sbFEb = sb(ar4, 6400, BF16, 64)
    sbRTb = sb(ar4, 6528, BF16, 64)
    sbRSb = sb(ar4, 6656, F32, 64, rows=64)
    sbREb = sb(ar4, 6912, F32, 64, rows=64)
    sbFE32b = sb(ar4, 7168, F32, 64)
    sbST1b = sb(ar4, 7424, F32, 16)
    sbST2b = sb(ar4, 7488, F32, 16)
    sbST3b = sb(ar4, 7552, F32, 16)
    sbRKV2 = sb(ar3, 24576, BF16, T // 2)
    sbRKV2 = sb(ar3, 24576, BF16, T)

    CTXS = [
        dict(FE=sbFE, FE32=sbFE32, RT=sbRT, RS=sbRS, RE=sbRE,
             ST1=sbST1, ST2=sbST2, ST3=sbST3, RKV=sbRKV, rr=0,
             sbank=pbank[0], obank=pbank[2], rbank=pbank[4], ptrX=ptr[0],
             po=0, vo=0),
        dict(FE=sbFEb, FE32=sbFE32b, RT=sbRTb, RS=sbRSb, RE=sbREb,
             ST1=sbST1b, ST2=sbST2b, ST3=sbST3b, RKV=sbRKV2, rr=64,
             sbank=pbank[1], obank=pbank[3], rbank=pbank[5], ptrX=ptr[1],
             po=512, vo=256),
    ]

    def sink_stages(hs, ldv):
        pi = hs % 2
        C = CTXS[pi]
        s = pi
        h = hs + 4
        rr = C["rr"]
        FE, FE32, RT, RS, RE = C["FE"], C["FE32"], C["RT"], C["RS"], C["RE"]
        ST1, ST2, ST3, RKV = C["ST1"], C["ST2"], C["ST3"], C["RKV"]
        sbank, obank, rbank, ptrX = C["sbank"], C["obank"], C["rbank"], C["ptrX"]
        po, vo = C["po"], C["vo"]

        def mk_f(e, s=s, FE32=FE32, FE=FE):
            e.tensor_reduce(out=FE32[0:64, 0:64],
                            in_=sbQH[s][0:64, :].rearrange("p (v t) -> p v t", v=NB),
                            axis=AX.X, op=ALU.add, opt_input=False, opt_output=False)
            e.tensor_reduce(out=FE32[64:128, 0:64],
                            in_=sbKH[s][0:64, :].rearrange("p (v t) -> p v t", v=NB),
                            axis=AX.X, op=ALU.add, opt_input=False, opt_output=False)
            e.drain()
            return e.tensor_copy(out=FE[:, 0:64], in_=FE32[:, 0:64])
        prog.op("vector", mk_f, waits=[(hld[s], ldv)], inc=evB)
        fd = evB.v
        yield

        prog.op("tensor",
                lambda e, hs=hs, FE=FE, rbank=rbank:
                    e.matmul(rbank[0:64, 0:64], FE[:, 0:64],
                             sbSORT[:, hs * 64:hs * 64 + 64], start=True, stop=True),
                waits=[(evB, fd)], inc=mmB)
        rmm = mmB.v
        yield

        def mk_rel(e, RS=RS, ST1=ST1, rbank=rbank):
            e.tensor_scalar_max(RS[0:64, :], rbank[0:64, 0:64], 0.0)
            e.drain()
            return e.reduce_max(out=ST1[0:64, 1:2], in_=RS[0:64, :], axis=AX.X,
                                negate=True, opt_input=False, opt_output=False)
        prog.op("vector", mk_rel, waits=[(mmB, rmm)], inc=evB)
        reld = evB.v
        yield

        prog.op("scalar",
                lambda e, RE=RE, RS=RS, ST1=ST1:
                    e.activation(out=RE[0:64, :], in_=RS[0:64, :], func=AF.Exp,
                                 bias=ST1[0:64, 1:2], accum_out=ST1[0:64, 3:4]),
                waits=[(evB, reld)], inc=scB)
        red = scB.v
        yield

        def mk_rn(e, FE=FE, RE=RE, ST1=ST1):
            e.reciprocal(ST1[0:64, 4:5], ST1[0:64, 3:4])
            e.drain()
            return e.tensor_scalar(out=FE[0:64, 0:64], in0=RE[0:64, :],
                                   scalar1=ST1[0:64, 4:5], scalar2=None, op0=ALU.mult)
        prog.op("vector", mk_rn, waits=[(scB, red)], inc=evB)
        rnd = evB.v
        yield

        prog.op("tensor",
                lambda e, FE=FE, ptrX=ptrX, rr=rr:
                    e.transpose(ptrX[rr:rr + 64, 0:64], FE[0:64, 0:64],
                                sbID[0:64, 0:64]),
                waits=[(evB, rnd)], inc=mmB)
        rtt = mmB.v
        prog.op("vector",
                lambda e, RT=RT, ptrX=ptrX, rr=rr:
                    e.tensor_copy(out=RT[rr:rr + 64, :], in_=ptrX[rr:rr + 64, 0:64]),
                waits=[(mmB, rtt)], inc=evB)
        rtd = evB.v
        yield

        if sink_stop < 2:
            return
        kv_ev = rtd
        r_ev_prev = 0
        for g in range(8):
            def mk_kvb(e, s=s, g=g, ptrX=ptrX, rr=rr):
                ins = None
                for j in range(8):
                    tp = g * 8 + j
                    ins = e.transpose(ptrX[rr:rr + 64, j * 128:(j + 1) * 128],
                                      sbKH[s].rearrange("p (v t) -> p t v", v=NB)[:, tp, :],
                                      sbID[:, :])
                return ins
            prog.op("tensor", mk_kvb, waits=[(evB, kv_ev)], inc=mmB)
            kv_mm = mmB.v
            prog.op("vector",
                    lambda e, g=g, ptrX=ptrX, rr=rr:
                        e.tensor_copy(out=sbKVB[rr:rr + 64, g * 1024:(g + 1) * 1024],
                                      in_=ptrX[rr:rr + 64, 0:1024]),
                    waits=[(mmB, kv_mm)], inc=evB)
            kv_ev = evB.v
            yield

            def mk_r(e, g=g, rbank=rbank, rr=rr, RT=RT):
                ins = None
                for j in range(8):
                    tp = g * 8 + j
                    ins = e.matmul(rbank[:, j * 64:(j + 1) * 64],
                                   sbKVB[rr:rr + 64, tp * 128:(tp + 1) * 128],
                                   RT[rr:rr + 64, :], start=True, stop=True)
                return ins
            rw = [(evB, kv_ev)]
            if r_ev_prev > 0:
                rw.append((evB, r_ev_prev))
            prog.op("tensor", mk_r, waits=rw, inc=mmB)
            r_mm = mmB.v

            def ev_r(e, g=g, rbank=rbank, RKV=RKV):
                src = rbank[:, 0:512].rearrange("p (j u) -> p j u", j=8)
                dst = RKV.rearrange("p (u t) -> p t u", u=NB)[:, g * 8:(g + 1) * 8, :]
                return e.tensor_copy(out=dst, in_=src)
            evw_r = [(mmB, r_mm)]
            if hs in (0, 1) and g == 0:
                evw_r.append((S["pcs0"], precast_done[0][1]))
                evw_r.append((S["pcs1"], precast_done[1][1]))
            prog.op("vector", ev_r, waits=evw_r, inc=evB)
            r_ev_prev = evB.v
            yield
        rkvd = evB.v

        if sink_stop < 4:
            return
        P2T = sbWX
        p2to = pi * 2048
        prev_exp = 0
        prev_tr = 0
        prev_et = 0
        prev_av = 0
        prev_ev = 0
        for b in range(16):
            t0 = 2 * b
            def mk_s2(e, s=s, b=b, t0=t0, sbank=sbank, RKV=RKV):
                ins = None
                for half in range(2):
                    tl = t0 + half
                    q_l = sbQH[s][0:64, tl * 128:(tl + 1) * 128]
                    ins = e.matmul(sbank[:, half * 256:half * 256 + 128],
                                   q_l, sbKH[s][0:64, tl * 128:(tl + 1) * 128],
                                   start=True, stop=True)
                    ins = e.matmul(sbank[:, half * 256 + 128:half * 256 + 256],
                                   q_l, RKV[0:64, tl * 128:(tl + 1) * 128],
                                   start=True, stop=True)
                return ins
            sw2 = [(evB, rkvd)]
            if prev_exp > 0:
                sw2.append((scB, prev_exp))
            prog.op("tensor", mk_s2, waits=sw2, inc=mmB)
            s2mm = mmB.v
            yield

            prog.op("vector",
                    lambda e, sbank=sbank:
                        e.tensor_tensor(sbank[:, 0:512], sbank[:, 0:512], sbMKS[:, :],
                                        ALU.add),
                    waits=[(mmB, s2mm)], inc=evB)
            mz2d = evB.v

            xw2 = [(evB, mz2d)]
            if prev_tr > 0:
                xw2.append((mmB, prev_tr))

            def mk_ex2(e, sbank=sbank, po=po, b=b, ST2=ST2):
                e.activation(out=sbP2[:, po:po + 256], in_=sbank[:, 0:256],
                             func=AF.Exp, scale=0.125,
                             accum_out=ST2[:, (b % 2) * 2:(b % 2) * 2 + 1])
                return e.activation(out=sbP2[:, po + 256:po + 512], in_=sbank[:, 256:512],
                                    func=AF.Exp, scale=0.125,
                                    accum_out=ST2[:, (b % 2) * 2 + 1:(b % 2) * 2 + 2])
            prog.op("scalar", mk_ex2, waits=xw2, inc=scB)
            prev_exp = scB.v
            s2exp = scB.v
            yield

            prog.op("vector",
                    lambda e, b=b, ST2=ST2, ST3=ST3:
                        e.reciprocal(ST3[:, (b % 2) * 2:(b % 2) * 2 + 2],
                                     ST2[:, (b % 2) * 2:(b % 2) * 2 + 2]),
                    waits=[(scB, s2exp)], inc=evB)
            sumd = evB.v
            yield

            def mk_t2(e, b=b, po=po, ptrX=ptrX, RKV=RKV):
                ins = None
                for blk in range(4):
                    ins = e.transpose(ptrX[:, blk * 128:(blk + 1) * 128],
                                      sbP2[:, po + blk * 128:po + (blk + 1) * 128],
                                      sbID[:, :])
                for j in range(4):
                    u = 4 * b + j
                    rbase = (u % 2) * 64
                    half = j // 2
                    ins = e.transpose(ptrX[rbase:rbase + 64, 512 + half * 64:512 + (half + 1) * 64],
                                      RKV[64:128, u * 64:(u + 1) * 64],
                                      sbID[64:128, 64:128])
                return ins
            t2w = [(evB, sumd)]
            if prev_et > 0:
                t2w.append((evB, prev_et))
            prog.op("tensor", mk_t2, waits=t2w, inc=mmB)
            prev_tr = mmB.v
            t2d = mmB.v
            yield

            def ev_t2(e, ptrX=ptrX, p2to=p2to):
                e.tensor_copy(out=P2T[:, p2to:p2to + 512], in_=ptrX[:, 0:512])
                return e.tensor_copy(out=P2T[:, p2to + 512:p2to + 640],
                                     in_=ptrX[:, 512:640])
            ew2 = [(mmB, t2d)]
            if prev_av > 0:
                ew2.append((mmB, prev_av))
            prog.op("vector", ev_t2, waits=ew2, inc=evB)
            prev_et = evB.v
            t2e = evB.v
            yield

            def mk_v2(e, s=s, t0=t0, obank=obank, p2to=p2to, h=h):
                hv = (h % 2) * 64
                ins = None
                for half in range(2):
                    tl = t0 + half
                    oc = half * 64
                    ins = e.matmul(obank[:, oc:oc + 64],
                                   P2T[:, p2to + half * 256:p2to + half * 256 + 128],
                                   sbVT[s][:, tl * 128 + hv:tl * 128 + hv + 64],
                                   start=True, stop=False)
                    ins = e.matmul(obank[:, oc:oc + 64],
                                   P2T[:, p2to + half * 256 + 128:p2to + half * 256 + 256],
                                   P2T[:, p2to + 512 + half * 64:p2to + 512 + (half + 1) * 64],
                                   start=False, stop=True)
                return ins
            avw = [(evB, t2e)]
            if prev_ev > 0:
                avw.append((evB, prev_ev))
            prog.op("tensor", mk_v2, waits=avw, inc=mmB)
            prev_av = mmB.v
            v2d = mmB.v
            yield

            def ev_v2(e, b=b, t0=t0, h=h, obank=obank, ST3=ST3):
                ins = None
                for half in range(2):
                    tl = t0 + half
                    col = tl * 1024 + h * 64
                    ins = e.tensor_scalar(
                        out=sbAO[:, col:col + 64],
                        in0=obank[:, half * 64:half * 64 + 64],
                        scalar1=ST3[:, (b % 2) * 2 + half:(b % 2) * 2 + half + 1],
                        scalar2=None, op0=ALU.mult)
                return ins
            prog.op("vector", ev_v2, waits=[(mmB, v2d)], inc=evB)
            prev_ev = evB.v
            yield

    if "Bs" in parts:
        for hp in range(SHH // 2):
            h0, h1 = 2 * hp, 2 * hp + 1
            ld0 = load_sink_head(h0)
            ld1 = load_sink_head(h1)
            g0 = sink_stages(h0, ld0)
            g1 = sink_stages(h1, ld1)
            alive = [g0, g1]
            while alive:
                for g in list(alive):
                    try:
                        next(g)
                    except StopIteration:
                        alive.remove(g)
            slot_free_at[0] = mmB.v
            slot_free_at[1] = mmB.v

    phaseB_ev = evB.v

    dbg_spill_done = 0
    if dbg_ao is not None:
        for tt in range(NT):
            prog.dma("sync", dbg_ao[tt * 128:(tt + 1) * 128, :],
                     sbAO[:, tt * 1024:(tt + 1) * 1024], S["gst"],
                     waits=[(evB, phaseB_ev)] if tt == 0 else ())
        dbg_spill_done = S["gst"].v

    mmC, evC, scC = S["mmC"], S["evC"], S["scC"]
    xld, x1t, x1f = S["xld"], S["x1t"], S["x1f"]

    if "C" in parts:
        c_ev2 = []
        c_ev5 = []
        c_spl = []
        for tc in range(NT):
            lw = [(evB, phaseB_ev)] if tc == 0 else [(evC, c_ev2[tc - 1])]
            prog.dma("sync", sbCx[:, :], x_in[tc * 128:(tc + 1) * 128, :], xld,
                     waits=lw)
            xv = xld.v

            tw = [(evB, phaseB_ev)] if tc == 0 else [(evC, c_ev5[tc - 1])]

            def mk_c1(e, tc=tc):
                ins = None
                for c in range(8):
                    ins = e.transpose(ptr[0][:, c * 128:(c + 1) * 128],
                                      sbAO[:, tc * 1024 + c * 128:tc * 1024 + (c + 1) * 128],
                                      sbID[:, :])
                return ins
            prog.op("tensor", mk_c1, waits=tw, inc=mmC)
            c1d = mmC.v

            prog.op("vector", lambda e: e.tensor_copy(out=sbCa[:, :], in_=ptr[0][:, :1024]),
                    waits=[(mmC, c1d)], inc=evC)
            cad = evC.v

            pw = [(evC, cad), (S["gld2"], setup_g2)]
            if tc >= 1:
                pw.append((evC, c_ev2[tc - 1]))

            def mk_c2(e):
                ins = None
                for dh2 in range(2):
                    for c in range(8):
                        ins = e.matmul(pbank[dh2][:, :],
                                       sbCa[:, c * 128:(c + 1) * 128],
                                       sbWO[:, c * 1024 + dh2 * 512:c * 1024 + dh2 * 512 + 512],
                                       start=(c == 0), stop=False)
                    ins = e.matmul(pbank[dh2][:, :], sbONE[:, :],
                                   sbBO[0:1, dh2 * 512:(dh2 + 1) * 512],
                                   start=False, stop=True)
                return ins
            prog.op("tensor", mk_c2, waits=pw, inc=mmC)
            c2d = mmC.v

            def mk_c3(e):
                e.tensor_tensor(sbCz[:, 0:512], pbank[0][:, :], sbCx[:, 0:512], ALU.add)
                e.tensor_tensor(sbCz[:, 512:1024], pbank[1][:, :], sbCx[:, 512:1024], ALU.add)
                e.drain()
                return e.tensor_reduce(out=sbLN[:, 0:1], in_=sbCz[:, :], axis=AX.X,
                                       op=ALU.add, opt_input=False, opt_output=False)
            prog.op("vector", mk_c3, waits=[(mmC, c2d), (xld, xv)], inc=evC)
            c_ev2.append(evC.v)
            c3d = evC.v

            sqw = [(evC, c3d)]
            if tc >= 1:
                sqw.append((x1t, c_spl[tc - 1][0]))
            prog.op("scalar",
                    lambda e: e.activation(out=sbCsq[:, :], in_=sbCz[:, :], func=AF.Square,
                                           accum_out=sbLN[:, 1:2]),
                    waits=sqw, inc=scC)
            csq = scC.v

            def mk_c4(e, tc=tc):
                e.tensor_scalar_mul(sbLN[:, 2:3], sbLN[:, 0:1], 1.0 / 1024.0)
                e.tensor_scalar_mul(sbLN[:, 3:4], sbLN[:, 1:2], 1.0 / 1024.0)
                e.drain()
                e.scalar_tensor_tensor(out=sbLN[:, 4:5], in0=sbLN[:, 2:3],
                                       scalar=sbLN[:, 2:3], in1=sbLN[:, 3:4],
                                       op0=ALU.mult, op1=ALU.subtract)
                e.drain()
                return e.scalar_tensor_tensor(out=sbLN[:, 5:6], in0=sbLN[:, 4:5],
                                              scalar=-1.0, in1=sbEPSC[:, :],
                                              op0=ALU.mult, op1=ALU.add)
            prog.op("vector", mk_c4, waits=[(scC, csq)], inc=evC)
            c4d = evC.v

            prog.op("scalar",
                    lambda e: e.activation(out=sbLN[:, 6:7], in_=sbLN[:, 5:6],
                                           func=AF.Sqrt),
                    waits=[(evC, c4d)], inc=scC)
            cstd = scC.v

            def mk_c5(e):
                e.reciprocal(sbLN[:, 8:9], sbLN[:, 6:7])
                e.drain()
                e.tensor_scalar(out=sbCsq[:, :], in0=sbCz[:, :], scalar1=sbLN[:, 2:3],
                                scalar2=sbLN[:, 8:9], op0=ALU.subtract, op1=ALU.mult)
                return e.tensor_scalar(out=sbCx1[:, :], in0=sbCz[:, :], scalar1=sbLN[:, 2:3],
                                       scalar2=sbLN[:, 8:9], op0=ALU.subtract, op1=ALU.mult)
            prog.op("vector", mk_c5, waits=[(scC, cstd)], inc=evC)
            c5d = evC.v

            prog.dma("sync", scr_x1t[tc * 128:(tc + 1) * 128, :], sbCsq[:, :], x1t,
                     waits=[(evC, c5d)])

            def mk_c6(e):
                ins = None
                for c in range(8):
                    ins = e.transpose(ptr[0][:, c * 128:(c + 1) * 128],
                                      sbCx1[:, c * 128:(c + 1) * 128], sbID[:, :])
                return ins
            prog.op("tensor", mk_c6, waits=[(evC, c5d)], inc=mmC)
            c6d = mmC.v

            evw2 = [(mmC, c6d)]
            if tc >= 1:
                evw2.append((x1f, c_spl[tc - 1][1]))
            prog.op("vector", lambda e: e.tensor_copy(out=sbCxf[:, :], in_=ptr[0][:, :1024]),
                    waits=evw2, inc=evC)
            c_ev5.append(evC.v)
            c5e = evC.v

            for c in range(8):
                prog.dma("sync", scr_x1f[c, :, tc * 128:(tc + 1) * 128],
                         sbCxf[:, c * 128:(c + 1) * 128], x1f,
                         waits=[(evC, c5e)] if c == 0 else ())
            c_spl.append((x1t.v, x1f.v))

        phaseC_done = [(x1t, x1t.v), (x1f, x1f.v)]


    if "D" in parts:
        mmD, evD, scD = S["mmD"], S["evD"], S["scD"]
        wld = [S["wld0"], S["wld1"]]
        xq, xt2, outd_ = S["xq"], S["xt2"], S["outd"]
        x1t_total = S["x1t"].v
        wld_free_at = [0, 0]
        up_ev = []
        up_sc = []
        dn_ev = []
        prev_sig = 0
        q_x1q_free = 0
        pending_epilogue = []
        epi_sq_guard = [0]

        def load_expert_weights(e_, first):
            sl = e_ % 2
            w = []
            if first:
                w = [(S["pcs0"], precast_done[0][1]), (S["pcs1"], precast_done[1][1])]
                if dbg_spill_done:
                    w.append((S["gst"], dbg_spill_done))
            if wld_free_at[sl] > 0:
                w.append((mmD, wld_free_at[sl]))
            for c in range(8):
                prog.dma("gpsimd", sbWU[sl][:, c * HID:(c + 1) * HID],
                         scr_wub[e_, c * 128:(c + 1) * 128, :], wld[sl], waits=w)
            for ch in range(16):
                prog.dma("gpsimd", sbWD[sl][:, ch * 1024:(ch + 1) * 1024],
                         scr_wdb[e_, ch * 128:(ch + 1) * 128, :], wld[sl], waits=w)
            return wld[sl].v

        for tq in range(4):
            xw = list(phaseC_done) if tq == 0 else []
            if q_x1q_free > 0:
                xw.append((mmD, q_x1q_free))
            for c in range(8):
                prog.dma("gpsimd", sbX1Q[:, c * 1024:(c + 1) * 1024],
                         scr_x1f[c, :, tq * 1024:(tq + 1) * 1024], xq, waits=xw)
            xqv = xq.v

            def mk_g(e):
                ins_ = None
                for ti in range(8):
                    for c in range(8):
                        ins_ = e.matmul(pbank[4][:, ti * 8:(ti + 1) * 8],
                                        sbX1Q[:, c * 1024 + ti * 128:c * 1024 + (ti + 1) * 128],
                                        sbGW[:, c * 8:(c + 1) * 8],
                                        start=(c == 0), stop=(c == 7))
                return ins_
            gw_ = [(xq, xqv)]
            if prev_sig > 0:
                gw_.append((scD, prev_sig))
            prog.op("tensor", mk_g, waits=gw_, inc=mmD)
            gmm = mmD.v

            prog.op("scalar",
                    lambda e: e.activation(out=sbGS[:, :], in_=pbank[4][:, 0:64],
                                           func=AF.Sigmoid),
                    waits=[(mmD, gmm)], inc=scD)
            prev_sig = scD.v

            def mk_w2(e):
                bias_b = sbEB[:, 0:8].unsqueeze(1).broadcast_to([128, 8, 8])
                e.tensor_tensor(sbGB[:, :].rearrange("p (t e) -> p t e", t=8),
                                sbGS[:, :].rearrange("p (t e) -> p t e", t=8),
                                bias_b, ALU.add)
                e.drain()
                ins_ = None
                for ti in range(8):
                    gb = sbGB[:, ti * 8:(ti + 1) * 8]
                    e.reduce_max(out=sbTOP[:, 0:1], in_=gb, axis=AX.X,
                                 opt_input=False, opt_output=False)
                    e.drain()
                    e.tensor_scalar(out=sbW2[:, ti * 8:(ti + 1) * 8], in0=gb,
                                    scalar1=sbTOP[:, 0:1], scalar2=None, op0=ALU.is_ge)
                    e.drain()
                    e.scalar_tensor_tensor(out=sbW2[:, ti * 8:(ti + 1) * 8],
                                           in0=sbW2[:, ti * 8:(ti + 1) * 8],
                                           scalar=-1.0e30, in1=gb,
                                           op0=ALU.mult, op1=ALU.add)
                    e.drain()
                    e.reduce_max(out=sbTOP[:, 1:2], in_=sbW2[:, ti * 8:(ti + 1) * 8],
                                 axis=AX.X, opt_input=False, opt_output=False)
                    e.drain()
                    e.tensor_scalar(out=sbW2[:, ti * 8:(ti + 1) * 8], in0=gb,
                                    scalar1=sbTOP[:, 1:2], scalar2=None, op0=ALU.is_ge)
                    e.drain()
                    e.tensor_tensor(sbW2[:, ti * 8:(ti + 1) * 8],
                                    sbW2[:, ti * 8:(ti + 1) * 8],
                                    sbGS[:, ti * 8:(ti + 1) * 8], ALU.mult)
                    e.drain()
                    ins_ = e.tensor_tensor(sbW2[:, ti * 8:(ti + 1) * 8],
                                           sbW2[:, ti * 8:(ti + 1) * 8],
                                           sbW2[:, ti * 8:(ti + 1) * 8], ALU.mult)
                return ins_
            w2w_ = [(scD, prev_sig)]
            if dbg_w2 is not None and S["gst"].v > 0:
                w2w_.append((S["gst"], S["gst"].v))
            prog.op("vector", mk_w2, waits=w2w_, inc=evD)
            w2d = evD.v

            if pending_epilogue:
                epi_sq_guard[0] = pending_epilogue.pop()()

            if dbg_w2 is not None:
                for ti in range(8):
                    prog.dma("sync", dbg_w2[tq * 1024 + ti * 128:tq * 1024 + (ti + 1) * 128, :],
                             sbW2[:, ti * 8:(ti + 1) * 8], S["gst"],
                             waits=[(evD, w2d)])

            wld_vals = {}
            wld_vals[0] = load_expert_weights(0, first=(tq == 0))
            wld_vals[1] = load_expert_weights(1, first=(tq == 0))

            for e_ in range(NE):
                sl = e_ % 2
                ldv = wld_vals[e_]
                for sl2 in range(2):
                    for jc in range(16):
                        bank = pbank[jc % 2]
                        def mk_up(e, sl=sl, sl2=sl2, jc=jc, bank=bank):
                            ins_ = None
                            for dc in range(8):
                                ins_ = e.matmul(bank[:, :],
                                                sbWU[sl][:, dc * HID + jc * 128:dc * HID + jc * 128 + 128],
                                                sbX1Q[:, dc * 1024 + sl2 * 512:dc * 1024 + sl2 * 512 + 512],
                                                start=(dc == 0), stop=(dc == 7))
                            return ins_
                        uw = [(wld[sl], ldv), (xq, xqv)]
                        if len(up_sc) >= 2:
                            uw.append((scD, up_sc[-2]))
                        prog.op("tensor", mk_up, waits=uw, inc=mmD)
                        umm = mmD.v
                        rw = [(mmD, umm)]
                        if len(up_ev) >= 2:
                            rw.append((evD, up_ev[-2]))
                        prog.op("scalar",
                                lambda e, jc=jc, bank=bank:
                                    e.activation(out=sbHR[jc % 2][:, :], in_=bank[:, :],
                                                 func=AF.Relu),
                                waits=rw, inc=scD)
                        up_sc.append(scD.v)
                        sqg = [(scD, up_sc[-1])]
                        if e_ == 0 and sl2 == 0 and epi_sq_guard[0] != 0:
                            sqg.append((scD, epi_sq_guard[0][0]))
                            sqg.append((outd_, epi_sq_guard[0][1]))
                        prog.op("vector",
                                lambda e, jc=jc:
                                    e.tensor_tensor(sbHE[:, jc * 512:(jc + 1) * 512],
                                                    sbHR[jc % 2][:, :], sbHR[jc % 2][:, :],
                                                    ALU.mult),
                                waits=sqg, inc=evD)
                        up_ev.append(evD.v)
                    h_ready = evD.v
                    for cg in range(4):
                        for dh2 in range(2):
                            bank = pbank[2 + (cg * 2 + dh2) % 2]
                            def mk_dn(e, sl=sl, cg=cg, dh2=dh2, bank=bank):
                                ins_ = None
                                for jc in range(16):
                                    ins_ = e.matmul(bank[:, :],
                                                    sbHE[:, jc * 512 + cg * 128:jc * 512 + cg * 128 + 128],
                                                    sbWD[sl][:, jc * 1024 + dh2 * 512:jc * 1024 + dh2 * 512 + 512],
                                                    start=(jc == 0), stop=(jc == 15))
                                return ins_
                            dw = [(evD, h_ready), (wld[sl], ldv)]
                            if len(dn_ev) >= 2:
                                dw.append((evD, dn_ev[-2]))
                            prog.op("tensor", mk_dn, waits=dw, inc=mmD)
                            dmm = mmD.v
                            ti = sl2 * 4 + cg
                            def mk_acc(e, bank=bank, ti=ti, e_=e_, dh2=dh2):
                                dst = sbACC[:, ti * 1024 + dh2 * 512:ti * 1024 + dh2 * 512 + 512]
                                if e_ == 0:
                                    return e.tensor_scalar(out=dst, in0=bank[:, :],
                                                           scalar1=sbW2[:, ti * 8:ti * 8 + 1],
                                                           scalar2=None, op0=ALU.mult)
                                return e.scalar_tensor_tensor(out=dst, in0=bank[:, :],
                                                              scalar=sbW2[:, ti * 8 + e_:ti * 8 + e_ + 1],
                                                              in1=dst, op0=ALU.mult, op1=ALU.add)
                            prog.op("vector", mk_acc, waits=[(mmD, dmm), (evD, w2d)], inc=evD)
                            dn_ev.append(evD.v)
                if e_ + 2 < NE:
                    wld_free_at[sl] = mmD.v
                    wld_vals[e_ + 2] = load_expert_weights(e_ + 2, first=False)
                else:
                    wld_free_at[sl] = mmD.v
            q_x1q_free = mmD.v

            def plan_epilogue(tq=tq):
                slots = [sbX1T, sbO]
                last_sq = [0, 0]
                last_ap = [0, 0]
                for pg in range(4):
                    zd, sqd_, stdd, sqr2 = [0, 0], [0, 0], [0, 0], [0, 0]
                    for j in range(2):
                        ti = pg * 2 + j
                        trow = tq * 1024 + ti * 128
                        lw = [(outd_, outd_.v)]
                        if last_ap[j] > 0:
                            lw.append((evD, last_ap[j]))
                        if tq == 0 and ti == 0:
                            lw.append((S["x1t"], x1t_total))
                        prog.dma("sync", slots[j][:, :], scr_x1t[trow:trow + 128, :],
                                 xt2, waits=lw)
                    xtv = xt2.v
                    for j in range(2):
                        ti = pg * 2 + j
                        prog.op("vector",
                                lambda e, ti=ti, j=j:
                                    e.tensor_tensor(slots[j][:, :],
                                                    sbACC[:, ti * 1024:(ti + 1) * 1024],
                                                    slots[j][:, :], ALU.add),
                                waits=[(xt2, xtv)] + ([(evD, last_ap[j])] if last_ap[j] else []),
                                inc=evD)
                        zd[j] = evD.v
                    sum_v = [0, 0]
                    for j in range(2):
                        prog.op("vector",
                                lambda e, j=j:
                                    e.tensor_reduce(out=sbLN[:, j * 8:j * 8 + 1],
                                                    in_=slots[j][:, :], axis=AX.X,
                                                    op=ALU.add, opt_input=False,
                                                    opt_output=False),
                                waits=[(evD, zd[j])], inc=evD)
                        sum_v[j] = evD.v
                        sqw = [(evD, zd[j])]
                        sqw.append((outd_, outd_.v))
                        prog.op("scalar",
                                lambda e, j=j:
                                    e.activation(out=sbEPQ[j][:, :], in_=slots[j][:, :],
                                                 func=AF.Square,
                                                 accum_out=sbLN[:, j * 8 + 1:j * 8 + 2]),
                                waits=sqw, inc=scD)
                        sqd_[j] = scD.v
                        last_sq[j] = scD.v
                    for j in range(2):
                        def mk_st(e, j=j):
                            e.tensor_scalar_mul(sbLN[:, j * 8 + 2:j * 8 + 3],
                                                sbLN[:, j * 8:j * 8 + 1], 1.0 / 1024.0)
                            e.tensor_scalar_mul(sbLN[:, j * 8 + 3:j * 8 + 4],
                                                sbLN[:, j * 8 + 1:j * 8 + 2], 1.0 / 1024.0)
                            e.drain()
                            e.scalar_tensor_tensor(out=sbLN[:, j * 8 + 4:j * 8 + 5],
                                                   in0=sbLN[:, j * 8 + 2:j * 8 + 3],
                                                   scalar=sbLN[:, j * 8 + 2:j * 8 + 3],
                                                   in1=sbLN[:, j * 8 + 3:j * 8 + 4],
                                                   op0=ALU.mult, op1=ALU.subtract)
                            e.drain()
                            return e.scalar_tensor_tensor(out=sbLN[:, j * 8 + 5:j * 8 + 6],
                                                          in0=sbLN[:, j * 8 + 4:j * 8 + 5],
                                                          scalar=-1.0, in1=sbEPSC[:, :],
                                                          op0=ALU.mult, op1=ALU.add)
                        prog.op("vector", mk_st, waits=[(scD, sqd_[j]), (evD, sum_v[j])],
                                inc=evD)
                        stdd[j] = evD.v
                    for j in range(2):
                        prog.op("scalar",
                                lambda e, j=j:
                                    e.activation(out=sbLN[:, j * 8 + 6:j * 8 + 7],
                                                 in_=sbLN[:, j * 8 + 5:j * 8 + 6],
                                                 func=AF.Sqrt),
                                waits=[(evD, stdd[j])], inc=scD)
                        sqr2[j] = scD.v
                    for j in range(2):
                        ti = pg * 2 + j
                        trow = tq * 1024 + ti * 128
                        def mk_ap(e, j=j):
                            e.reciprocal(sbLN[:, j * 8 + 7:j * 8 + 8],
                                         sbLN[:, j * 8 + 6:j * 8 + 7])
                            e.drain()
                            return e.tensor_scalar(out=sbEPQ[j][:, :], in0=slots[j][:, :],
                                                   scalar1=sbLN[:, j * 8 + 2:j * 8 + 3],
                                                   scalar2=sbLN[:, j * 8 + 7:j * 8 + 8],
                                                   op0=ALU.subtract, op1=ALU.mult)
                        prog.op("vector", mk_ap, waits=[(scD, sqr2[j])], inc=evD)
                        apd = evD.v
                        last_ap[j] = apd
                        prog.dma("sync", out[trow:trow + 128, :], sbEPQ[j][:, :], outd_,
                                 waits=[(evD, apd)])
                return (max(last_sq), outd_.v)
            pending_epilogue.append(plan_epilogue)

        if pending_epilogue:
            pending_epilogue.pop()()

    outd = S["outd"]
    if "D" not in parts:
        src = scr_x1t if "C" in parts else x_in
        for tt in range(NT):
            lw = [(outd, outd.v)]
            if tt == 0:
                lw.extend(phaseA_done)
                if "C" in parts:
                    lw.extend(phaseC_done)
            prog.dma("sync", sbO[:, :], src[tt * 128:(tt + 1) * 128, :], S["sld"], waits=lw)
            prog.dma("sync", out[tt * 128:(tt + 1) * 128, :], sbO[:, :], outd,
                     waits=[(S["sld"], S["sld"].v)])
    prog.wait("sync", outd, outd.v)

    with nc.Block() as block:
        for eng in ENGINES:
            ops = prog.ops[eng]
            if not ops:
                continue
            def body(e, ops=ops):
                for fn in ops:
                    fn(e)
            getattr(block, eng)(body)
    estack.close()
    return nc


_NC_CACHE = {}


def _get_nc(debug=()):
    key = tuple(sorted(debug))
    if key not in _NC_CACHE:
        _NC_CACHE[key] = build_nc(debug=key)
    return _NC_CACHE[key]


def _shard_inputs(inputs):
    x = np.ascontiguousarray(inputs["L0_attn_x"])
    wq = np.ascontiguousarray(inputs["L0_attn_w_q"])
    wkv = np.ascontiguousarray(inputs["L0_attn_w_kv"])
    wo = np.ascontiguousarray(inputs["L0_attn_w_out"])
    bo = np.ascontiguousarray(inputs["L0_attn_b_out"])
    sortw = np.ascontiguousarray(inputs["L0_attn_sort_w"]).reshape(SHH, 2 * DHH, NB)
    gw = np.ascontiguousarray(inputs["L0_ffn_gate_w"])
    wu = np.ascontiguousarray(inputs["L0_ffn_w_up"])
    wd = np.ascontiguousarray(inputs["L0_ffn_w_down"])
    eb = np.ascontiguousarray(inputs["L0_ffn_expert_bias"])

    gwT = np.ascontiguousarray(gw.T)
    wuT = np.ascontiguousarray(wu.transpose(0, 2, 1))
    wdT = np.ascontiguousarray(wd.transpose(0, 2, 1))
    ebr = np.ascontiguousarray(np.broadcast_to(eb[None, :], (128, NE)))

    in_maps = []
    for c in range(8):
        xc = np.ascontiguousarray(x[c])
        in_maps.append({
            "xT": np.ascontiguousarray(xc.T),
            "x_in": xc,
            "wq": wq, "wkv": wkv, "wo": wo, "bo": bo,
            "sortw": sortw, "gwT": gwT, "ebias": ebr,
            "wuT": wuT, "wdT": wdT,
        })
    return in_maps


def kernel(**inputs):
    nc = _get_nc()
    in_maps = _shard_inputs(inputs)
    res = run_bass_kernel_spmd(nc, in_maps, core_ids=list(range(8)))
    return np.stack([r["out"] for r in res.results], axis=0)


def run_debug(inputs, debug, parts=("A", "Bl", "Bs", "C", "D"), cores=8, sink_stop=99):
    nc = build_nc(debug=tuple(debug), parts=parts, sink_stop=sink_stop)
    in_maps = _shard_inputs(inputs)[:cores]
    res = run_bass_kernel_spmd(nc, in_maps, core_ids=list(range(cores)))
    return res


# version 08
# speedup vs baseline: 1.0346x; 1.0346x over previous
import numpy as np
import ml_dtypes
import concourse.bass as bass
import concourse.mybir as mybir
from concourse.bass_utils import run_bass_kernel_spmd

F32 = mybir.dt.float32
BF16 = mybir.dt.bfloat16
AF = mybir.ActivationFunctionType
ALU = mybir.AluOpType
AX = mybir.AxisListType

T = 4096
D = 1024
H, LHH, SHH, DHH = 16, 4, 12, 64
NB, BW = 64, 64
NT = T // 128
NE, HID = 8, 2048
EPS = 1e-5

ENGINES = ("sync", "tensor", "vector", "scalar", "gpsimd")


class Sem:
    def __init__(self, handle):
        self.h = handle
        self.v = 0

    def bump(self, k):
        self.v += k
        return self.v


class Prog:
    def __init__(self):
        self.ops = {e: [] for e in ENGINES}

    def emit(self, eng, fn):
        self.ops[eng].append(fn)

    def op(self, eng, build, waits=(), inc=None, inc_n=1):
        waits = tuple((s, v) for (s, v) in waits if v > 0)
        if inc is not None:
            inc.bump(inc_n)

        def f(e, build=build, waits=waits, inc=inc, inc_n=inc_n):
            for s, v in waits:
                e.wait_ge(s.h, v)
            ins = build(e)
            if inc is not None:
                ins.then_inc(inc.h, inc_n)

        self.emit(eng, f)

    def dma(self, qeng, out_ap, in_ap, sem, waits=()):
        waits = tuple((s, v) for (s, v) in waits if v > 0)
        sem.bump(16)

        def f(e, out_ap=out_ap, in_ap=in_ap, waits=waits, sem=sem):
            for s, w in waits:
                e.wait_ge(s.h, w)
            e.dma_start(out=out_ap, in_=in_ap).then_inc(sem.h, 16)

        self.emit(qeng, f)
        return sem.v

    def wait(self, eng, sem, v):
        if v > 0:
            self.emit(eng, lambda e, sem=sem, v=v: e.wait_ge(sem.h, v))


def build_nc(debug=(), parts=("A", "Bl", "Bs", "C", "D"), sink_stop=99):
    nc = bass.Bass("TRN2", target_bir_lowering=False)

    xT = nc.dram_tensor("xT", [D, T], F32, kind="ExternalInput").ap()
    x_in = nc.dram_tensor("x_in", [T, D], F32, kind="ExternalInput").ap()
    wq = nc.dram_tensor("wq", [D, D], F32, kind="ExternalInput").ap()
    wkv = nc.dram_tensor("wkv", [D, 2 * D], F32, kind="ExternalInput").ap()
    wo = nc.dram_tensor("wo", [D, D], F32, kind="ExternalInput").ap()
    bo = nc.dram_tensor("bo", [D], F32, kind="ExternalInput").ap()
    sortw = nc.dram_tensor("sortw", [SHH, 2 * DHH, NB], F32, kind="ExternalInput").ap()
    gwT = nc.dram_tensor("gwT", [D, NE], F32, kind="ExternalInput").ap()
    ebias = nc.dram_tensor("ebias", [128, NE], F32, kind="ExternalInput").ap()
    wuT = nc.dram_tensor("wuT", [NE, D, HID], F32, kind="ExternalInput").ap()
    wdT = nc.dram_tensor("wdT", [NE, HID, D], F32, kind="ExternalInput").ap()
    out = nc.dram_tensor("out", [T, D], F32, kind="ExternalOutput").ap()

    def scratch(name, shape, dtype):
        kind = "ExternalOutput" if name in debug else "Internal"
        return nc.dram_tensor(name, shape, dtype, kind=kind).ap()

    scr_q = scratch("scr_q", [8, 128, T], BF16)
    scr_kl = scratch("scr_kl", [2, 128, T], BF16)
    scr_kv = scratch("scr_kv", [SHH, 128, T], BF16)
    scr_vt = scratch("scr_vt", [8, NT, 128, 128], BF16)
    scr_x1f = scratch("scr_x1f", [8, 128, T], BF16)
    scr_x1t = scratch("scr_x1t", [T, D], F32)
    scr_wub = scratch("scr_wub", [NE, D, HID], BF16)
    scr_wdb = scratch("scr_wdb", [NE, HID, D], BF16)
    dbg_ao = scratch("dbg_ao", [T, D], BF16) if "dbg_ao" in debug else None
    dbg_w2 = scratch("dbg_w2", [T, NE], F32) if "dbg_w2" in debug else None

    ident_h = nc.inline_tensor(np.eye(128, dtype=ml_dtypes.bfloat16), "ident128").ap()
    ones_h = nc.inline_tensor(np.ones((1, 128), dtype=ml_dtypes.bfloat16), "onesrow").ap()
    NEG = -1.5e38
    _ml = np.zeros((128, 256), dtype=np.float32)
    _ml[0:64, 192:256] = NEG
    _ml[64:128, 0:64] = NEG
    maskl_h = nc.inline_tensor(_ml.astype(ml_dtypes.bfloat16), "maskl").ap()
    _ms = np.zeros((128, 512), dtype=np.float32)
    for _half in range(2):
        _c0 = _half * 256
        _ms[0:64, _c0 + 64:_c0 + 128] = NEG
        _ms[64:128, _c0:_c0 + 64] = NEG
        _ms[0:64, _c0 + 192:_c0 + 256] = NEG
        _ms[64:128, _c0 + 128:_c0 + 192] = NEG
    masks_h = nc.inline_tensor(_ms.astype(ml_dtypes.bfloat16), "masks").ap()
    negrow_h = nc.inline_tensor(np.full((1, 64), NEG, dtype=ml_dtypes.bfloat16), "negrow").ap()
    epscol_h = nc.inline_tensor(np.full((128, 1), EPS, dtype=np.float32), "epscol").ap()

    ar1 = nc.alloc_sbuf_tensor("arena1", [128, 65536], mybir.dt.uint8).ap()
    ar2 = nc.alloc_sbuf_tensor("arena2", [128, 65536], mybir.dt.uint8).ap()
    ar3 = nc.alloc_sbuf_tensor("arena3", [128, 32768], mybir.dt.uint8).ap()
    ar4 = nc.alloc_sbuf_tensor("arena4", [128, 32768], mybir.dt.uint8).ap()
    arS = nc.alloc_sbuf_tensor("arenaS", [128, 15680], mybir.dt.uint8).ap()

    def sb(arena, off, dtype, cols, rows=128):
        es = mybir.dt.size(dtype)
        assert off % 32 == 0 and off + cols * es <= arena.shape[1] * 1
        return arena.bitcast(dtype)[0:rows, off // es: off // es + cols]

    sbXF = sb(ar1, 0, BF16, 8 * T)
    sbAO = sb(ar1, 0, BF16, NT * 1024)
    sbWU = [sb(ar1, i * 32768, BF16, 8 * HID) for i in range(2)]

    sbWQ = sb(ar2, 0, BF16, 8 * 1024)
    sbWKV = sb(ar2, 16384, BF16, 8 * 2048)
    sbQH = [sb(ar2, i * 8192, BF16, T) for i in range(2)]
    sbKH = [sb(ar2, 16384 + i * 8192, BF16, T) for i in range(2)]
    sbVT = [sb(ar2, 32768 + i * 8192, BF16, NT * 128) for i in range(2)]
    sbWO = sb(ar2, 49152, BF16, 8 * 1024)
    sbWD = [sb(ar2, i * 32768, BF16, 16 * 1024) for i in range(2)]
    sbHR = [sb(arS, 13632 + i * 1024, BF16, 512) for i in range(2)]

    sbSTG = sb(ar3, 0, BF16, 4 * 1024)
    sbKVB = sb(ar3, 0, BF16, BW * 128)
    sbRKV = sb(ar3, 16384, BF16, T)
    sbACC = sb(ar3, 0, F32, 8 * 1024)
    sbCz = sb(ar3, 0, F32, 1024)
    sbCsq = sb(ar3, 4096, F32, 1024)
    sbCx = sb(ar3, 8192, F32, 1024)
    sbCa = sb(ar3, 12288, BF16, 1024)
    sbCx1 = sb(ar3, 14336, BF16, 1024)
    sbCxf = sb(ar3, 16384, BF16, 1024)

    sbX1Q = sb(ar4, 0, BF16, 8 * 1024)
    sbHE = sb(ar4, 16384, BF16, 16 * 512)
    sbPP = sb(ar4, 0, BF16, 512)
    sbP2 = sb(ar4, 1024, BF16, 8 * 128)
    sbP2T = sb(ar4, 3072, BF16, 8 * 64)
    sbBVR = sb(ar4, 4096, BF16, 8 * 64)
    sbFE = sb(ar4, 5120, BF16, 64)
    sbRT = sb(ar4, 5248, BF16, 64)
    sbRS = sb(ar4, 5376, F32, 64, rows=64)
    sbRE = sb(ar4, 5632, F32, 64, rows=64)
    sbST1 = sb(ar4, 5888, F32, 16)
    sbST2 = sb(ar4, 5952, F32, 16)
    sbST3 = sb(ar4, 6016, F32, 16)
    sbFE32 = sb(ar4, 6144, F32, 64)
    sbWX = sb(ar4, 8192, BF16, 8 * 512)

    sbID = sb(arS, 0, BF16, 128)
    sbSORT = sb(arS, 256, BF16, SHH * NB)
    sbGW = sb(arS, 1792, BF16, 64)
    sbBO = sb(arS, 1920, BF16, 1024, rows=1)
    sbONE = sb(arS, 3968, BF16, 128, rows=1)
    sbEB = sb(arS, 4224, F32, NE)
    sbW2 = sb(arS, 4480, F32, 8 * NE)
    sbGS = sb(arS, 4736, F32, 8 * NE)
    sbGB = sb(arS, 4992, F32, 8 * NE)
    sbTOP = sb(arS, 5248, F32, 8)
    sbLN = sb(arS, 5312, F32, 16)
    sbEPSC = sb(arS, 5376, F32, 1)
    sbMKL = sb(ar4, 3072, BF16, 256)
    sbMKS = sb(ar4, 3584, BF16, 512)
    sbNEGR = sb(ar4, 4608, BF16, 64, rows=1)
    sbO = sb(arS, 5440, F32, 1024)
    sbX1T = sb(arS, 9536, F32, 1024)
    sbEPQ = [sb(ar4, 16384 + i * 4096, F32, 1024) for i in range(2)]

    pbank = [nc.alloc_psum_tensor(f"pb{i}", [128, 512], F32).ap() for i in range(6)]
    ptr = [nc.alloc_psum_tensor(f"pt{i}", [128, 1024], BF16).ap() for i in range(2)]

    import contextlib
    estack = contextlib.ExitStack()
    semnames = [
        "gld", "sld", "sst", "gst",
        "stg0", "stg1", "stg2", "stg3",
        "mmA", "evA",
        "mmB", "evB", "scB", "hld0", "hld1",
        "mmC", "evC", "scC", "xld", "x1t", "x1f", "gld2",
        "mmD", "evD", "scD", "wld0", "wld1", "pcl0", "pcs0", "pcl1", "pcs1", "xq", "xt2",
        "outd",
    ]
    S = {nm: Sem(estack.enter_context(nc.semaphore(nm))) for nm in semnames}

    prog = Prog()

    for c in range(8):
        prog.dma("gpsimd", sbXF[:, c * T:(c + 1) * T], xT[c * 128:(c + 1) * 128, :], S["gld"])
    for c in range(8):
        prog.dma("gpsimd", sbWQ[:, c * 1024:(c + 1) * 1024], wq[c * 128:(c + 1) * 128, :], S["gld"])
    for c in range(8):
        prog.dma("gpsimd", sbWKV[:, c * 2048:(c + 1) * 2048], wkv[c * 128:(c + 1) * 128, :], S["gld"])
    for hh in range(SHH):
        prog.dma("gpsimd", sbSORT[:, hh * NB:(hh + 1) * NB], sortw[hh], S["gld"])
    for c in range(8):
        prog.dma("gpsimd", sbGW[:, c * 8:(c + 1) * 8], gwT[c * 128:(c + 1) * 128, :], S["gld"])
    prog.dma("gpsimd", sbBO[:, :], bo.rearrange("(a b) -> a b", a=1), S["gld"])
    setup_g = S["gld"].v
    prog.dma("sync", sbID[:, :], ident_h, S["sld"])
    prog.dma("sync", sbONE[:, :], ones_h, S["sld"])
    prog.dma("sync", sbEB[:, :], ebias, S["sld"])
    prog.dma("sync", sbMKL[:, :], maskl_h, S["sld"])
    prog.dma("sync", sbMKS[:, :], masks_h, S["sld"])
    prog.dma("sync", sbNEGR[:, :], negrow_h, S["sld"])
    prog.dma("sync", sbEPSC[:, :], epscol_h, S["sld"])
    setup_s = S["sld"].v
    prog.wait("tensor", S["sld"], setup_s)
    prog.wait("vector", S["sld"], setup_s)
    prog.wait("scalar", S["sld"], setup_s)
    for c in range(8):
        prog.dma("gpsimd", sbWO[:, c * 1024:(c + 1) * 1024], wo[c * 128:(c + 1) * 128, :], S["gld2"])
    setup_g2 = S["gld2"].v

    sbPC = [sb(ar4, 16384 + i * 4096, BF16, 2048) for i in range(2)]
    pcl = [S["pcl0"], S["pcl1"]]
    pcs = [S["pcs0"], S["pcs1"]]
    pc_jobs = []
    for e_ in range(NE):
        for ch in range(8):
            pc_jobs.append((wuT[e_, ch * 128:(ch + 1) * 128, :],
                            scr_wub[e_, ch * 128:(ch + 1) * 128, :]))
    for e_ in range(NE):
        for ch in range(16):
            pc_jobs.append((wdT[e_, ch * 128:(ch + 1) * 128, :],
                            scr_wdb[e_, ch * 128:(ch + 1) * 128, :]))
    for j, (src, dst) in enumerate(pc_jobs):
        slot = j % 2
        ncols = src.shape[1]
        w = [(pcs[slot], pcs[slot].v)] if j >= 2 else []
        prog.dma("gpsimd", sbPC[slot][:, 0:ncols], src, pcl[slot], waits=w)
        prog.dma("gpsimd", dst, sbPC[slot][:, 0:ncols], pcs[slot],
                 waits=[(pcl[slot], pcl[slot].v)])
    precast_done = [(pcs[0], pcs[0].v), (pcs[1], pcs[1].v)]

    mmA, evA = S["mmA"], S["evA"]
    stgsem = [S["stg0"], S["stg1"], S["stg2"], S["stg3"]]
    unit = [0]

    def proj_unit(mk_mms, evacs, stores):
        k = unit[0]
        bank = pbank[k % 2]
        slot = stgsem[k % 4]
        prog.op("tensor", (lambda e, bank=bank: mk_mms(e, bank)),
                waits=[(S["gld"], setup_g), (evA, k - 1)], inc=mmA)
        mw = mmA.v

        evw = [(mmA, mw), (slot, slot.v)]

        def ev(e, bank=bank, evacs=evacs):
            ins = None
            for fn in evacs:
                ins = fn(e, bank)
            return ins
        prog.op("vector", ev, waits=evw, inc=evA)
        ew = evA.v
        for dst, src in stores:
            prog.dma("sync", dst, src, slot, waits=[(evA, ew)])
        unit[0] += 1

    def stg(k, r0, nr, cols=512):
        s0 = (k % 4) * 1024
        return sbSTG[r0:r0 + nr, s0:s0 + cols]

    for pc in range(8):
        for sl in range(8):
            k = unit[0]
            def mk(e, bank, pc=pc, sl=sl):
                ins = None
                for dc in range(8):
                    ins = e.matmul(bank[:, :],
                                   sbWQ[:, dc * 1024 + pc * 128: dc * 1024 + pc * 128 + 128],
                                   sbXF[:, dc * T + sl * 512: dc * T + sl * 512 + 512],
                                   start=(dc == 0), stop=(dc == 7))
                return ins
            def ev0(e, bank, k=k):
                return e.tensor_copy(out=stg(k, 0, 128), in_=bank[:, :])
            proj_unit(mk, [ev0],
                      [(scr_q[pc, :, sl * 512:(sl + 1) * 512], stg(k, 0, 128))])

    for pc in range(8):
        for sl in range(8):
            k = unit[0]
            def mk(e, bank, pc=pc, sl=sl):
                ins = None
                for dc in range(8):
                    ins = e.matmul(bank[:, :],
                                   sbWKV[:, dc * 2048 + pc * 128: dc * 2048 + pc * 128 + 128],
                                   sbXF[:, dc * T + sl * 512: dc * T + sl * 512 + 512],
                                   start=(dc == 0), stop=(dc == 7))
                return ins
            def ev0(e, bank, k=k):
                return e.tensor_copy(out=stg(k, 0, 128), in_=bank[:, :])
            if pc < 2:
                proj_unit(mk, [ev0],
                          [(scr_kl[pc, :, sl * 512:(sl + 1) * 512], stg(k, 0, 128))])
            else:
                hs0, hs1 = 2 * pc - 4, 2 * pc - 3
                full = stg(k, 0, 128)
                proj_unit(mk, [ev0],
                          [(scr_kv[hs0, 0:64, sl * 512:(sl + 1) * 512], full[0:64, :]),
                           (scr_kv[hs1, 0:64, sl * 512:(sl + 1) * 512], full[64:128, :])])

    for vc in range(2, 8):
        for sl in range(8):
            k = unit[0]
            def mk(e, bank, vc=vc, sl=sl):
                ins = None
                for dc in range(8):
                    ins = e.matmul(bank[:, :],
                                   sbWKV[:, dc * 2048 + 1024 + vc * 128: dc * 2048 + 1024 + vc * 128 + 128],
                                   sbXF[:, dc * T + sl * 512: dc * T + sl * 512 + 512],
                                   start=(dc == 0), stop=(dc == 7))
                return ins
            def ev0(e, bank, k=k):
                return e.tensor_copy(out=stg(k, 0, 128), in_=bank[:, :])
            hs0, hs1 = 2 * vc - 4, 2 * vc - 3
            full = stg(k, 0, 128)
            proj_unit(mk, [ev0],
                      [(scr_kv[hs0, 64:128, sl * 512:(sl + 1) * 512], full[0:64, :]),
                       (scr_kv[hs1, 64:128, sl * 512:(sl + 1) * 512], full[64:128, :])])

    for tt in range(NT):
        for vh in range(2):
            k = unit[0]
            def mk(e, bank, tt=tt, vh=vh):
                ins = None
                for dc in range(8):
                    ins = e.matmul(bank[:, :],
                                   sbXF[:, dc * T + tt * 128: dc * T + tt * 128 + 128],
                                   sbWKV[:, dc * 2048 + 1024 + vh * 512: dc * 2048 + 1024 + vh * 512 + 512],
                                   start=(dc == 0), stop=(dc == 7))
                return ins
            def ev0(e, bank, k=k):
                return e.tensor_copy(out=stg(k, 0, 128), in_=bank[:, :])
            stores = []
            for pp in range(4):
                p = 4 * vh + pp
                stores.append((scr_vt[p, tt], stg(k, 0, 128)[:, pp * 128:(pp + 1) * 128]))
            proj_unit(mk, [ev0], stores)

    phaseA_done = [(s, s.v) for s in stgsem]

    mmB, scB, evB = S["mmB"], S["scB"], S["evB"]
    hld = [S["hld0"], S["hld1"]]
    slot_free_at = [0, 0]

    def load_local_pair(p):
        s = p % 2
        w = list(phaseA_done) if p == 0 else []
        if slot_free_at[s] > 0:
            w.append((mmB, slot_free_at[s]))
        prog.dma("sync", sbQH[s][:, :], scr_q[p], hld[s], waits=w)
        prog.dma("sync", sbKH[s][:, :], scr_kl[p], hld[s], waits=w)
        for tc in range(4):
            prog.dma("sync",
                     sbVT[s].rearrange("p (t c) -> p t c", c=128)[:, tc * 8:(tc + 1) * 8, :],
                     scr_vt[p, tc * 8:(tc + 1) * 8].transpose([1, 0, 2]),
                     hld[s], waits=w)
        return hld[s].v

    def load_sink_head(hs):
        s = hs % 2
        w = []
        if slot_free_at[s] > 0:
            w.append((mmB, slot_free_at[s]))
        h = hs + 4
        pair = h // 2
        rb = (h % 2) * 64
        prog.dma("sync", sbQH[s][0:64, :], scr_q[pair][rb:rb + 64, :], hld[s], waits=w)
        prog.dma("sync", sbKH[s][:, :], scr_kv[hs], hld[s], waits=w)
        for tc in range(4):
            prog.dma("sync",
                     sbVT[s].rearrange("p (t c) -> p t c", c=128)[:, tc * 8:(tc + 1) * 8, :],
                     scr_vt[pair, tc * 8:(tc + 1) * 8].transpose([1, 0, 2]),
                     hld[s], waits=w)
        return hld[s].v

    lu = [0]
    l_tr_mm = []
    l_av_mm = []
    l_pt_ev = []
    l_av_ev = []
    l_exp = []

    def local_head_tile(s, h, tt, ldv):
        u = lu[0]
        rb = (h % 2) * 64
        hv = (h % 2) * 64
        sbank = pbank[u % 2]
        obank = pbank[2 + u % 2]
        pslot = (u % 2) * 256
        wslot = (u % 2) * 1024

        if tt == 0:
            c0, c1, p0 = 0, 192, 64
        elif tt == NT - 1:
            c0, c1, p0 = tt * 128 - 64, tt * 128 + 128, 0
        else:
            c0, c1, p0 = tt * 128 - 64, tt * 128 + 192, 0

        def mk_s(e, s=s, rb=rb, tt=tt, c0=c0, c1=c1, p0=p0, sbank=sbank):
            ins = e.matmul(sbank[:, p0:p0 + (c1 - c0)],
                           sbQH[s][rb:rb + 64, tt * 128:tt * 128 + 128],
                           sbKH[s][rb:rb + 64, c0:c1],
                           start=True, stop=True)
            if tt == 0:
                ins = e.matmul(sbank[:, 0:64], sbONE[:, :], sbNEGR[:, :],
                               start=True, stop=True)
            if tt == NT - 1:
                ins = e.matmul(sbank[:, 192:256], sbONE[:, :], sbNEGR[:, :],
                               start=True, stop=True)
            return ins
        sw = [(hld[s], ldv)]
        if u >= 2:
            sw.append((scB, l_exp[u - 2]))
        prog.op("tensor", mk_s, waits=sw, inc=mmB)
        smm = mmB.v

        prog.op("vector",
                lambda e, sbank=sbank:
                    e.tensor_tensor(sbank[:, 0:256], sbank[:, 0:256], sbMKL[:, :],
                                    ALU.add),
                waits=[(mmB, smm)], inc=evB)
        mzd = evB.v

        expw = [(evB, mzd)]
        if u >= 2:
            expw.append((mmB, l_tr_mm[u - 2]))

        prog.op("scalar",
                lambda e, sbank=sbank, pslot=pslot:
                    e.activation(out=sbPP[:, pslot:pslot + 256], in_=sbank[:, 0:256],
                                 func=AF.Exp, accum_out=sbST1[:, 0:1]),
                waits=expw, inc=scB)
        l_exp.append(scB.v)
        sexp = scB.v

        prog.op("vector",
                lambda e: e.reciprocal(sbST2[:, 0:1], sbST1[:, 0:1]),
                waits=[(scB, sexp)], inc=evB)
        maskd = evB.v

        pb0 = (u % 2) * 512
        trw = [(evB, maskd)]
        if u >= 2:
            trw.append((evB, l_pt_ev[u - 2]))

        def mk_tr(e, pslot=pslot, tt=tt, pb0=pb0):
            ins = None
            if tt > 0:
                ins = e.transpose(ptr[0][64:128, pb0:pb0 + 128], sbPP[:, pslot:pslot + 64],
                                  sbID[:, :])
            ins = e.transpose(ptr[0][:, pb0 + 128:pb0 + 256], sbPP[:, pslot + 64:pslot + 192],
                              sbID[:, :])
            if tt < NT - 1:
                ins = e.transpose(ptr[0][0:64, pb0 + 256:pb0 + 384], sbPP[:, pslot + 192:pslot + 256],
                                  sbID[:, :])
            return ins
        prog.op("tensor", mk_tr, waits=trw, inc=mmB)
        l_tr_mm.append(mmB.v)
        trd = mmB.v

        pw = [(mmB, trd)]
        if u >= 2:
            pw.append((mmB, l_av_mm[u - 2]))
        def ev_pt(e, wslot=wslot, pb0=pb0, tt=tt):
            if tt > 0:
                e.tensor_copy(out=sbWX[64:128, wslot:wslot + 128],
                              in_=ptr[0][64:128, pb0:pb0 + 128])
            ins = e.tensor_copy(out=sbWX[:, wslot + 128:wslot + 256],
                                in_=ptr[0][:, pb0 + 128:pb0 + 256])
            if tt < NT - 1:
                ins = e.tensor_copy(out=sbWX[0:64, wslot + 256:wslot + 384],
                                    in_=ptr[0][0:64, pb0 + 256:pb0 + 384])
            return ins
        prog.op("vector", ev_pt, waits=pw, inc=evB)
        l_pt_ev.append(evB.v)
        ptd = evB.v

        avw = [(evB, ptd)]
        if u >= 2:
            avw.append((evB, l_av_ev[u - 2]))

        def mk_av(e, s=s, hv=hv, tt=tt, wslot=wslot, obank=obank):
            first = True
            ins = None
            if tt > 0:
                ins = e.matmul(obank[:, 0:64], sbWX[64:128, wslot:wslot + 128],
                               sbVT[s][64:128, (tt - 1) * 128 + hv:(tt - 1) * 128 + hv + 64],
                               start=first, stop=False)
                first = False
            ins = e.matmul(obank[:, 0:64], sbWX[:, wslot + 128:wslot + 256],
                           sbVT[s][:, tt * 128 + hv:tt * 128 + hv + 64],
                           start=first, stop=(tt == NT - 1))
            if tt < NT - 1:
                ins = e.matmul(obank[:, 0:64], sbWX[0:64, wslot + 256:wslot + 384],
                               sbVT[s][0:64, (tt + 1) * 128 + hv:(tt + 1) * 128 + hv + 64],
                               start=False, stop=True)
            return ins
        prog.op("tensor", mk_av, waits=avw, inc=mmB)
        l_av_mm.append(mmB.v)
        avd = mmB.v

        prog.op("vector",
                lambda e, obank=obank, tt=tt, h=h:
                    e.tensor_scalar(out=sbAO[:, tt * 1024 + h * 64:tt * 1024 + h * 64 + 64],
                                    in0=obank[:, 0:64], scalar1=sbST2[:, 0:1],
                                    scalar2=None, op0=ALU.mult),
                waits=[(mmB, avd)], inc=evB)
        l_av_ev.append(evB.v)
        lu[0] += 1


    if "Bl" in parts:
        for p in range(2):
            ldv = load_local_pair(p)
            for hh in range(2):
                h = 2 * p + hh
                for tt in range(NT):
                    local_head_tile(p % 2, h, tt, ldv)
            slot_free_at[p % 2] = mmB.v

    sbFEb = sb(ar4, 6400, BF16, 64)
    sbRTb = sb(ar4, 6528, BF16, 64)
    sbRSb = sb(ar4, 6656, F32, 64, rows=64)
    sbREb = sb(ar4, 6912, F32, 64, rows=64)
    sbFE32b = sb(ar4, 7168, F32, 64)
    sbST1b = sb(ar4, 7424, F32, 16)
    sbST2b = sb(ar4, 7488, F32, 16)
    sbST3b = sb(ar4, 7552, F32, 16)
    sbRKV2 = sb(ar3, 24576, BF16, T // 2)
    sbRKV2 = sb(ar3, 24576, BF16, T)

    CTXS = [
        dict(FE=sbFE, FE32=sbFE32, RT=sbRT, RS=sbRS, RE=sbRE,
             ST1=sbST1, ST2=sbST2, ST3=sbST3, RKV=sbRKV, rr=0,
             sbank=pbank[0], obank=pbank[2], rbank=pbank[4], ptrX=ptr[0],
             po=0, vo=0),
        dict(FE=sbFEb, FE32=sbFE32b, RT=sbRTb, RS=sbRSb, RE=sbREb,
             ST1=sbST1b, ST2=sbST2b, ST3=sbST3b, RKV=sbRKV2, rr=64,
             sbank=pbank[1], obank=pbank[3], rbank=pbank[5], ptrX=ptr[1],
             po=512, vo=256),
    ]

    def sink_stages(hs, ldv):
        pi = hs % 2
        C = CTXS[pi]
        s = pi
        h = hs + 4
        rr = C["rr"]
        FE, FE32, RT, RS, RE = C["FE"], C["FE32"], C["RT"], C["RS"], C["RE"]
        ST1, ST2, ST3, RKV = C["ST1"], C["ST2"], C["ST3"], C["RKV"]
        sbank, obank, rbank, ptrX = C["sbank"], C["obank"], C["rbank"], C["ptrX"]
        po, vo = C["po"], C["vo"]

        def mk_f(e, s=s, FE32=FE32, FE=FE):
            e.tensor_reduce(out=FE32[0:64, 0:64],
                            in_=sbQH[s][0:64, :].rearrange("p (v t) -> p v t", v=NB),
                            axis=AX.X, op=ALU.add, opt_input=False, opt_output=False)
            e.tensor_reduce(out=FE32[64:128, 0:64],
                            in_=sbKH[s][0:64, :].rearrange("p (v t) -> p v t", v=NB),
                            axis=AX.X, op=ALU.add, opt_input=False, opt_output=False)
            e.drain()
            return e.tensor_copy(out=FE[:, 0:64], in_=FE32[:, 0:64])
        prog.op("vector", mk_f, waits=[(hld[s], ldv)], inc=evB)
        fd = evB.v
        yield

        prog.op("tensor",
                lambda e, hs=hs, FE=FE, rbank=rbank:
                    e.matmul(rbank[0:64, 0:64], FE[:, 0:64],
                             sbSORT[:, hs * 64:hs * 64 + 64], start=True, stop=True),
                waits=[(evB, fd)], inc=mmB)
        rmm = mmB.v
        yield

        def mk_rel(e, RS=RS, ST1=ST1, rbank=rbank):
            e.tensor_scalar_max(RS[0:64, :], rbank[0:64, 0:64], 0.0)
            e.drain()
            return e.reduce_max(out=ST1[0:64, 1:2], in_=RS[0:64, :], axis=AX.X,
                                negate=True, opt_input=False, opt_output=False)
        prog.op("vector", mk_rel, waits=[(mmB, rmm)], inc=evB)
        reld = evB.v
        yield

        prog.op("scalar",
                lambda e, RE=RE, RS=RS, ST1=ST1:
                    e.activation(out=RE[0:64, :], in_=RS[0:64, :], func=AF.Exp,
                                 bias=ST1[0:64, 1:2], accum_out=ST1[0:64, 3:4]),
                waits=[(evB, reld)], inc=scB)
        red = scB.v
        yield

        def mk_rn(e, FE=FE, RE=RE, ST1=ST1):
            e.reciprocal(ST1[0:64, 4:5], ST1[0:64, 3:4])
            e.drain()
            return e.tensor_scalar(out=FE[0:64, 0:64], in0=RE[0:64, :],
                                   scalar1=ST1[0:64, 4:5], scalar2=None, op0=ALU.mult)
        prog.op("vector", mk_rn, waits=[(scB, red)], inc=evB)
        rnd = evB.v
        yield

        prog.op("tensor",
                lambda e, FE=FE, ptrX=ptrX, rr=rr:
                    e.transpose(ptrX[rr:rr + 64, 0:64], FE[0:64, 0:64],
                                sbID[0:64, 0:64]),
                waits=[(evB, rnd)], inc=mmB)
        rtt = mmB.v
        prog.op("vector",
                lambda e, RT=RT, ptrX=ptrX, rr=rr:
                    e.tensor_copy(out=RT[rr:rr + 64, :], in_=ptrX[rr:rr + 64, 0:64]),
                waits=[(mmB, rtt)], inc=evB)
        rtd = evB.v
        yield

        if sink_stop < 2:
            return
        kv_ev = rtd
        r_ev_prev = 0
        for g in range(8):
            def mk_kvb(e, s=s, g=g, ptrX=ptrX, rr=rr):
                ins = None
                for j in range(8):
                    tp = g * 8 + j
                    ins = e.transpose(ptrX[rr:rr + 64, j * 128:(j + 1) * 128],
                                      sbKH[s].rearrange("p (v t) -> p t v", v=NB)[:, tp, :],
                                      sbID[:, :])
                return ins
            prog.op("tensor", mk_kvb, waits=[(evB, kv_ev)], inc=mmB)
            kv_mm = mmB.v
            prog.op("vector",
                    lambda e, g=g, ptrX=ptrX, rr=rr:
                        e.tensor_copy(out=sbKVB[rr:rr + 64, g * 1024:(g + 1) * 1024],
                                      in_=ptrX[rr:rr + 64, 0:1024]),
                    waits=[(mmB, kv_mm)], inc=evB)
            kv_ev = evB.v
            yield

            def mk_r(e, g=g, rbank=rbank, rr=rr, RT=RT):
                ins = None
                for j in range(8):
                    tp = g * 8 + j
                    ins = e.matmul(rbank[:, j * 64:(j + 1) * 64],
                                   sbKVB[rr:rr + 64, tp * 128:(tp + 1) * 128],
                                   RT[rr:rr + 64, :], start=True, stop=True)
                return ins
            rw = [(evB, kv_ev)]
            if r_ev_prev > 0:
                rw.append((evB, r_ev_prev))
            prog.op("tensor", mk_r, waits=rw, inc=mmB)
            r_mm = mmB.v

            def ev_r(e, g=g, rbank=rbank, RKV=RKV):
                src = rbank[:, 0:512].rearrange("p (j u) -> p j u", j=8)
                dst = RKV.rearrange("p (u t) -> p t u", u=NB)[:, g * 8:(g + 1) * 8, :]
                return e.tensor_copy(out=dst, in_=src)
            prog.op("vector", ev_r, waits=[(mmB, r_mm)], inc=evB)
            r_ev_prev = evB.v
            yield
        rkvd = evB.v

        if sink_stop < 4:
            return
        P2T = sbWX
        p2to = pi * 2048
        prev_exp = 0
        prev_tr = 0
        prev_et = 0
        prev_av = 0
        prev_ev = 0
        for b in range(16):
            t0 = 2 * b
            def mk_s2(e, s=s, b=b, t0=t0, sbank=sbank, RKV=RKV):
                ins = None
                for half in range(2):
                    tl = t0 + half
                    q_l = sbQH[s][0:64, tl * 128:(tl + 1) * 128]
                    ins = e.matmul(sbank[:, half * 256:half * 256 + 128],
                                   q_l, sbKH[s][0:64, tl * 128:(tl + 1) * 128],
                                   start=True, stop=True)
                    ins = e.matmul(sbank[:, half * 256 + 128:half * 256 + 256],
                                   q_l, RKV[0:64, tl * 128:(tl + 1) * 128],
                                   start=True, stop=True)
                return ins
            sw2 = [(evB, rkvd)]
            if prev_exp > 0:
                sw2.append((scB, prev_exp))
            prog.op("tensor", mk_s2, waits=sw2, inc=mmB)
            s2mm = mmB.v
            yield

            prog.op("vector",
                    lambda e, sbank=sbank:
                        e.tensor_tensor(sbank[:, 0:512], sbank[:, 0:512], sbMKS[:, :],
                                        ALU.add),
                    waits=[(mmB, s2mm)], inc=evB)
            mz2d = evB.v

            xw2 = [(evB, mz2d)]
            if prev_tr > 0:
                xw2.append((mmB, prev_tr))

            def mk_ex2(e, sbank=sbank, po=po, b=b, ST2=ST2):
                e.activation(out=sbP2[:, po:po + 256], in_=sbank[:, 0:256],
                             func=AF.Exp, scale=0.125,
                             accum_out=ST2[:, (b % 2) * 2:(b % 2) * 2 + 1])
                return e.activation(out=sbP2[:, po + 256:po + 512], in_=sbank[:, 256:512],
                                    func=AF.Exp, scale=0.125,
                                    accum_out=ST2[:, (b % 2) * 2 + 1:(b % 2) * 2 + 2])
            prog.op("scalar", mk_ex2, waits=xw2, inc=scB)
            prev_exp = scB.v
            s2exp = scB.v
            yield

            prog.op("vector",
                    lambda e, b=b, ST2=ST2, ST3=ST3:
                        e.reciprocal(ST3[:, (b % 2) * 2:(b % 2) * 2 + 2],
                                     ST2[:, (b % 2) * 2:(b % 2) * 2 + 2]),
                    waits=[(scB, s2exp)], inc=evB)
            sumd = evB.v
            yield

            def mk_t2(e, b=b, po=po, ptrX=ptrX, RKV=RKV):
                ins = None
                for blk in range(4):
                    ins = e.transpose(ptrX[:, blk * 128:(blk + 1) * 128],
                                      sbP2[:, po + blk * 128:po + (blk + 1) * 128],
                                      sbID[:, :])
                for j in range(4):
                    u = 4 * b + j
                    rbase = (u % 2) * 64
                    half = j // 2
                    ins = e.transpose(ptrX[rbase:rbase + 64, 512 + half * 64:512 + (half + 1) * 64],
                                      RKV[64:128, u * 64:(u + 1) * 64],
                                      sbID[64:128, 64:128])
                return ins
            t2w = [(evB, sumd)]
            if prev_et > 0:
                t2w.append((evB, prev_et))
            prog.op("tensor", mk_t2, waits=t2w, inc=mmB)
            prev_tr = mmB.v
            t2d = mmB.v
            yield

            def ev_t2(e, ptrX=ptrX, p2to=p2to):
                e.tensor_copy(out=P2T[:, p2to:p2to + 512], in_=ptrX[:, 0:512])
                return e.tensor_copy(out=P2T[:, p2to + 512:p2to + 640],
                                     in_=ptrX[:, 512:640])
            ew2 = [(mmB, t2d)]
            if prev_av > 0:
                ew2.append((mmB, prev_av))
            prog.op("vector", ev_t2, waits=ew2, inc=evB)
            prev_et = evB.v
            t2e = evB.v
            yield

            def mk_v2(e, s=s, t0=t0, obank=obank, p2to=p2to, h=h):
                hv = (h % 2) * 64
                ins = None
                for half in range(2):
                    tl = t0 + half
                    oc = half * 64
                    ins = e.matmul(obank[:, oc:oc + 64],
                                   P2T[:, p2to + half * 256:p2to + half * 256 + 128],
                                   sbVT[s][:, tl * 128 + hv:tl * 128 + hv + 64],
                                   start=True, stop=False)
                    ins = e.matmul(obank[:, oc:oc + 64],
                                   P2T[:, p2to + half * 256 + 128:p2to + half * 256 + 256],
                                   P2T[:, p2to + 512 + half * 64:p2to + 512 + (half + 1) * 64],
                                   start=False, stop=True)
                return ins
            avw = [(evB, t2e)]
            if prev_ev > 0:
                avw.append((evB, prev_ev))
            prog.op("tensor", mk_v2, waits=avw, inc=mmB)
            prev_av = mmB.v
            v2d = mmB.v
            yield

            def ev_v2(e, b=b, t0=t0, h=h, obank=obank, ST3=ST3):
                ins = None
                for half in range(2):
                    tl = t0 + half
                    col = tl * 1024 + h * 64
                    ins = e.tensor_scalar(
                        out=sbAO[:, col:col + 64],
                        in0=obank[:, half * 64:half * 64 + 64],
                        scalar1=ST3[:, (b % 2) * 2 + half:(b % 2) * 2 + half + 1],
                        scalar2=None, op0=ALU.mult)
                return ins
            prog.op("vector", ev_v2, waits=[(mmB, v2d)], inc=evB)
            prev_ev = evB.v
            yield

    if "Bs" in parts:
        for hp in range(SHH // 2):
            h0, h1 = 2 * hp, 2 * hp + 1
            ld0 = load_sink_head(h0)
            ld1 = load_sink_head(h1)
            g0 = sink_stages(h0, ld0)
            g1 = sink_stages(h1, ld1)
            alive = [g0, g1]
            while alive:
                for g in list(alive):
                    try:
                        next(g)
                    except StopIteration:
                        alive.remove(g)
            slot_free_at[0] = mmB.v
            slot_free_at[1] = mmB.v

    phaseB_ev = evB.v

    dbg_spill_done = 0
    if dbg_ao is not None:
        for tt in range(NT):
            prog.dma("sync", dbg_ao[tt * 128:(tt + 1) * 128, :],
                     sbAO[:, tt * 1024:(tt + 1) * 1024], S["gst"],
                     waits=[(evB, phaseB_ev)] if tt == 0 else ())
        dbg_spill_done = S["gst"].v

    mmC, evC, scC = S["mmC"], S["evC"], S["scC"]
    xld, x1t, x1f = S["xld"], S["x1t"], S["x1f"]

    if "C" in parts:
        c_ev2 = []
        c_ev5 = []
        c_spl = []
        for tc in range(NT):
            lw = [(evB, phaseB_ev)] if tc == 0 else [(evC, c_ev2[tc - 1])]
            prog.dma("sync", sbCx[:, :], x_in[tc * 128:(tc + 1) * 128, :], xld,
                     waits=lw)
            xv = xld.v

            tw = [(evB, phaseB_ev)] if tc == 0 else [(evC, c_ev5[tc - 1])]

            def mk_c1(e, tc=tc):
                ins = None
                for c in range(8):
                    ins = e.transpose(ptr[0][:, c * 128:(c + 1) * 128],
                                      sbAO[:, tc * 1024 + c * 128:tc * 1024 + (c + 1) * 128],
                                      sbID[:, :])
                return ins
            prog.op("tensor", mk_c1, waits=tw, inc=mmC)
            c1d = mmC.v

            prog.op("vector", lambda e: e.tensor_copy(out=sbCa[:, :], in_=ptr[0][:, :1024]),
                    waits=[(mmC, c1d)], inc=evC)
            cad = evC.v

            pw = [(evC, cad), (S["gld2"], setup_g2)]
            if tc >= 1:
                pw.append((evC, c_ev2[tc - 1]))

            def mk_c2(e):
                ins = None
                for dh2 in range(2):
                    for c in range(8):
                        ins = e.matmul(pbank[dh2][:, :],
                                       sbCa[:, c * 128:(c + 1) * 128],
                                       sbWO[:, c * 1024 + dh2 * 512:c * 1024 + dh2 * 512 + 512],
                                       start=(c == 0), stop=False)
                    ins = e.matmul(pbank[dh2][:, :], sbONE[:, :],
                                   sbBO[0:1, dh2 * 512:(dh2 + 1) * 512],
                                   start=False, stop=True)
                return ins
            prog.op("tensor", mk_c2, waits=pw, inc=mmC)
            c2d = mmC.v

            def mk_c3(e):
                e.tensor_tensor(sbCz[:, 0:512], pbank[0][:, :], sbCx[:, 0:512], ALU.add)
                e.tensor_tensor(sbCz[:, 512:1024], pbank[1][:, :], sbCx[:, 512:1024], ALU.add)
                e.drain()
                return e.tensor_reduce(out=sbLN[:, 0:1], in_=sbCz[:, :], axis=AX.X,
                                       op=ALU.add, opt_input=False, opt_output=False)
            prog.op("vector", mk_c3, waits=[(mmC, c2d), (xld, xv)], inc=evC)
            c_ev2.append(evC.v)
            c3d = evC.v

            sqw = [(evC, c3d)]
            if tc >= 1:
                sqw.append((x1t, c_spl[tc - 1][0]))
            prog.op("scalar",
                    lambda e: e.activation(out=sbCsq[:, :], in_=sbCz[:, :], func=AF.Square,
                                           accum_out=sbLN[:, 1:2]),
                    waits=sqw, inc=scC)
            csq = scC.v

            def mk_c4(e, tc=tc):
                e.tensor_scalar_mul(sbLN[:, 2:3], sbLN[:, 0:1], 1.0 / 1024.0)
                e.tensor_scalar_mul(sbLN[:, 3:4], sbLN[:, 1:2], 1.0 / 1024.0)
                e.drain()
                e.scalar_tensor_tensor(out=sbLN[:, 4:5], in0=sbLN[:, 2:3],
                                       scalar=sbLN[:, 2:3], in1=sbLN[:, 3:4],
                                       op0=ALU.mult, op1=ALU.subtract)
                e.drain()
                return e.scalar_tensor_tensor(out=sbLN[:, 5:6], in0=sbLN[:, 4:5],
                                              scalar=-1.0, in1=sbEPSC[:, :],
                                              op0=ALU.mult, op1=ALU.add)
            prog.op("vector", mk_c4, waits=[(scC, csq)], inc=evC)
            c4d = evC.v

            prog.op("scalar",
                    lambda e: e.activation(out=sbLN[:, 6:7], in_=sbLN[:, 5:6],
                                           func=AF.Sqrt),
                    waits=[(evC, c4d)], inc=scC)
            cstd = scC.v

            def mk_c5(e):
                e.reciprocal(sbLN[:, 8:9], sbLN[:, 6:7])
                e.drain()
                e.tensor_scalar(out=sbCsq[:, :], in0=sbCz[:, :], scalar1=sbLN[:, 2:3],
                                scalar2=sbLN[:, 8:9], op0=ALU.subtract, op1=ALU.mult)
                return e.tensor_scalar(out=sbCx1[:, :], in0=sbCz[:, :], scalar1=sbLN[:, 2:3],
                                       scalar2=sbLN[:, 8:9], op0=ALU.subtract, op1=ALU.mult)
            prog.op("vector", mk_c5, waits=[(scC, cstd)], inc=evC)
            c5d = evC.v

            prog.dma("sync", scr_x1t[tc * 128:(tc + 1) * 128, :], sbCsq[:, :], x1t,
                     waits=[(evC, c5d)])

            def mk_c6(e):
                ins = None
                for c in range(8):
                    ins = e.transpose(ptr[0][:, c * 128:(c + 1) * 128],
                                      sbCx1[:, c * 128:(c + 1) * 128], sbID[:, :])
                return ins
            prog.op("tensor", mk_c6, waits=[(evC, c5d)], inc=mmC)
            c6d = mmC.v

            evw2 = [(mmC, c6d)]
            if tc >= 1:
                evw2.append((x1f, c_spl[tc - 1][1]))
            prog.op("vector", lambda e: e.tensor_copy(out=sbCxf[:, :], in_=ptr[0][:, :1024]),
                    waits=evw2, inc=evC)
            c_ev5.append(evC.v)
            c5e = evC.v

            for c in range(8):
                prog.dma("sync", scr_x1f[c, :, tc * 128:(tc + 1) * 128],
                         sbCxf[:, c * 128:(c + 1) * 128], x1f,
                         waits=[(evC, c5e)] if c == 0 else ())
            c_spl.append((x1t.v, x1f.v))

        phaseC_done = [(x1t, x1t.v), (x1f, x1f.v)]


    if "D" in parts:
        mmD, evD, scD = S["mmD"], S["evD"], S["scD"]
        wld = [S["wld0"], S["wld1"]]
        xq, xt2, outd_ = S["xq"], S["xt2"], S["outd"]
        x1t_total = S["x1t"].v
        wld_free_at = [0, 0]
        up_ev = []
        up_sc = []
        dn_ev = []
        prev_sig = 0
        q_x1q_free = 0
        pending_epilogue = []
        epi_sq_guard = [0]

        def load_expert_weights(e_, first):
            sl = e_ % 2
            w = []
            if first:
                w = [(S["pcs0"], precast_done[0][1]), (S["pcs1"], precast_done[1][1])]
                if dbg_spill_done:
                    w.append((S["gst"], dbg_spill_done))
            if wld_free_at[sl] > 0:
                w.append((mmD, wld_free_at[sl]))
            for c in range(8):
                prog.dma("gpsimd", sbWU[sl][:, c * HID:(c + 1) * HID],
                         scr_wub[e_, c * 128:(c + 1) * 128, :], wld[sl], waits=w)
            for ch in range(16):
                prog.dma("gpsimd", sbWD[sl][:, ch * 1024:(ch + 1) * 1024],
                         scr_wdb[e_, ch * 128:(ch + 1) * 128, :], wld[sl], waits=w)
            return wld[sl].v

        for tq in range(4):
            xw = list(phaseC_done) if tq == 0 else []
            if q_x1q_free > 0:
                xw.append((mmD, q_x1q_free))
            for c in range(8):
                prog.dma("gpsimd", sbX1Q[:, c * 1024:(c + 1) * 1024],
                         scr_x1f[c, :, tq * 1024:(tq + 1) * 1024], xq, waits=xw)
            xqv = xq.v

            def mk_g(e):
                ins_ = None
                for ti in range(8):
                    for c in range(8):
                        ins_ = e.matmul(pbank[4][:, ti * 8:(ti + 1) * 8],
                                        sbX1Q[:, c * 1024 + ti * 128:c * 1024 + (ti + 1) * 128],
                                        sbGW[:, c * 8:(c + 1) * 8],
                                        start=(c == 0), stop=(c == 7))
                return ins_
            gw_ = [(xq, xqv)]
            if prev_sig > 0:
                gw_.append((scD, prev_sig))
            prog.op("tensor", mk_g, waits=gw_, inc=mmD)
            gmm = mmD.v

            prog.op("scalar",
                    lambda e: e.activation(out=sbGS[:, :], in_=pbank[4][:, 0:64],
                                           func=AF.Sigmoid),
                    waits=[(mmD, gmm)], inc=scD)
            prev_sig = scD.v

            def mk_w2(e):
                bias_b = sbEB[:, 0:8].unsqueeze(1).broadcast_to([128, 8, 8])
                e.tensor_tensor(sbGB[:, :].rearrange("p (t e) -> p t e", t=8),
                                sbGS[:, :].rearrange("p (t e) -> p t e", t=8),
                                bias_b, ALU.add)
                e.drain()
                ins_ = None
                for ti in range(8):
                    gb = sbGB[:, ti * 8:(ti + 1) * 8]
                    e.reduce_max(out=sbTOP[:, 0:1], in_=gb, axis=AX.X,
                                 opt_input=False, opt_output=False)
                    e.drain()
                    e.tensor_scalar(out=sbW2[:, ti * 8:(ti + 1) * 8], in0=gb,
                                    scalar1=sbTOP[:, 0:1], scalar2=None, op0=ALU.is_ge)
                    e.drain()
                    e.scalar_tensor_tensor(out=sbW2[:, ti * 8:(ti + 1) * 8],
                                           in0=sbW2[:, ti * 8:(ti + 1) * 8],
                                           scalar=-1.0e30, in1=gb,
                                           op0=ALU.mult, op1=ALU.add)
                    e.drain()
                    e.reduce_max(out=sbTOP[:, 1:2], in_=sbW2[:, ti * 8:(ti + 1) * 8],
                                 axis=AX.X, opt_input=False, opt_output=False)
                    e.drain()
                    e.tensor_scalar(out=sbW2[:, ti * 8:(ti + 1) * 8], in0=gb,
                                    scalar1=sbTOP[:, 1:2], scalar2=None, op0=ALU.is_ge)
                    e.drain()
                    e.tensor_tensor(sbW2[:, ti * 8:(ti + 1) * 8],
                                    sbW2[:, ti * 8:(ti + 1) * 8],
                                    sbGS[:, ti * 8:(ti + 1) * 8], ALU.mult)
                    e.drain()
                    ins_ = e.tensor_tensor(sbW2[:, ti * 8:(ti + 1) * 8],
                                           sbW2[:, ti * 8:(ti + 1) * 8],
                                           sbW2[:, ti * 8:(ti + 1) * 8], ALU.mult)
                return ins_
            w2w_ = [(scD, prev_sig)]
            if dbg_w2 is not None and S["gst"].v > 0:
                w2w_.append((S["gst"], S["gst"].v))
            prog.op("vector", mk_w2, waits=w2w_, inc=evD)
            w2d = evD.v

            if pending_epilogue:
                epi_sq_guard[0] = pending_epilogue.pop()()

            if dbg_w2 is not None:
                for ti in range(8):
                    prog.dma("sync", dbg_w2[tq * 1024 + ti * 128:tq * 1024 + (ti + 1) * 128, :],
                             sbW2[:, ti * 8:(ti + 1) * 8], S["gst"],
                             waits=[(evD, w2d)])

            wld_vals = {}
            wld_vals[0] = load_expert_weights(0, first=(tq == 0))
            wld_vals[1] = load_expert_weights(1, first=(tq == 0))

            for e_ in range(NE):
                sl = e_ % 2
                ldv = wld_vals[e_]
                for sl2 in range(2):
                    for jc in range(16):
                        bank = pbank[jc % 2]
                        def mk_up(e, sl=sl, sl2=sl2, jc=jc, bank=bank):
                            ins_ = None
                            for dc in range(8):
                                ins_ = e.matmul(bank[:, :],
                                                sbWU[sl][:, dc * HID + jc * 128:dc * HID + jc * 128 + 128],
                                                sbX1Q[:, dc * 1024 + sl2 * 512:dc * 1024 + sl2 * 512 + 512],
                                                start=(dc == 0), stop=(dc == 7))
                            return ins_
                        uw = [(wld[sl], ldv), (xq, xqv)]
                        if len(up_sc) >= 2:
                            uw.append((scD, up_sc[-2]))
                        prog.op("tensor", mk_up, waits=uw, inc=mmD)
                        umm = mmD.v
                        rw = [(mmD, umm)]
                        if len(up_ev) >= 2:
                            rw.append((evD, up_ev[-2]))
                        prog.op("scalar",
                                lambda e, jc=jc, bank=bank:
                                    e.activation(out=sbHR[jc % 2][:, :], in_=bank[:, :],
                                                 func=AF.Relu),
                                waits=rw, inc=scD)
                        up_sc.append(scD.v)
                        sqg = [(scD, up_sc[-1])]
                        if e_ == 0 and sl2 == 0 and epi_sq_guard[0] != 0:
                            sqg.append((scD, epi_sq_guard[0][0]))
                            sqg.append((outd_, epi_sq_guard[0][1]))
                        prog.op("vector",
                                lambda e, jc=jc:
                                    e.tensor_tensor(sbHE[:, jc * 512:(jc + 1) * 512],
                                                    sbHR[jc % 2][:, :], sbHR[jc % 2][:, :],
                                                    ALU.mult),
                                waits=sqg, inc=evD)
                        up_ev.append(evD.v)
                    h_ready = evD.v
                    for cg in range(4):
                        for dh2 in range(2):
                            bank = pbank[2 + (cg * 2 + dh2) % 2]
                            def mk_dn(e, sl=sl, cg=cg, dh2=dh2, bank=bank):
                                ins_ = None
                                for jc in range(16):
                                    ins_ = e.matmul(bank[:, :],
                                                    sbHE[:, jc * 512 + cg * 128:jc * 512 + cg * 128 + 128],
                                                    sbWD[sl][:, jc * 1024 + dh2 * 512:jc * 1024 + dh2 * 512 + 512],
                                                    start=(jc == 0), stop=(jc == 15))
                                return ins_
                            dw = [(evD, h_ready), (wld[sl], ldv)]
                            if len(dn_ev) >= 2:
                                dw.append((evD, dn_ev[-2]))
                            prog.op("tensor", mk_dn, waits=dw, inc=mmD)
                            dmm = mmD.v
                            ti = sl2 * 4 + cg
                            def mk_acc(e, bank=bank, ti=ti, e_=e_, dh2=dh2):
                                dst = sbACC[:, ti * 1024 + dh2 * 512:ti * 1024 + dh2 * 512 + 512]
                                if e_ == 0:
                                    return e.tensor_scalar(out=dst, in0=bank[:, :],
                                                           scalar1=sbW2[:, ti * 8:ti * 8 + 1],
                                                           scalar2=None, op0=ALU.mult)
                                return e.scalar_tensor_tensor(out=dst, in0=bank[:, :],
                                                              scalar=sbW2[:, ti * 8 + e_:ti * 8 + e_ + 1],
                                                              in1=dst, op0=ALU.mult, op1=ALU.add)
                            prog.op("vector", mk_acc, waits=[(mmD, dmm), (evD, w2d)], inc=evD)
                            dn_ev.append(evD.v)
                if e_ + 2 < NE:
                    wld_free_at[sl] = mmD.v
                    wld_vals[e_ + 2] = load_expert_weights(e_ + 2, first=False)
                else:
                    wld_free_at[sl] = mmD.v
            q_x1q_free = mmD.v

            def plan_epilogue(tq=tq):
                slots = [sbX1T, sbO]
                last_sq = [0, 0]
                last_ap = [0, 0]
                for pg in range(4):
                    zd, sqd_, stdd, sqr2 = [0, 0], [0, 0], [0, 0], [0, 0]
                    for j in range(2):
                        ti = pg * 2 + j
                        trow = tq * 1024 + ti * 128
                        lw = [(outd_, outd_.v)]
                        if last_ap[j] > 0:
                            lw.append((evD, last_ap[j]))
                        if tq == 0 and ti == 0:
                            lw.append((S["x1t"], x1t_total))
                        prog.dma("sync", slots[j][:, :], scr_x1t[trow:trow + 128, :],
                                 xt2, waits=lw)
                    xtv = xt2.v
                    for j in range(2):
                        ti = pg * 2 + j
                        prog.op("vector",
                                lambda e, ti=ti, j=j:
                                    e.tensor_tensor(slots[j][:, :],
                                                    sbACC[:, ti * 1024:(ti + 1) * 1024],
                                                    slots[j][:, :], ALU.add),
                                waits=[(xt2, xtv)] + ([(evD, last_ap[j])] if last_ap[j] else []),
                                inc=evD)
                        zd[j] = evD.v
                    sum_v = [0, 0]
                    for j in range(2):
                        prog.op("vector",
                                lambda e, j=j:
                                    e.tensor_reduce(out=sbLN[:, j * 8:j * 8 + 1],
                                                    in_=slots[j][:, :], axis=AX.X,
                                                    op=ALU.add, opt_input=False,
                                                    opt_output=False),
                                waits=[(evD, zd[j])], inc=evD)
                        sum_v[j] = evD.v
                        sqw = [(evD, zd[j])]
                        sqw.append((outd_, outd_.v))
                        prog.op("scalar",
                                lambda e, j=j:
                                    e.activation(out=sbEPQ[j][:, :], in_=slots[j][:, :],
                                                 func=AF.Square,
                                                 accum_out=sbLN[:, j * 8 + 1:j * 8 + 2]),
                                waits=sqw, inc=scD)
                        sqd_[j] = scD.v
                        last_sq[j] = scD.v
                    for j in range(2):
                        def mk_st(e, j=j):
                            e.tensor_scalar_mul(sbLN[:, j * 8 + 2:j * 8 + 3],
                                                sbLN[:, j * 8:j * 8 + 1], 1.0 / 1024.0)
                            e.tensor_scalar_mul(sbLN[:, j * 8 + 3:j * 8 + 4],
                                                sbLN[:, j * 8 + 1:j * 8 + 2], 1.0 / 1024.0)
                            e.drain()
                            e.scalar_tensor_tensor(out=sbLN[:, j * 8 + 4:j * 8 + 5],
                                                   in0=sbLN[:, j * 8 + 2:j * 8 + 3],
                                                   scalar=sbLN[:, j * 8 + 2:j * 8 + 3],
                                                   in1=sbLN[:, j * 8 + 3:j * 8 + 4],
                                                   op0=ALU.mult, op1=ALU.subtract)
                            e.drain()
                            return e.scalar_tensor_tensor(out=sbLN[:, j * 8 + 5:j * 8 + 6],
                                                          in0=sbLN[:, j * 8 + 4:j * 8 + 5],
                                                          scalar=-1.0, in1=sbEPSC[:, :],
                                                          op0=ALU.mult, op1=ALU.add)
                        prog.op("vector", mk_st, waits=[(scD, sqd_[j]), (evD, sum_v[j])],
                                inc=evD)
                        stdd[j] = evD.v
                    for j in range(2):
                        prog.op("scalar",
                                lambda e, j=j:
                                    e.activation(out=sbLN[:, j * 8 + 6:j * 8 + 7],
                                                 in_=sbLN[:, j * 8 + 5:j * 8 + 6],
                                                 func=AF.Sqrt),
                                waits=[(evD, stdd[j])], inc=scD)
                        sqr2[j] = scD.v
                    for j in range(2):
                        ti = pg * 2 + j
                        trow = tq * 1024 + ti * 128
                        def mk_ap(e, j=j):
                            e.reciprocal(sbLN[:, j * 8 + 7:j * 8 + 8],
                                         sbLN[:, j * 8 + 6:j * 8 + 7])
                            e.drain()
                            return e.tensor_scalar(out=sbEPQ[j][:, :], in0=slots[j][:, :],
                                                   scalar1=sbLN[:, j * 8 + 2:j * 8 + 3],
                                                   scalar2=sbLN[:, j * 8 + 7:j * 8 + 8],
                                                   op0=ALU.subtract, op1=ALU.mult)
                        prog.op("vector", mk_ap, waits=[(scD, sqr2[j])], inc=evD)
                        apd = evD.v
                        last_ap[j] = apd
                        prog.dma("sync", out[trow:trow + 128, :], sbEPQ[j][:, :], outd_,
                                 waits=[(evD, apd)])
                return (max(last_sq), outd_.v)
            pending_epilogue.append(plan_epilogue)

        if pending_epilogue:
            pending_epilogue.pop()()

    outd = S["outd"]
    if "D" not in parts:
        src = scr_x1t if "C" in parts else x_in
        for tt in range(NT):
            lw = [(outd, outd.v)]
            if tt == 0:
                lw.extend(phaseA_done)
                if "C" in parts:
                    lw.extend(phaseC_done)
            prog.dma("sync", sbO[:, :], src[tt * 128:(tt + 1) * 128, :], S["sld"], waits=lw)
            prog.dma("sync", out[tt * 128:(tt + 1) * 128, :], sbO[:, :], outd,
                     waits=[(S["sld"], S["sld"].v)])
    prog.wait("sync", outd, outd.v)

    with nc.Block() as block:
        for eng in ENGINES:
            ops = prog.ops[eng]
            if not ops:
                continue
            def body(e, ops=ops):
                for fn in ops:
                    fn(e)
            getattr(block, eng)(body)
    estack.close()
    return nc


_NC_CACHE = {}


def _get_nc(debug=()):
    key = tuple(sorted(debug))
    if key not in _NC_CACHE:
        _NC_CACHE[key] = build_nc(debug=key)
    return _NC_CACHE[key]


def _shard_inputs(inputs):
    x = np.ascontiguousarray(inputs["L0_attn_x"])
    wq = np.ascontiguousarray(inputs["L0_attn_w_q"])
    wkv = np.ascontiguousarray(inputs["L0_attn_w_kv"])
    wo = np.ascontiguousarray(inputs["L0_attn_w_out"])
    bo = np.ascontiguousarray(inputs["L0_attn_b_out"])
    sortw = np.ascontiguousarray(inputs["L0_attn_sort_w"]).reshape(SHH, 2 * DHH, NB)
    gw = np.ascontiguousarray(inputs["L0_ffn_gate_w"])
    wu = np.ascontiguousarray(inputs["L0_ffn_w_up"])
    wd = np.ascontiguousarray(inputs["L0_ffn_w_down"])
    eb = np.ascontiguousarray(inputs["L0_ffn_expert_bias"])

    gwT = np.ascontiguousarray(gw.T)
    wuT = np.ascontiguousarray(wu.transpose(0, 2, 1))
    wdT = np.ascontiguousarray(wd.transpose(0, 2, 1))
    ebr = np.ascontiguousarray(np.broadcast_to(eb[None, :], (128, NE)))

    in_maps = []
    for c in range(8):
        xc = np.ascontiguousarray(x[c])
        in_maps.append({
            "xT": np.ascontiguousarray(xc.T),
            "x_in": xc,
            "wq": wq, "wkv": wkv, "wo": wo, "bo": bo,
            "sortw": sortw, "gwT": gwT, "ebias": ebr,
            "wuT": wuT, "wdT": wdT,
        })
    return in_maps


def kernel(**inputs):
    nc = _get_nc()
    in_maps = _shard_inputs(inputs)
    res = run_bass_kernel_spmd(nc, in_maps, core_ids=list(range(8)))
    return np.stack([r["out"] for r in res.results], axis=0)


def run_debug(inputs, debug, parts=("A", "Bl", "Bs", "C", "D"), cores=8, sink_stop=99):
    nc = build_nc(debug=tuple(debug), parts=parts, sink_stop=sink_stop)
    in_maps = _shard_inputs(inputs)[:cores]
    res = run_bass_kernel_spmd(nc, in_maps, core_ids=list(range(cores)))
    return res


# version 09
# speedup vs baseline: 1.0406x; 1.0057x over previous
import numpy as np
import ml_dtypes
import concourse.bass as bass
import concourse.mybir as mybir
from concourse.bass_utils import run_bass_kernel_spmd

F32 = mybir.dt.float32
BF16 = mybir.dt.bfloat16
AF = mybir.ActivationFunctionType
ALU = mybir.AluOpType
AX = mybir.AxisListType

T = 4096
D = 1024
H, LHH, SHH, DHH = 16, 4, 12, 64
NB, BW = 64, 64
NT = T // 128
NE, HID = 8, 2048
EPS = 1e-5

ENGINES = ("sync", "tensor", "vector", "scalar", "gpsimd")


class Sem:
    def __init__(self, handle):
        self.h = handle
        self.v = 0

    def bump(self, k):
        self.v += k
        return self.v


class Prog:
    def __init__(self):
        self.ops = {e: [] for e in ENGINES}

    def emit(self, eng, fn):
        self.ops[eng].append(fn)

    def op(self, eng, build, waits=(), inc=None, inc_n=1):
        waits = tuple((s, v) for (s, v) in waits if v > 0)
        if inc is not None:
            inc.bump(inc_n)

        def f(e, build=build, waits=waits, inc=inc, inc_n=inc_n):
            for s, v in waits:
                e.wait_ge(s.h, v)
            ins = build(e)
            if inc is not None:
                ins.then_inc(inc.h, inc_n)

        self.emit(eng, f)

    def dma(self, qeng, out_ap, in_ap, sem, waits=()):
        waits = tuple((s, v) for (s, v) in waits if v > 0)
        sem.bump(16)

        def f(e, out_ap=out_ap, in_ap=in_ap, waits=waits, sem=sem):
            for s, w in waits:
                e.wait_ge(s.h, w)
            e.dma_start(out=out_ap, in_=in_ap).then_inc(sem.h, 16)

        self.emit(qeng, f)
        return sem.v

    def wait(self, eng, sem, v):
        if v > 0:
            self.emit(eng, lambda e, sem=sem, v=v: e.wait_ge(sem.h, v))


def build_nc(debug=(), parts=("A", "Bl", "Bs", "C", "D"), sink_stop=99):
    nc = bass.Bass("TRN2", target_bir_lowering=False)

    xT = nc.dram_tensor("xT", [D, T], F32, kind="ExternalInput").ap()
    x_in = nc.dram_tensor("x_in", [T, D], F32, kind="ExternalInput").ap()
    wq = nc.dram_tensor("wq", [D, D], F32, kind="ExternalInput").ap()
    wkv = nc.dram_tensor("wkv", [D, 2 * D], F32, kind="ExternalInput").ap()
    wo = nc.dram_tensor("wo", [D, D], F32, kind="ExternalInput").ap()
    bo = nc.dram_tensor("bo", [D], F32, kind="ExternalInput").ap()
    sortw = nc.dram_tensor("sortw", [SHH, 2 * DHH, NB], F32, kind="ExternalInput").ap()
    gwT = nc.dram_tensor("gwT", [D, NE], F32, kind="ExternalInput").ap()
    ebias = nc.dram_tensor("ebias", [128, NE], F32, kind="ExternalInput").ap()
    wuT = nc.dram_tensor("wuT", [NE, D, HID], F32, kind="ExternalInput").ap()
    wdT = nc.dram_tensor("wdT", [NE, HID, D], F32, kind="ExternalInput").ap()
    out = nc.dram_tensor("out", [T, D], F32, kind="ExternalOutput").ap()

    def scratch(name, shape, dtype):
        kind = "ExternalOutput" if name in debug else "Internal"
        return nc.dram_tensor(name, shape, dtype, kind=kind).ap()

    scr_q = scratch("scr_q", [8, 128, T], BF16)
    scr_kl = scratch("scr_kl", [2, 128, T], BF16)
    scr_kv = scratch("scr_kv", [SHH, 128, T], BF16)
    scr_vt = scratch("scr_vt", [8, NT, 128, 128], BF16)
    scr_x1f = scratch("scr_x1f", [8, 128, T], BF16)
    scr_x1t = scratch("scr_x1t", [T, D], F32)
    scr_wub = scratch("scr_wub", [NE, D, HID], BF16)
    scr_wdb = scratch("scr_wdb", [NE, HID, D], BF16)
    dbg_ao = scratch("dbg_ao", [T, D], BF16) if "dbg_ao" in debug else None
    dbg_w2 = scratch("dbg_w2", [T, NE], F32) if "dbg_w2" in debug else None

    ident_h = nc.inline_tensor(np.eye(128, dtype=ml_dtypes.bfloat16), "ident128").ap()
    ones_h = nc.inline_tensor(np.ones((1, 128), dtype=ml_dtypes.bfloat16), "onesrow").ap()
    NEG = -1.5e38
    _ml = np.zeros((128, 256), dtype=np.float32)
    _ml[0:64, 192:256] = NEG
    _ml[64:128, 0:64] = NEG
    maskl_h = nc.inline_tensor(_ml.astype(ml_dtypes.bfloat16), "maskl").ap()
    _ms = np.zeros((128, 512), dtype=np.float32)
    for _half in range(2):
        _c0 = _half * 256
        _ms[0:64, _c0 + 64:_c0 + 128] = NEG
        _ms[64:128, _c0:_c0 + 64] = NEG
        _ms[0:64, _c0 + 192:_c0 + 256] = NEG
        _ms[64:128, _c0 + 128:_c0 + 192] = NEG
    masks_h = nc.inline_tensor(_ms.astype(ml_dtypes.bfloat16), "masks").ap()
    negrow_h = nc.inline_tensor(np.full((1, 64), NEG, dtype=ml_dtypes.bfloat16), "negrow").ap()
    epscol_h = nc.inline_tensor(np.full((128, 1), EPS, dtype=np.float32), "epscol").ap()

    ar1 = nc.alloc_sbuf_tensor("arena1", [128, 65536], mybir.dt.uint8).ap()
    ar2 = nc.alloc_sbuf_tensor("arena2", [128, 65536], mybir.dt.uint8).ap()
    ar3 = nc.alloc_sbuf_tensor("arena3", [128, 32768], mybir.dt.uint8).ap()
    ar4 = nc.alloc_sbuf_tensor("arena4", [128, 32768], mybir.dt.uint8).ap()
    arS = nc.alloc_sbuf_tensor("arenaS", [128, 15680], mybir.dt.uint8).ap()

    def sb(arena, off, dtype, cols, rows=128):
        es = mybir.dt.size(dtype)
        assert off % 32 == 0 and off + cols * es <= arena.shape[1] * 1
        return arena.bitcast(dtype)[0:rows, off // es: off // es + cols]

    sbXF = sb(ar1, 0, BF16, 8 * T)
    sbAO = sb(ar1, 0, BF16, NT * 1024)
    sbWU = [sb(ar1, i * 32768, BF16, 8 * HID) for i in range(2)]

    sbWQ = sb(ar2, 0, BF16, 8 * 1024)
    sbWKV = sb(ar2, 16384, BF16, 8 * 2048)
    sbQH = [sb(ar2, i * 8192, BF16, T) for i in range(2)]
    sbKH = [sb(ar2, 16384 + i * 8192, BF16, T) for i in range(2)]
    sbVT = [sb(ar2, 32768 + i * 8192, BF16, NT * 128) for i in range(2)]
    sbWO = sb(ar2, 49152, BF16, 8 * 1024)
    sbWD = [sb(ar2, i * 32768, BF16, 16 * 1024) for i in range(2)]
    sbHR = [sb(arS, 13632 + i * 1024, BF16, 512) for i in range(2)]

    sbSTG = sb(ar3, 0, BF16, 4 * 1024)
    sbKVB = sb(ar3, 0, BF16, BW * 128)
    sbRKV = sb(ar3, 16384, BF16, T)
    sbACC = sb(ar3, 0, F32, 8 * 1024)
    sbCz = sb(ar3, 0, F32, 1024)
    sbCsq = sb(ar3, 4096, F32, 1024)
    sbCx = sb(ar3, 8192, F32, 1024)
    sbCa = sb(ar3, 12288, BF16, 1024)
    sbCx1 = sb(ar3, 14336, BF16, 1024)
    sbCxf = sb(ar3, 16384, BF16, 1024)

    sbX1Q = sb(ar4, 0, BF16, 8 * 1024)
    sbHE = sb(ar4, 16384, BF16, 16 * 512)
    sbPP = sb(ar4, 0, BF16, 512)
    sbP2 = sb(ar4, 1024, BF16, 8 * 128)
    sbP2T = sb(ar4, 3072, BF16, 8 * 64)
    sbBVR = sb(ar4, 4096, BF16, 8 * 64)
    sbFE = sb(ar4, 5120, BF16, 64)
    sbRT = sb(ar4, 5248, BF16, 64)
    sbRS = sb(ar4, 5376, F32, 64, rows=64)
    sbRE = sb(ar4, 5632, F32, 64, rows=64)
    sbST1 = sb(ar4, 5888, F32, 16)
    sbST2 = sb(ar4, 5952, F32, 16)
    sbST3 = sb(ar4, 6016, F32, 16)
    sbFE32 = sb(ar4, 6144, F32, 64)
    sbWX = sb(ar4, 8192, BF16, 8 * 512)

    sbID = sb(arS, 0, BF16, 128)
    sbSORT = sb(arS, 256, BF16, SHH * NB)
    sbGW = sb(arS, 1792, BF16, 64)
    sbBO = sb(arS, 1920, BF16, 1024, rows=1)
    sbONE = sb(arS, 3968, BF16, 128, rows=1)
    sbEB = sb(arS, 4224, F32, NE)
    sbW2 = sb(arS, 4480, F32, 8 * NE)
    sbGS = sb(arS, 4736, F32, 8 * NE)
    sbGB = sb(arS, 4992, F32, 8 * NE)
    sbTOP = sb(arS, 5248, F32, 8)
    sbLN = sb(arS, 5312, F32, 16)
    sbEPSC = sb(arS, 5376, F32, 1)
    sbMKL = sb(ar4, 3072, BF16, 256)
    sbMKS = sb(ar4, 3584, BF16, 512)
    sbNEGR = sb(ar4, 4608, BF16, 64, rows=1)
    sbO = sb(arS, 5440, F32, 1024)
    sbX1T = sb(arS, 9536, F32, 1024)
    sbEPQ = [sb(ar4, 16384 + i * 4096, F32, 1024) for i in range(2)]

    pbank = [nc.alloc_psum_tensor(f"pb{i}", [128, 512], F32).ap() for i in range(6)]
    ptr = [nc.alloc_psum_tensor(f"pt{i}", [128, 1024], BF16).ap() for i in range(2)]

    import contextlib
    estack = contextlib.ExitStack()
    semnames = [
        "gld", "sld", "sst", "gst",
        "stg0", "stg1", "stg2", "stg3",
        "mmA", "evA",
        "mmB", "evB", "scB", "hld0", "hld1",
        "mmC", "evC", "scC", "xld", "x1t", "x1f", "gld2",
        "mmD", "evD", "scD", "wld0", "wld1", "pcl0", "pcs0", "pcl1", "pcs1", "xq", "xt2",
        "outd",
    ]
    S = {nm: Sem(estack.enter_context(nc.semaphore(nm))) for nm in semnames}

    prog = Prog()

    for c in range(8):
        prog.dma("gpsimd", sbXF[:, c * T:(c + 1) * T], xT[c * 128:(c + 1) * 128, :], S["gld"])
    for c in range(8):
        prog.dma("gpsimd", sbWQ[:, c * 1024:(c + 1) * 1024], wq[c * 128:(c + 1) * 128, :], S["gld"])
    for c in range(8):
        prog.dma("gpsimd", sbWKV[:, c * 2048:(c + 1) * 2048], wkv[c * 128:(c + 1) * 128, :], S["gld"])
    for hh in range(SHH):
        prog.dma("gpsimd", sbSORT[:, hh * NB:(hh + 1) * NB], sortw[hh], S["gld"])
    for c in range(8):
        prog.dma("gpsimd", sbGW[:, c * 8:(c + 1) * 8], gwT[c * 128:(c + 1) * 128, :], S["gld"])
    prog.dma("gpsimd", sbBO[:, :], bo.rearrange("(a b) -> a b", a=1), S["gld"])
    setup_g = S["gld"].v
    prog.dma("sync", sbID[:, :], ident_h, S["sld"])
    prog.dma("sync", sbONE[:, :], ones_h, S["sld"])
    prog.dma("sync", sbEB[:, :], ebias, S["sld"])
    prog.dma("sync", sbMKL[:, :], maskl_h, S["sld"])
    prog.dma("sync", sbMKS[:, :], masks_h, S["sld"])
    prog.dma("sync", sbNEGR[:, :], negrow_h, S["sld"])
    prog.dma("sync", sbEPSC[:, :], epscol_h, S["sld"])
    setup_s = S["sld"].v
    prog.wait("tensor", S["sld"], setup_s)
    prog.wait("vector", S["sld"], setup_s)
    prog.wait("scalar", S["sld"], setup_s)
    for c in range(8):
        prog.dma("gpsimd", sbWO[:, c * 1024:(c + 1) * 1024], wo[c * 128:(c + 1) * 128, :], S["gld2"])
    setup_g2 = S["gld2"].v

    sbPC = [sb(ar4, 16384 + i * 4096, BF16, 2048) for i in range(2)]
    pcl = [S["pcl0"], S["pcl1"]]
    pcs = [S["pcs0"], S["pcs1"]]
    pc_jobs = []
    for e_ in range(NE):
        for ch in range(8):
            pc_jobs.append((wuT[e_, ch * 128:(ch + 1) * 128, :],
                            scr_wub[e_, ch * 128:(ch + 1) * 128, :]))
    for e_ in range(NE):
        for ch in range(16):
            pc_jobs.append((wdT[e_, ch * 128:(ch + 1) * 128, :],
                            scr_wdb[e_, ch * 128:(ch + 1) * 128, :]))
    for j, (src, dst) in enumerate(pc_jobs):
        slot = j % 2
        ncols = src.shape[1]
        w = [(pcs[slot], pcs[slot].v)] if j >= 2 else []
        prog.dma("gpsimd", sbPC[slot][:, 0:ncols], src, pcl[slot], waits=w)
        prog.dma("gpsimd", dst, sbPC[slot][:, 0:ncols], pcs[slot],
                 waits=[(pcl[slot], pcl[slot].v)])
    precast_done = [(pcs[0], pcs[0].v), (pcs[1], pcs[1].v)]

    mmA, evA = S["mmA"], S["evA"]
    stgsem = [S["stg0"], S["stg1"], S["stg2"], S["stg3"]]
    unit = [0]

    def proj_unit(mk_mms, evacs, stores):
        k = unit[0]
        bank = pbank[k % 2]
        slot = stgsem[k % 4]
        prog.op("tensor", (lambda e, bank=bank: mk_mms(e, bank)),
                waits=[(S["gld"], setup_g), (evA, k - 1)], inc=mmA)
        mw = mmA.v

        evw = [(mmA, mw), (slot, slot.v)]

        def ev(e, bank=bank, evacs=evacs):
            ins = None
            for fn in evacs:
                ins = fn(e, bank)
            return ins
        prog.op("vector", ev, waits=evw, inc=evA)
        ew = evA.v
        for dst, src in stores:
            prog.dma("sync", dst, src, slot, waits=[(evA, ew)])
        unit[0] += 1

    def stg(k, r0, nr, cols=512):
        s0 = (k % 4) * 1024
        return sbSTG[r0:r0 + nr, s0:s0 + cols]

    for pc in range(8):
        for sl in range(8):
            k = unit[0]
            def mk(e, bank, pc=pc, sl=sl):
                ins = None
                for dc in range(8):
                    ins = e.matmul(bank[:, :],
                                   sbWQ[:, dc * 1024 + pc * 128: dc * 1024 + pc * 128 + 128],
                                   sbXF[:, dc * T + sl * 512: dc * T + sl * 512 + 512],
                                   start=(dc == 0), stop=(dc == 7))
                return ins
            def ev0(e, bank, k=k):
                return e.tensor_copy(out=stg(k, 0, 128), in_=bank[:, :])
            proj_unit(mk, [ev0],
                      [(scr_q[pc, :, sl * 512:(sl + 1) * 512], stg(k, 0, 128))])

    for pc in range(8):
        for sl in range(8):
            k = unit[0]
            def mk(e, bank, pc=pc, sl=sl):
                ins = None
                for dc in range(8):
                    ins = e.matmul(bank[:, :],
                                   sbWKV[:, dc * 2048 + pc * 128: dc * 2048 + pc * 128 + 128],
                                   sbXF[:, dc * T + sl * 512: dc * T + sl * 512 + 512],
                                   start=(dc == 0), stop=(dc == 7))
                return ins
            def ev0(e, bank, k=k):
                return e.tensor_copy(out=stg(k, 0, 128), in_=bank[:, :])
            if pc < 2:
                proj_unit(mk, [ev0],
                          [(scr_kl[pc, :, sl * 512:(sl + 1) * 512], stg(k, 0, 128))])
            else:
                hs0, hs1 = 2 * pc - 4, 2 * pc - 3
                full = stg(k, 0, 128)
                proj_unit(mk, [ev0],
                          [(scr_kv[hs0, 0:64, sl * 512:(sl + 1) * 512], full[0:64, :]),
                           (scr_kv[hs1, 0:64, sl * 512:(sl + 1) * 512], full[64:128, :])])

    for vc in range(2, 8):
        for sl in range(8):
            k = unit[0]
            def mk(e, bank, vc=vc, sl=sl):
                ins = None
                for dc in range(8):
                    ins = e.matmul(bank[:, :],
                                   sbWKV[:, dc * 2048 + 1024 + vc * 128: dc * 2048 + 1024 + vc * 128 + 128],
                                   sbXF[:, dc * T + sl * 512: dc * T + sl * 512 + 512],
                                   start=(dc == 0), stop=(dc == 7))
                return ins
            def ev0(e, bank, k=k):
                return e.tensor_copy(out=stg(k, 0, 128), in_=bank[:, :])
            hs0, hs1 = 2 * vc - 4, 2 * vc - 3
            full = stg(k, 0, 128)
            proj_unit(mk, [ev0],
                      [(scr_kv[hs0, 64:128, sl * 512:(sl + 1) * 512], full[0:64, :]),
                       (scr_kv[hs1, 64:128, sl * 512:(sl + 1) * 512], full[64:128, :])])

    for tt in range(NT):
        for vh in range(2):
            k = unit[0]
            def mk(e, bank, tt=tt, vh=vh):
                ins = None
                for dc in range(8):
                    ins = e.matmul(bank[:, :],
                                   sbXF[:, dc * T + tt * 128: dc * T + tt * 128 + 128],
                                   sbWKV[:, dc * 2048 + 1024 + vh * 512: dc * 2048 + 1024 + vh * 512 + 512],
                                   start=(dc == 0), stop=(dc == 7))
                return ins
            def ev0(e, bank, k=k):
                return e.tensor_copy(out=stg(k, 0, 128), in_=bank[:, :])
            stores = []
            for pp in range(4):
                p = 4 * vh + pp
                stores.append((scr_vt[p, tt], stg(k, 0, 128)[:, pp * 128:(pp + 1) * 128]))
            proj_unit(mk, [ev0], stores)

    phaseA_done = [(s, s.v) for s in stgsem]

    mmB, scB, evB = S["mmB"], S["scB"], S["evB"]
    hld = [S["hld0"], S["hld1"]]
    slot_free_at = [0, 0]

    def load_local_pair(p):
        s = p % 2
        w = list(phaseA_done) if p == 0 else []
        if slot_free_at[s] > 0:
            w.append((mmB, slot_free_at[s]))
        prog.dma("sync", sbQH[s][:, :], scr_q[p], hld[s], waits=w)
        prog.dma("sync", sbKH[s][:, :], scr_kl[p], hld[s], waits=w)
        for tc in range(4):
            prog.dma("sync",
                     sbVT[s].rearrange("p (t c) -> p t c", c=128)[:, tc * 8:(tc + 1) * 8, :],
                     scr_vt[p, tc * 8:(tc + 1) * 8].transpose([1, 0, 2]),
                     hld[s], waits=w)
        return hld[s].v

    def load_sink_head(hs):
        s = hs % 2
        w = []
        if slot_free_at[s] > 0:
            w.append((mmB, slot_free_at[s]))
        h = hs + 4
        pair = h // 2
        rb = (h % 2) * 64
        prog.dma("sync", sbQH[s][0:64, :], scr_q[pair][rb:rb + 64, :], hld[s], waits=w)
        prog.dma("sync", sbKH[s][:, :], scr_kv[hs], hld[s], waits=w)
        for tc in range(4):
            prog.dma("sync",
                     sbVT[s].rearrange("p (t c) -> p t c", c=128)[:, tc * 8:(tc + 1) * 8, :],
                     scr_vt[pair, tc * 8:(tc + 1) * 8].transpose([1, 0, 2]),
                     hld[s], waits=w)
        return hld[s].v

    lu = [0]
    l_tr_mm = []
    l_av_mm = []
    l_pt_ev = []
    l_av_ev = []
    l_exp = []

    def local_head_tile(s, h, tt, ldv):
        u = lu[0]
        rb = (h % 2) * 64
        hv = (h % 2) * 64
        sbank = pbank[u % 2]
        obank = pbank[2 + u % 2]
        pslot = (u % 2) * 256
        wslot = (u % 2) * 1024

        if tt == 0:
            c0, c1, p0 = 0, 192, 64
        elif tt == NT - 1:
            c0, c1, p0 = tt * 128 - 64, tt * 128 + 128, 0
        else:
            c0, c1, p0 = tt * 128 - 64, tt * 128 + 192, 0

        def mk_s(e, s=s, rb=rb, tt=tt, c0=c0, c1=c1, p0=p0, sbank=sbank):
            ins = e.matmul(sbank[:, p0:p0 + (c1 - c0)],
                           sbQH[s][rb:rb + 64, tt * 128:tt * 128 + 128],
                           sbKH[s][rb:rb + 64, c0:c1],
                           start=True, stop=True)
            if tt == 0:
                ins = e.matmul(sbank[:, 0:64], sbONE[:, :], sbNEGR[:, :],
                               start=True, stop=True)
            if tt == NT - 1:
                ins = e.matmul(sbank[:, 192:256], sbONE[:, :], sbNEGR[:, :],
                               start=True, stop=True)
            return ins
        sw = [(hld[s], ldv)]
        if u >= 2:
            sw.append((scB, l_exp[u - 2]))
        prog.op("tensor", mk_s, waits=sw, inc=mmB)
        smm = mmB.v

        prog.op("vector",
                lambda e, sbank=sbank:
                    e.tensor_tensor(sbank[:, 0:256], sbank[:, 0:256], sbMKL[:, :],
                                    ALU.add),
                waits=[(mmB, smm)], inc=evB)
        mzd = evB.v

        expw = [(evB, mzd)]
        if u >= 2:
            expw.append((mmB, l_tr_mm[u - 2]))

        prog.op("scalar",
                lambda e, sbank=sbank, pslot=pslot:
                    e.activation(out=sbPP[:, pslot:pslot + 256], in_=sbank[:, 0:256],
                                 func=AF.Exp, accum_out=sbST1[:, 0:1]),
                waits=expw, inc=scB)
        l_exp.append(scB.v)
        sexp = scB.v

        prog.op("vector",
                lambda e: e.reciprocal(sbST2[:, 0:1], sbST1[:, 0:1]),
                waits=[(scB, sexp)], inc=evB)
        maskd = evB.v

        pb0 = (u % 2) * 512
        trw = [(evB, maskd)]
        if u >= 2:
            trw.append((evB, l_pt_ev[u - 2]))

        def mk_tr(e, pslot=pslot, tt=tt, pb0=pb0):
            ins = None
            if tt > 0:
                ins = e.transpose(ptr[0][64:128, pb0:pb0 + 128], sbPP[:, pslot:pslot + 64],
                                  sbID[:, :])
            ins = e.transpose(ptr[0][:, pb0 + 128:pb0 + 256], sbPP[:, pslot + 64:pslot + 192],
                              sbID[:, :])
            if tt < NT - 1:
                ins = e.transpose(ptr[0][0:64, pb0 + 256:pb0 + 384], sbPP[:, pslot + 192:pslot + 256],
                                  sbID[:, :])
            return ins
        prog.op("tensor", mk_tr, waits=trw, inc=mmB)
        l_tr_mm.append(mmB.v)
        trd = mmB.v

        pw = [(mmB, trd)]
        if u >= 2:
            pw.append((mmB, l_av_mm[u - 2]))
        def ev_pt(e, wslot=wslot, pb0=pb0, tt=tt):
            if tt > 0:
                e.tensor_copy(out=sbWX[64:128, wslot:wslot + 128],
                              in_=ptr[0][64:128, pb0:pb0 + 128])
            ins = e.tensor_copy(out=sbWX[:, wslot + 128:wslot + 256],
                                in_=ptr[0][:, pb0 + 128:pb0 + 256])
            if tt < NT - 1:
                ins = e.tensor_copy(out=sbWX[0:64, wslot + 256:wslot + 384],
                                    in_=ptr[0][0:64, pb0 + 256:pb0 + 384])
            return ins
        prog.op("vector", ev_pt, waits=pw, inc=evB)
        l_pt_ev.append(evB.v)
        ptd = evB.v

        avw = [(evB, ptd)]
        if u >= 2:
            avw.append((evB, l_av_ev[u - 2]))

        def mk_av(e, s=s, hv=hv, tt=tt, wslot=wslot, obank=obank):
            first = True
            ins = None
            if tt > 0:
                ins = e.matmul(obank[:, 0:64], sbWX[64:128, wslot:wslot + 128],
                               sbVT[s][64:128, (tt - 1) * 128 + hv:(tt - 1) * 128 + hv + 64],
                               start=first, stop=False)
                first = False
            ins = e.matmul(obank[:, 0:64], sbWX[:, wslot + 128:wslot + 256],
                           sbVT[s][:, tt * 128 + hv:tt * 128 + hv + 64],
                           start=first, stop=(tt == NT - 1))
            if tt < NT - 1:
                ins = e.matmul(obank[:, 0:64], sbWX[0:64, wslot + 256:wslot + 384],
                               sbVT[s][0:64, (tt + 1) * 128 + hv:(tt + 1) * 128 + hv + 64],
                               start=False, stop=True)
            return ins
        prog.op("tensor", mk_av, waits=avw, inc=mmB)
        l_av_mm.append(mmB.v)
        avd = mmB.v

        prog.op("vector",
                lambda e, obank=obank, tt=tt, h=h:
                    e.tensor_scalar(out=sbAO[:, tt * 1024 + h * 64:tt * 1024 + h * 64 + 64],
                                    in0=obank[:, 0:64], scalar1=sbST2[:, 0:1],
                                    scalar2=None, op0=ALU.mult),
                waits=[(mmB, avd)], inc=evB)
        l_av_ev.append(evB.v)
        lu[0] += 1


    if "Bl" in parts:
        for p in range(2):
            ldv = load_local_pair(p)
            for hh in range(2):
                h = 2 * p + hh
                for tt in range(NT):
                    local_head_tile(p % 2, h, tt, ldv)
            slot_free_at[p % 2] = mmB.v

    sbFEb = sb(ar4, 6400, BF16, 64)
    sbRTb = sb(ar4, 6528, BF16, 64)
    sbRSb = sb(ar4, 6656, F32, 64, rows=64)
    sbREb = sb(ar4, 6912, F32, 64, rows=64)
    sbFE32b = sb(ar4, 7168, F32, 64)
    sbST1b = sb(ar4, 7424, F32, 16)
    sbST2b = sb(ar4, 7488, F32, 16)
    sbST3b = sb(ar4, 7552, F32, 16)
    sbRKV2 = sb(ar3, 24576, BF16, T // 2)
    sbRKV2 = sb(ar3, 24576, BF16, T)

    CTXS = [
        dict(FE=sbFE, FE32=sbFE32, RT=sbRT, RS=sbRS, RE=sbRE,
             ST1=sbST1, ST2=sbST2, ST3=sbST3, RKV=sbRKV, rr=0,
             sbank=pbank[0], obank=pbank[2], rbank=pbank[4], ptrX=ptr[0],
             po=0, vo=0),
        dict(FE=sbFEb, FE32=sbFE32b, RT=sbRTb, RS=sbRSb, RE=sbREb,
             ST1=sbST1b, ST2=sbST2b, ST3=sbST3b, RKV=sbRKV2, rr=64,
             sbank=pbank[1], obank=pbank[3], rbank=pbank[5], ptrX=ptr[1],
             po=512, vo=256),
    ]

    def sink_stages(hs, ldv):
        pi = hs % 2
        C = CTXS[pi]
        s = pi
        h = hs + 4
        rr = C["rr"]
        FE, FE32, RT, RS, RE = C["FE"], C["FE32"], C["RT"], C["RS"], C["RE"]
        ST1, ST2, ST3, RKV = C["ST1"], C["ST2"], C["ST3"], C["RKV"]
        sbank, obank, rbank, ptrX = C["sbank"], C["obank"], C["rbank"], C["ptrX"]
        po, vo = C["po"], C["vo"]

        def mk_f(e, s=s, FE32=FE32, FE=FE):
            e.tensor_reduce(out=FE32[0:64, 0:64],
                            in_=sbQH[s][0:64, :].rearrange("p (v t) -> p v t", v=NB),
                            axis=AX.X, op=ALU.add, opt_input=False, opt_output=False)
            e.tensor_reduce(out=FE32[64:128, 0:64],
                            in_=sbKH[s][0:64, :].rearrange("p (v t) -> p v t", v=NB),
                            axis=AX.X, op=ALU.add, opt_input=False, opt_output=False)
            e.drain()
            return e.tensor_copy(out=FE[:, 0:64], in_=FE32[:, 0:64])
        prog.op("vector", mk_f, waits=[(hld[s], ldv)], inc=evB)
        fd = evB.v
        yield

        prog.op("tensor",
                lambda e, hs=hs, FE=FE, rbank=rbank:
                    e.matmul(rbank[0:64, 0:64], FE[:, 0:64],
                             sbSORT[:, hs * 64:hs * 64 + 64], start=True, stop=True),
                waits=[(evB, fd)], inc=mmB)
        rmm = mmB.v
        yield

        def mk_rel(e, RS=RS, ST1=ST1, rbank=rbank):
            e.tensor_scalar_max(RS[0:64, :], rbank[0:64, 0:64], 0.0)
            e.drain()
            return e.reduce_max(out=ST1[0:64, 1:2], in_=RS[0:64, :], axis=AX.X,
                                negate=True, opt_input=False, opt_output=False)
        prog.op("vector", mk_rel, waits=[(mmB, rmm)], inc=evB)
        reld = evB.v
        yield

        prog.op("scalar",
                lambda e, RE=RE, RS=RS, ST1=ST1:
                    e.activation(out=RE[0:64, :], in_=RS[0:64, :], func=AF.Exp,
                                 bias=ST1[0:64, 1:2], accum_out=ST1[0:64, 3:4]),
                waits=[(evB, reld)], inc=scB)
        red = scB.v
        yield

        def mk_rn(e, FE=FE, RE=RE, ST1=ST1):
            e.reciprocal(ST1[0:64, 4:5], ST1[0:64, 3:4])
            e.drain()
            return e.tensor_scalar(out=FE[0:64, 0:64], in0=RE[0:64, :],
                                   scalar1=ST1[0:64, 4:5], scalar2=None, op0=ALU.mult)
        prog.op("vector", mk_rn, waits=[(scB, red)], inc=evB)
        rnd = evB.v
        yield

        prog.op("tensor",
                lambda e, FE=FE, ptrX=ptrX, rr=rr:
                    e.transpose(ptrX[rr:rr + 64, 0:64], FE[0:64, 0:64],
                                sbID[0:64, 0:64]),
                waits=[(evB, rnd)], inc=mmB)
        rtt = mmB.v
        prog.op("vector",
                lambda e, RT=RT, ptrX=ptrX, rr=rr:
                    e.tensor_copy(out=RT[rr:rr + 64, :], in_=ptrX[rr:rr + 64, 0:64]),
                waits=[(mmB, rtt)], inc=evB)
        rtd = evB.v
        yield

        if sink_stop < 2:
            return
        kv_ev = rtd
        r_ev_prev = 0
        for g in range(8):
            def mk_kvb(e, s=s, g=g, ptrX=ptrX, rr=rr):
                ins = None
                for j in range(8):
                    tp = g * 8 + j
                    ins = e.transpose(ptrX[rr:rr + 64, j * 128:(j + 1) * 128],
                                      sbKH[s].rearrange("p (v t) -> p t v", v=NB)[:, tp, :],
                                      sbID[:, :])
                return ins
            prog.op("tensor", mk_kvb, waits=[(evB, kv_ev)], inc=mmB)
            kv_mm = mmB.v
            prog.op("vector",
                    lambda e, g=g, ptrX=ptrX, rr=rr:
                        e.tensor_copy(out=sbKVB[rr:rr + 64, g * 1024:(g + 1) * 1024],
                                      in_=ptrX[rr:rr + 64, 0:1024]),
                    waits=[(mmB, kv_mm)], inc=evB)
            kv_ev = evB.v
            yield

            def mk_r(e, g=g, rbank=rbank, rr=rr, RT=RT):
                ins = None
                for j in range(8):
                    tp = g * 8 + j
                    ins = e.matmul(rbank[:, j * 64:(j + 1) * 64],
                                   sbKVB[rr:rr + 64, tp * 128:(tp + 1) * 128],
                                   RT[rr:rr + 64, :], start=True, stop=True)
                return ins
            rw = [(evB, kv_ev)]
            if r_ev_prev > 0:
                rw.append((evB, r_ev_prev))
            prog.op("tensor", mk_r, waits=rw, inc=mmB)
            r_mm = mmB.v

            def ev_r(e, g=g, rbank=rbank, RKV=RKV):
                src = rbank[:, 0:512].rearrange("p (j u) -> p j u", j=8)
                dst = RKV.rearrange("p (u t) -> p t u", u=NB)[:, g * 8:(g + 1) * 8, :]
                return e.tensor_copy(out=dst, in_=src)
            prog.op("vector", ev_r, waits=[(mmB, r_mm)], inc=evB)
            r_ev_prev = evB.v
            yield
        rkvd = evB.v

        if sink_stop < 4:
            return
        P2T = sbWX
        p2to = pi * 2048
        prev_exp = 0
        prev_tr = 0
        prev_et = 0
        prev_av = 0
        prev_ev = 0
        for b in range(16):
            t0 = 2 * b
            def mk_s2(e, s=s, b=b, t0=t0, sbank=sbank, RKV=RKV):
                ins = None
                for half in range(2):
                    tl = t0 + half
                    q_l = sbQH[s][0:64, tl * 128:(tl + 1) * 128]
                    ins = e.matmul(sbank[:, half * 256:half * 256 + 128],
                                   q_l, sbKH[s][0:64, tl * 128:(tl + 1) * 128],
                                   start=True, stop=True)
                    ins = e.matmul(sbank[:, half * 256 + 128:half * 256 + 256],
                                   q_l, RKV[0:64, tl * 128:(tl + 1) * 128],
                                   start=True, stop=True)
                return ins
            sw2 = [(evB, rkvd)]
            if prev_exp > 0:
                sw2.append((scB, prev_exp))
            prog.op("tensor", mk_s2, waits=sw2, inc=mmB)
            s2mm = mmB.v
            yield

            prog.op("vector",
                    lambda e, sbank=sbank:
                        e.tensor_tensor(sbank[:, 0:512], sbank[:, 0:512], sbMKS[:, :],
                                        ALU.add),
                    waits=[(mmB, s2mm)], inc=evB)
            mz2d = evB.v

            xw2 = [(evB, mz2d)]
            if prev_tr > 0:
                xw2.append((mmB, prev_tr))

            def mk_ex2(e, sbank=sbank, po=po, b=b, ST2=ST2):
                e.activation(out=sbP2[:, po:po + 256], in_=sbank[:, 0:256],
                             func=AF.Exp, scale=0.125,
                             accum_out=ST2[:, (b % 2) * 2:(b % 2) * 2 + 1])
                return e.activation(out=sbP2[:, po + 256:po + 512], in_=sbank[:, 256:512],
                                    func=AF.Exp, scale=0.125,
                                    accum_out=ST2[:, (b % 2) * 2 + 1:(b % 2) * 2 + 2])
            prog.op("scalar", mk_ex2, waits=xw2, inc=scB)
            prev_exp = scB.v
            s2exp = scB.v
            yield

            prog.op("vector",
                    lambda e, b=b, ST2=ST2, ST3=ST3:
                        e.reciprocal(ST3[:, (b % 2) * 2:(b % 2) * 2 + 2],
                                     ST2[:, (b % 2) * 2:(b % 2) * 2 + 2]),
                    waits=[(scB, s2exp)], inc=evB)
            sumd = evB.v
            yield

            def mk_t2(e, b=b, po=po, ptrX=ptrX, RKV=RKV):
                ins = None
                for blk in range(4):
                    ins = e.transpose(ptrX[:, blk * 128:(blk + 1) * 128],
                                      sbP2[:, po + blk * 128:po + (blk + 1) * 128],
                                      sbID[:, :])
                for j in range(4):
                    u = 4 * b + j
                    rbase = (u % 2) * 64
                    half = j // 2
                    ins = e.transpose(ptrX[rbase:rbase + 64, 512 + half * 64:512 + (half + 1) * 64],
                                      RKV[64:128, u * 64:(u + 1) * 64],
                                      sbID[64:128, 64:128])
                return ins
            t2w = [(evB, sumd)]
            if prev_et > 0:
                t2w.append((evB, prev_et))
            prog.op("tensor", mk_t2, waits=t2w, inc=mmB)
            prev_tr = mmB.v
            t2d = mmB.v
            yield

            def ev_t2(e, ptrX=ptrX, p2to=p2to):
                e.tensor_copy(out=P2T[:, p2to:p2to + 512], in_=ptrX[:, 0:512])
                return e.tensor_copy(out=P2T[:, p2to + 512:p2to + 640],
                                     in_=ptrX[:, 512:640])
            ew2 = [(mmB, t2d)]
            if prev_av > 0:
                ew2.append((mmB, prev_av))
            prog.op("vector", ev_t2, waits=ew2, inc=evB)
            prev_et = evB.v
            t2e = evB.v
            yield

            def mk_v2(e, s=s, t0=t0, obank=obank, p2to=p2to, h=h):
                hv = (h % 2) * 64
                ins = None
                for half in range(2):
                    tl = t0 + half
                    oc = half * 64
                    ins = e.matmul(obank[:, oc:oc + 64],
                                   P2T[:, p2to + half * 256:p2to + half * 256 + 128],
                                   sbVT[s][:, tl * 128 + hv:tl * 128 + hv + 64],
                                   start=True, stop=False)
                    ins = e.matmul(obank[:, oc:oc + 64],
                                   P2T[:, p2to + half * 256 + 128:p2to + half * 256 + 256],
                                   P2T[:, p2to + 512 + half * 64:p2to + 512 + (half + 1) * 64],
                                   start=False, stop=True)
                return ins
            avw = [(evB, t2e)]
            if prev_ev > 0:
                avw.append((evB, prev_ev))
            prog.op("tensor", mk_v2, waits=avw, inc=mmB)
            prev_av = mmB.v
            v2d = mmB.v
            yield

            def ev_v2(e, b=b, t0=t0, h=h, obank=obank, ST3=ST3):
                ins = None
                for half in range(2):
                    tl = t0 + half
                    col = tl * 1024 + h * 64
                    ins = e.tensor_scalar(
                        out=sbAO[:, col:col + 64],
                        in0=obank[:, half * 64:half * 64 + 64],
                        scalar1=ST3[:, (b % 2) * 2 + half:(b % 2) * 2 + half + 1],
                        scalar2=None, op0=ALU.mult)
                return ins
            prog.op("vector", ev_v2, waits=[(mmB, v2d)], inc=evB)
            prev_ev = evB.v
            yield

    if "Bs" in parts:
        for hp in range(SHH // 2):
            h0, h1 = 2 * hp, 2 * hp + 1
            ld0 = load_sink_head(h0)
            ld1 = load_sink_head(h1)
            g0 = sink_stages(h0, ld0)
            g1 = sink_stages(h1, ld1)
            alive = [g0, g1]
            while alive:
                for g in list(alive):
                    try:
                        next(g)
                    except StopIteration:
                        alive.remove(g)
            slot_free_at[0] = mmB.v
            slot_free_at[1] = mmB.v

    phaseB_ev = evB.v

    dbg_spill_done = 0
    if dbg_ao is not None:
        for tt in range(NT):
            prog.dma("sync", dbg_ao[tt * 128:(tt + 1) * 128, :],
                     sbAO[:, tt * 1024:(tt + 1) * 1024], S["gst"],
                     waits=[(evB, phaseB_ev)] if tt == 0 else ())
        dbg_spill_done = S["gst"].v

    mmC, evC, scC = S["mmC"], S["evC"], S["scC"]
    xld, x1t, x1f = S["xld"], S["x1t"], S["x1f"]

    if "C" in parts:
        c_ev2 = []
        c_ev5 = []
        c_spl = []
        for tc in range(NT):
            lw = [(evB, phaseB_ev)] if tc == 0 else [(evC, c_ev2[tc - 1])]
            prog.dma("sync", sbCx[:, :], x_in[tc * 128:(tc + 1) * 128, :], xld,
                     waits=lw)
            xv = xld.v

            tw = [(evB, phaseB_ev)] if tc == 0 else [(evC, c_ev5[tc - 1])]

            def mk_c1(e, tc=tc):
                ins = None
                for c in range(8):
                    ins = e.transpose(ptr[0][:, c * 128:(c + 1) * 128],
                                      sbAO[:, tc * 1024 + c * 128:tc * 1024 + (c + 1) * 128],
                                      sbID[:, :])
                return ins
            prog.op("tensor", mk_c1, waits=tw, inc=mmC)
            c1d = mmC.v

            prog.op("vector", lambda e: e.tensor_copy(out=sbCa[:, :], in_=ptr[0][:, :1024]),
                    waits=[(mmC, c1d)], inc=evC)
            cad = evC.v

            pw = [(evC, cad), (S["gld2"], setup_g2)]
            if tc >= 1:
                pw.append((evC, c_ev2[tc - 1]))

            def mk_c2(e):
                ins = None
                for dh2 in range(2):
                    for c in range(8):
                        ins = e.matmul(pbank[dh2][:, :],
                                       sbCa[:, c * 128:(c + 1) * 128],
                                       sbWO[:, c * 1024 + dh2 * 512:c * 1024 + dh2 * 512 + 512],
                                       start=(c == 0), stop=False)
                    ins = e.matmul(pbank[dh2][:, :], sbONE[:, :],
                                   sbBO[0:1, dh2 * 512:(dh2 + 1) * 512],
                                   start=False, stop=True)
                return ins
            prog.op("tensor", mk_c2, waits=pw, inc=mmC)
            c2d = mmC.v

            def mk_c3(e):
                e.tensor_tensor(sbCz[:, 0:512], pbank[0][:, :], sbCx[:, 0:512], ALU.add)
                e.tensor_tensor(sbCz[:, 512:1024], pbank[1][:, :], sbCx[:, 512:1024], ALU.add)
                e.drain()
                return e.tensor_reduce(out=sbLN[:, 0:1], in_=sbCz[:, :], axis=AX.X,
                                       op=ALU.add, opt_input=False, opt_output=False)
            prog.op("vector", mk_c3, waits=[(mmC, c2d), (xld, xv)], inc=evC)
            c_ev2.append(evC.v)
            c3d = evC.v

            sqw = [(evC, c3d)]
            if tc >= 1:
                sqw.append((x1t, c_spl[tc - 1][0]))
            prog.op("scalar",
                    lambda e: e.activation(out=sbCsq[:, :], in_=sbCz[:, :], func=AF.Square,
                                           accum_out=sbLN[:, 1:2]),
                    waits=sqw, inc=scC)
            csq = scC.v

            def mk_c4(e, tc=tc):
                e.tensor_scalar_mul(sbLN[:, 2:3], sbLN[:, 0:1], 1.0 / 1024.0)
                e.tensor_scalar_mul(sbLN[:, 3:4], sbLN[:, 1:2], 1.0 / 1024.0)
                e.drain()
                e.scalar_tensor_tensor(out=sbLN[:, 4:5], in0=sbLN[:, 2:3],
                                       scalar=sbLN[:, 2:3], in1=sbLN[:, 3:4],
                                       op0=ALU.mult, op1=ALU.subtract)
                e.drain()
                return e.scalar_tensor_tensor(out=sbLN[:, 5:6], in0=sbLN[:, 4:5],
                                              scalar=-1.0, in1=sbEPSC[:, :],
                                              op0=ALU.mult, op1=ALU.add)
            prog.op("vector", mk_c4, waits=[(scC, csq)], inc=evC)
            c4d = evC.v

            prog.op("scalar",
                    lambda e: e.activation(out=sbLN[:, 6:7], in_=sbLN[:, 5:6],
                                           func=AF.Sqrt),
                    waits=[(evC, c4d)], inc=scC)
            cstd = scC.v

            def mk_c5(e):
                e.reciprocal(sbLN[:, 8:9], sbLN[:, 6:7])
                e.drain()
                e.tensor_scalar(out=sbCsq[:, :], in0=sbCz[:, :], scalar1=sbLN[:, 2:3],
                                scalar2=sbLN[:, 8:9], op0=ALU.subtract, op1=ALU.mult)
                return e.tensor_scalar(out=sbCx1[:, :], in0=sbCz[:, :], scalar1=sbLN[:, 2:3],
                                       scalar2=sbLN[:, 8:9], op0=ALU.subtract, op1=ALU.mult)
            prog.op("vector", mk_c5, waits=[(scC, cstd)], inc=evC)
            c5d = evC.v

            prog.dma("sync", scr_x1t[tc * 128:(tc + 1) * 128, :], sbCsq[:, :], x1t,
                     waits=[(evC, c5d)])

            def mk_c6(e):
                ins = None
                for c in range(8):
                    ins = e.transpose(ptr[0][:, c * 128:(c + 1) * 128],
                                      sbCx1[:, c * 128:(c + 1) * 128], sbID[:, :])
                return ins
            prog.op("tensor", mk_c6, waits=[(evC, c5d)], inc=mmC)
            c6d = mmC.v

            evw2 = [(mmC, c6d)]
            if tc >= 1:
                evw2.append((x1f, c_spl[tc - 1][1]))
            prog.op("vector", lambda e: e.tensor_copy(out=sbCxf[:, :], in_=ptr[0][:, :1024]),
                    waits=evw2, inc=evC)
            c_ev5.append(evC.v)
            c5e = evC.v

            for c in range(8):
                prog.dma("sync", scr_x1f[c, :, tc * 128:(tc + 1) * 128],
                         sbCxf[:, c * 128:(c + 1) * 128], x1f,
                         waits=[(evC, c5e)] if c == 0 else ())
            c_spl.append((x1t.v, x1f.v))

        phaseC_done = [(x1t, x1t.v), (x1f, x1f.v)]


    if "D" in parts:
        mmD, evD, scD = S["mmD"], S["evD"], S["scD"]
        wld = [S["wld0"], S["wld1"]]
        xq, xt2, outd_ = S["xq"], S["xt2"], S["outd"]
        x1t_total = S["x1t"].v
        wld_free_at = [0, 0]
        up_ev = []
        up_sc = []
        dn_ev = []
        prev_sig = 0
        q_x1q_free = 0
        pending_epilogue = []
        epi_sq_guard = [0]

        def load_expert_weights(e_, first):
            sl = e_ % 2
            w = []
            if first:
                w = [(S["pcs0"], precast_done[0][1]), (S["pcs1"], precast_done[1][1])]
                if dbg_spill_done:
                    w.append((S["gst"], dbg_spill_done))
            if wld_free_at[sl] > 0:
                w.append((mmD, wld_free_at[sl]))
            for c in range(8):
                prog.dma("gpsimd", sbWU[sl][:, c * HID:(c + 1) * HID],
                         scr_wub[e_, c * 128:(c + 1) * 128, :], wld[sl], waits=w)
            for ch in range(16):
                prog.dma("gpsimd", sbWD[sl][:, ch * 1024:(ch + 1) * 1024],
                         scr_wdb[e_, ch * 128:(ch + 1) * 128, :], wld[sl], waits=w)
            return wld[sl].v

        for tq in range(4):
            xw = list(phaseC_done) if tq == 0 else []
            if q_x1q_free > 0:
                xw.append((mmD, q_x1q_free))
            for c in range(8):
                prog.dma("gpsimd", sbX1Q[:, c * 1024:(c + 1) * 1024],
                         scr_x1f[c, :, tq * 1024:(tq + 1) * 1024], xq, waits=xw)
            xqv = xq.v

            def mk_g(e):
                ins_ = None
                for ti in range(8):
                    for c in range(8):
                        ins_ = e.matmul(pbank[4][:, ti * 8:(ti + 1) * 8],
                                        sbX1Q[:, c * 1024 + ti * 128:c * 1024 + (ti + 1) * 128],
                                        sbGW[:, c * 8:(c + 1) * 8],
                                        start=(c == 0), stop=(c == 7))
                return ins_
            gw_ = [(xq, xqv)]
            if prev_sig > 0:
                gw_.append((scD, prev_sig))
            prog.op("tensor", mk_g, waits=gw_, inc=mmD)
            gmm = mmD.v

            prog.op("scalar",
                    lambda e: e.activation(out=sbGS[:, :], in_=pbank[4][:, 0:64],
                                           func=AF.Sigmoid),
                    waits=[(mmD, gmm)], inc=scD)
            prev_sig = scD.v

            def mk_w2(e):
                bias_b = sbEB[:, 0:8].unsqueeze(1).broadcast_to([128, 8, 8])
                e.tensor_tensor(sbGB[:, :].rearrange("p (t e) -> p t e", t=8),
                                sbGS[:, :].rearrange("p (t e) -> p t e", t=8),
                                bias_b, ALU.add)
                e.drain()
                ins_ = None
                for ti in range(8):
                    gb = sbGB[:, ti * 8:(ti + 1) * 8]
                    e.reduce_max(out=sbTOP[:, 0:1], in_=gb, axis=AX.X,
                                 opt_input=False, opt_output=False)
                    e.drain()
                    e.tensor_scalar(out=sbW2[:, ti * 8:(ti + 1) * 8], in0=gb,
                                    scalar1=sbTOP[:, 0:1], scalar2=None, op0=ALU.is_ge)
                    e.drain()
                    e.scalar_tensor_tensor(out=sbW2[:, ti * 8:(ti + 1) * 8],
                                           in0=sbW2[:, ti * 8:(ti + 1) * 8],
                                           scalar=-1.0e30, in1=gb,
                                           op0=ALU.mult, op1=ALU.add)
                    e.drain()
                    e.reduce_max(out=sbTOP[:, 1:2], in_=sbW2[:, ti * 8:(ti + 1) * 8],
                                 axis=AX.X, opt_input=False, opt_output=False)
                    e.drain()
                    e.tensor_scalar(out=sbW2[:, ti * 8:(ti + 1) * 8], in0=gb,
                                    scalar1=sbTOP[:, 1:2], scalar2=None, op0=ALU.is_ge)
                    e.drain()
                    e.tensor_tensor(sbW2[:, ti * 8:(ti + 1) * 8],
                                    sbW2[:, ti * 8:(ti + 1) * 8],
                                    sbGS[:, ti * 8:(ti + 1) * 8], ALU.mult)
                    e.drain()
                    ins_ = e.tensor_tensor(sbW2[:, ti * 8:(ti + 1) * 8],
                                           sbW2[:, ti * 8:(ti + 1) * 8],
                                           sbW2[:, ti * 8:(ti + 1) * 8], ALU.mult)
                return ins_
            w2w_ = [(scD, prev_sig)]
            if dbg_w2 is not None and S["gst"].v > 0:
                w2w_.append((S["gst"], S["gst"].v))
            prog.op("vector", mk_w2, waits=w2w_, inc=evD)
            w2d = evD.v

            if pending_epilogue:
                epi_sq_guard[0] = pending_epilogue.pop()()

            if dbg_w2 is not None:
                for ti in range(8):
                    prog.dma("sync", dbg_w2[tq * 1024 + ti * 128:tq * 1024 + (ti + 1) * 128, :],
                             sbW2[:, ti * 8:(ti + 1) * 8], S["gst"],
                             waits=[(evD, w2d)])

            wld_vals = {}
            wld_vals[0] = load_expert_weights(0, first=(tq == 0))
            wld_vals[1] = load_expert_weights(1, first=(tq == 0))

            for e_ in range(NE):
                sl = e_ % 2
                ldv = wld_vals[e_]
                for sl2 in range(2):
                    for jc in range(16):
                        bank = pbank[jc % 2]
                        def mk_up(e, sl=sl, sl2=sl2, jc=jc, bank=bank):
                            ins_ = None
                            for dc in range(8):
                                ins_ = e.matmul(bank[:, :],
                                                sbWU[sl][:, dc * HID + jc * 128:dc * HID + jc * 128 + 128],
                                                sbX1Q[:, dc * 1024 + sl2 * 512:dc * 1024 + sl2 * 512 + 512],
                                                start=(dc == 0), stop=(dc == 7))
                            return ins_
                        uw = [(wld[sl], ldv), (xq, xqv)]
                        if len(up_sc) >= 2:
                            uw.append((scD, up_sc[-2]))
                        prog.op("tensor", mk_up, waits=uw, inc=mmD)
                        umm = mmD.v
                        rw = [(mmD, umm)]
                        if len(up_ev) >= 2:
                            rw.append((evD, up_ev[-2]))
                        prog.op("scalar",
                                lambda e, jc=jc, bank=bank:
                                    e.activation(out=sbHR[jc % 2][:, :], in_=bank[:, :],
                                                 func=AF.Relu),
                                waits=rw, inc=scD)
                        up_sc.append(scD.v)
                        if e_ == 0 and sl2 == 0:
                            sqg = [(scD, up_sc[-1])]
                            if jc == 0 and epi_sq_guard[0] != 0:
                                sqg.append((outd_, epi_sq_guard[0][1]))
                            prog.op("scalar",
                                    lambda e, jc=jc:
                                        e.activation(out=sbHE[:, jc * 512:(jc + 1) * 512],
                                                     in_=sbHR[jc % 2][:, :],
                                                     func=AF.Square),
                                    waits=sqg, inc=scD)
                            up_sc[-1] = scD.v
                        else:
                            sqg = [(scD, up_sc[-1])]
                            if e_ == 0 and epi_sq_guard[0] != 0:
                                sqg.append((scD, epi_sq_guard[0][0]))
                                sqg.append((outd_, epi_sq_guard[0][1]))
                            prog.op("vector",
                                    lambda e, jc=jc:
                                        e.tensor_tensor(sbHE[:, jc * 512:(jc + 1) * 512],
                                                        sbHR[jc % 2][:, :], sbHR[jc % 2][:, :],
                                                        ALU.mult),
                                    waits=sqg, inc=evD)
                            up_ev.append(evD.v)
                    h_ready_w = (scD, scD.v) if (e_ == 0 and sl2 == 0) else (evD, evD.v)
                    h_ready = evD.v
                    for cg in range(4):
                        for dh2 in range(2):
                            bank = pbank[2 + (cg * 2 + dh2) % 2]
                            def mk_dn(e, sl=sl, cg=cg, dh2=dh2, bank=bank):
                                ins_ = None
                                for jc in range(16):
                                    ins_ = e.matmul(bank[:, :],
                                                    sbHE[:, jc * 512 + cg * 128:jc * 512 + cg * 128 + 128],
                                                    sbWD[sl][:, jc * 1024 + dh2 * 512:jc * 1024 + dh2 * 512 + 512],
                                                    start=(jc == 0), stop=(jc == 15))
                                return ins_
                            dw = [h_ready_w, (wld[sl], ldv)]
                            if len(dn_ev) >= 2:
                                dw.append((evD, dn_ev[-2]))
                            prog.op("tensor", mk_dn, waits=dw, inc=mmD)
                            dmm = mmD.v
                            ti = sl2 * 4 + cg
                            def mk_acc(e, bank=bank, ti=ti, e_=e_, dh2=dh2):
                                dst = sbACC[:, ti * 1024 + dh2 * 512:ti * 1024 + dh2 * 512 + 512]
                                if e_ == 0:
                                    return e.tensor_scalar(out=dst, in0=bank[:, :],
                                                           scalar1=sbW2[:, ti * 8:ti * 8 + 1],
                                                           scalar2=None, op0=ALU.mult)
                                return e.scalar_tensor_tensor(out=dst, in0=bank[:, :],
                                                              scalar=sbW2[:, ti * 8 + e_:ti * 8 + e_ + 1],
                                                              in1=dst, op0=ALU.mult, op1=ALU.add)
                            prog.op("vector", mk_acc, waits=[(mmD, dmm), (evD, w2d)], inc=evD)
                            dn_ev.append(evD.v)
                if e_ + 2 < NE:
                    wld_free_at[sl] = mmD.v
                    wld_vals[e_ + 2] = load_expert_weights(e_ + 2, first=False)
                else:
                    wld_free_at[sl] = mmD.v
            q_x1q_free = mmD.v

            def plan_epilogue(tq=tq):
                slots = [sbX1T, sbO]
                last_sq = [0, 0]
                last_ap = [0, 0]
                for pg in range(4):
                    zd, sqd_, stdd, sqr2 = [0, 0], [0, 0], [0, 0], [0, 0]
                    for j in range(2):
                        ti = pg * 2 + j
                        trow = tq * 1024 + ti * 128
                        lw = [(outd_, outd_.v)]
                        if last_ap[j] > 0:
                            lw.append((evD, last_ap[j]))
                        if tq == 0 and ti == 0:
                            lw.append((S["x1t"], x1t_total))
                        prog.dma("sync", slots[j][:, :], scr_x1t[trow:trow + 128, :],
                                 xt2, waits=lw)
                    xtv = xt2.v
                    for j in range(2):
                        ti = pg * 2 + j
                        prog.op("vector",
                                lambda e, ti=ti, j=j:
                                    e.tensor_tensor(slots[j][:, :],
                                                    sbACC[:, ti * 1024:(ti + 1) * 1024],
                                                    slots[j][:, :], ALU.add),
                                waits=[(xt2, xtv)] + ([(evD, last_ap[j])] if last_ap[j] else []),
                                inc=evD)
                        zd[j] = evD.v
                    sum_v = [0, 0]
                    for j in range(2):
                        prog.op("vector",
                                lambda e, j=j:
                                    e.tensor_reduce(out=sbLN[:, j * 8:j * 8 + 1],
                                                    in_=slots[j][:, :], axis=AX.X,
                                                    op=ALU.add, opt_input=False,
                                                    opt_output=False),
                                waits=[(evD, zd[j])], inc=evD)
                        sum_v[j] = evD.v
                        sqw = [(evD, zd[j])]
                        sqw.append((outd_, outd_.v))
                        prog.op("scalar",
                                lambda e, j=j:
                                    e.activation(out=sbEPQ[j][:, :], in_=slots[j][:, :],
                                                 func=AF.Square,
                                                 accum_out=sbLN[:, j * 8 + 1:j * 8 + 2]),
                                waits=sqw, inc=scD)
                        sqd_[j] = scD.v
                        last_sq[j] = scD.v
                    for j in range(2):
                        def mk_st(e, j=j):
                            e.tensor_scalar_mul(sbLN[:, j * 8 + 2:j * 8 + 3],
                                                sbLN[:, j * 8:j * 8 + 1], 1.0 / 1024.0)
                            e.tensor_scalar_mul(sbLN[:, j * 8 + 3:j * 8 + 4],
                                                sbLN[:, j * 8 + 1:j * 8 + 2], 1.0 / 1024.0)
                            e.drain()
                            e.scalar_tensor_tensor(out=sbLN[:, j * 8 + 4:j * 8 + 5],
                                                   in0=sbLN[:, j * 8 + 2:j * 8 + 3],
                                                   scalar=sbLN[:, j * 8 + 2:j * 8 + 3],
                                                   in1=sbLN[:, j * 8 + 3:j * 8 + 4],
                                                   op0=ALU.mult, op1=ALU.subtract)
                            e.drain()
                            return e.scalar_tensor_tensor(out=sbLN[:, j * 8 + 5:j * 8 + 6],
                                                          in0=sbLN[:, j * 8 + 4:j * 8 + 5],
                                                          scalar=-1.0, in1=sbEPSC[:, :],
                                                          op0=ALU.mult, op1=ALU.add)
                        prog.op("vector", mk_st, waits=[(scD, sqd_[j]), (evD, sum_v[j])],
                                inc=evD)
                        stdd[j] = evD.v
                    for j in range(2):
                        prog.op("scalar",
                                lambda e, j=j:
                                    e.activation(out=sbLN[:, j * 8 + 6:j * 8 + 7],
                                                 in_=sbLN[:, j * 8 + 5:j * 8 + 6],
                                                 func=AF.Sqrt),
                                waits=[(evD, stdd[j])], inc=scD)
                        sqr2[j] = scD.v
                    for j in range(2):
                        ti = pg * 2 + j
                        trow = tq * 1024 + ti * 128
                        def mk_ap(e, j=j):
                            e.reciprocal(sbLN[:, j * 8 + 7:j * 8 + 8],
                                         sbLN[:, j * 8 + 6:j * 8 + 7])
                            e.drain()
                            return e.tensor_scalar(out=sbEPQ[j][:, :], in0=slots[j][:, :],
                                                   scalar1=sbLN[:, j * 8 + 2:j * 8 + 3],
                                                   scalar2=sbLN[:, j * 8 + 7:j * 8 + 8],
                                                   op0=ALU.subtract, op1=ALU.mult)
                        prog.op("vector", mk_ap, waits=[(scD, sqr2[j])], inc=evD)
                        apd = evD.v
                        last_ap[j] = apd
                        prog.dma("sync", out[trow:trow + 128, :], sbEPQ[j][:, :], outd_,
                                 waits=[(evD, apd)])
                return (max(last_sq), outd_.v)
            pending_epilogue.append(plan_epilogue)

        if pending_epilogue:
            pending_epilogue.pop()()

    outd = S["outd"]
    if "D" not in parts:
        src = scr_x1t if "C" in parts else x_in
        for tt in range(NT):
            lw = [(outd, outd.v)]
            if tt == 0:
                lw.extend(phaseA_done)
                if "C" in parts:
                    lw.extend(phaseC_done)
            prog.dma("sync", sbO[:, :], src[tt * 128:(tt + 1) * 128, :], S["sld"], waits=lw)
            prog.dma("sync", out[tt * 128:(tt + 1) * 128, :], sbO[:, :], outd,
                     waits=[(S["sld"], S["sld"].v)])
    prog.wait("sync", outd, outd.v)

    with nc.Block() as block:
        for eng in ENGINES:
            ops = prog.ops[eng]
            if not ops:
                continue
            def body(e, ops=ops):
                for fn in ops:
                    fn(e)
            getattr(block, eng)(body)
    estack.close()
    return nc


_NC_CACHE = {}


def _get_nc(debug=()):
    key = tuple(sorted(debug))
    if key not in _NC_CACHE:
        _NC_CACHE[key] = build_nc(debug=key)
    return _NC_CACHE[key]


def _shard_inputs(inputs):
    x = np.ascontiguousarray(inputs["L0_attn_x"])
    wq = np.ascontiguousarray(inputs["L0_attn_w_q"])
    wkv = np.ascontiguousarray(inputs["L0_attn_w_kv"])
    wo = np.ascontiguousarray(inputs["L0_attn_w_out"])
    bo = np.ascontiguousarray(inputs["L0_attn_b_out"])
    sortw = np.ascontiguousarray(inputs["L0_attn_sort_w"]).reshape(SHH, 2 * DHH, NB)
    gw = np.ascontiguousarray(inputs["L0_ffn_gate_w"])
    wu = np.ascontiguousarray(inputs["L0_ffn_w_up"])
    wd = np.ascontiguousarray(inputs["L0_ffn_w_down"])
    eb = np.ascontiguousarray(inputs["L0_ffn_expert_bias"])

    gwT = np.ascontiguousarray(gw.T)
    wuT = np.ascontiguousarray(wu.transpose(0, 2, 1))
    wdT = np.ascontiguousarray(wd.transpose(0, 2, 1))
    ebr = np.ascontiguousarray(np.broadcast_to(eb[None, :], (128, NE)))

    in_maps = []
    for c in range(8):
        xc = np.ascontiguousarray(x[c])
        in_maps.append({
            "xT": np.ascontiguousarray(xc.T),
            "x_in": xc,
            "wq": wq, "wkv": wkv, "wo": wo, "bo": bo,
            "sortw": sortw, "gwT": gwT, "ebias": ebr,
            "wuT": wuT, "wdT": wdT,
        })
    return in_maps


def kernel(**inputs):
    nc = _get_nc()
    in_maps = _shard_inputs(inputs)
    res = run_bass_kernel_spmd(nc, in_maps, core_ids=list(range(8)))
    return np.stack([r["out"] for r in res.results], axis=0)


def run_debug(inputs, debug, parts=("A", "Bl", "Bs", "C", "D"), cores=8, sink_stop=99):
    nc = build_nc(debug=tuple(debug), parts=parts, sink_stop=sink_stop)
    in_maps = _shard_inputs(inputs)[:cores]
    res = run_bass_kernel_spmd(nc, in_maps, core_ids=list(range(cores)))
    return res


# version 10 (final)
# speedup vs baseline: 1.0453x; 1.0046x over previous
import numpy as np
import ml_dtypes
import concourse.bass as bass
import concourse.mybir as mybir
from concourse.bass_utils import run_bass_kernel_spmd

F32 = mybir.dt.float32
BF16 = mybir.dt.bfloat16
AF = mybir.ActivationFunctionType
ALU = mybir.AluOpType
AX = mybir.AxisListType

T = 4096
D = 1024
H, LHH, SHH, DHH = 16, 4, 12, 64
NB, BW = 64, 64
NT = T // 128
NE, HID = 8, 2048
EPS = 1e-5

ENGINES = ("sync", "tensor", "vector", "scalar", "gpsimd")


class Sem:
    def __init__(self, handle):
        self.h = handle
        self.v = 0

    def bump(self, k):
        self.v += k
        return self.v


class Prog:
    def __init__(self):
        self.ops = {e: [] for e in ENGINES}

    def emit(self, eng, fn):
        self.ops[eng].append(fn)

    def op(self, eng, build, waits=(), inc=None, inc_n=1):
        waits = tuple((s, v) for (s, v) in waits if v > 0)
        if inc is not None:
            inc.bump(inc_n)

        def f(e, build=build, waits=waits, inc=inc, inc_n=inc_n):
            for s, v in waits:
                e.wait_ge(s.h, v)
            ins = build(e)
            if inc is not None:
                ins.then_inc(inc.h, inc_n)

        self.emit(eng, f)

    def dma(self, qeng, out_ap, in_ap, sem, waits=()):
        waits = tuple((s, v) for (s, v) in waits if v > 0)
        sem.bump(16)

        def f(e, out_ap=out_ap, in_ap=in_ap, waits=waits, sem=sem):
            for s, w in waits:
                e.wait_ge(s.h, w)
            e.dma_start(out=out_ap, in_=in_ap).then_inc(sem.h, 16)

        self.emit(qeng, f)
        return sem.v

    def wait(self, eng, sem, v):
        if v > 0:
            self.emit(eng, lambda e, sem=sem, v=v: e.wait_ge(sem.h, v))


def build_nc(debug=(), parts=("A", "Bl", "Bs", "C", "D"), sink_stop=99):
    nc = bass.Bass("TRN2", target_bir_lowering=False)

    xT = nc.dram_tensor("xT", [D, T], F32, kind="ExternalInput").ap()
    x_in = nc.dram_tensor("x_in", [T, D], F32, kind="ExternalInput").ap()
    wq = nc.dram_tensor("wq", [D, D], F32, kind="ExternalInput").ap()
    wkv = nc.dram_tensor("wkv", [D, 2 * D], F32, kind="ExternalInput").ap()
    wo = nc.dram_tensor("wo", [D, D], F32, kind="ExternalInput").ap()
    bo = nc.dram_tensor("bo", [D], F32, kind="ExternalInput").ap()
    sortw = nc.dram_tensor("sortw", [SHH, 2 * DHH, NB], F32, kind="ExternalInput").ap()
    gwT = nc.dram_tensor("gwT", [D, NE], F32, kind="ExternalInput").ap()
    ebias = nc.dram_tensor("ebias", [128, NE], F32, kind="ExternalInput").ap()
    wuT = nc.dram_tensor("wuT", [NE, D, HID], F32, kind="ExternalInput").ap()
    wdT = nc.dram_tensor("wdT", [NE, HID, D], F32, kind="ExternalInput").ap()
    out = nc.dram_tensor("out", [T, D], F32, kind="ExternalOutput").ap()

    def scratch(name, shape, dtype):
        kind = "ExternalOutput" if name in debug else "Internal"
        return nc.dram_tensor(name, shape, dtype, kind=kind).ap()

    scr_q = scratch("scr_q", [8, 128, T], BF16)
    scr_kl = scratch("scr_kl", [2, 128, T], BF16)
    scr_kv = scratch("scr_kv", [SHH, 128, T], BF16)
    scr_vt = scratch("scr_vt", [8, NT, 128, 128], BF16)
    scr_x1f = scratch("scr_x1f", [8, 128, T], BF16)
    scr_x1t = scratch("scr_x1t", [T, D], F32)
    scr_wub = scratch("scr_wub", [NE, D, HID], BF16)
    scr_wdb = scratch("scr_wdb", [NE, HID, D], BF16)
    dbg_ao = scratch("dbg_ao", [T, D], BF16) if "dbg_ao" in debug else None
    dbg_w2 = scratch("dbg_w2", [T, NE], F32) if "dbg_w2" in debug else None

    ident_h = nc.inline_tensor(np.eye(128, dtype=ml_dtypes.bfloat16), "ident128").ap()
    ones_h = nc.inline_tensor(np.ones((1, 128), dtype=ml_dtypes.bfloat16), "onesrow").ap()
    NEG = -1.5e38
    _ml = np.zeros((128, 256), dtype=np.float32)
    _ml[0:64, 192:256] = NEG
    _ml[64:128, 0:64] = NEG
    maskl_h = nc.inline_tensor(_ml.astype(ml_dtypes.bfloat16), "maskl").ap()
    _ms = np.zeros((128, 512), dtype=np.float32)
    for _half in range(2):
        _c0 = _half * 256
        _ms[0:64, _c0 + 64:_c0 + 128] = NEG
        _ms[64:128, _c0:_c0 + 64] = NEG
        _ms[0:64, _c0 + 192:_c0 + 256] = NEG
        _ms[64:128, _c0 + 128:_c0 + 192] = NEG
    masks_h = nc.inline_tensor(_ms.astype(ml_dtypes.bfloat16), "masks").ap()
    negrow_h = nc.inline_tensor(np.full((1, 64), NEG, dtype=ml_dtypes.bfloat16), "negrow").ap()
    epscol_h = nc.inline_tensor(np.full((128, 1), EPS, dtype=np.float32), "epscol").ap()

    ar1 = nc.alloc_sbuf_tensor("arena1", [128, 65536], mybir.dt.uint8).ap()
    ar2 = nc.alloc_sbuf_tensor("arena2", [128, 65536], mybir.dt.uint8).ap()
    ar3 = nc.alloc_sbuf_tensor("arena3", [128, 32768], mybir.dt.uint8).ap()
    ar4 = nc.alloc_sbuf_tensor("arena4", [128, 32768], mybir.dt.uint8).ap()
    arS = nc.alloc_sbuf_tensor("arenaS", [128, 15680], mybir.dt.uint8).ap()

    def sb(arena, off, dtype, cols, rows=128):
        es = mybir.dt.size(dtype)
        assert off % 32 == 0 and off + cols * es <= arena.shape[1] * 1
        return arena.bitcast(dtype)[0:rows, off // es: off // es + cols]

    sbXF = sb(ar1, 0, BF16, 8 * T)
    sbAO = sb(ar1, 0, BF16, NT * 1024)
    sbWU = [sb(ar1, i * 32768, BF16, 8 * HID) for i in range(2)]

    sbWQ = sb(ar2, 0, BF16, 8 * 1024)
    sbWKV = sb(ar2, 16384, BF16, 8 * 2048)
    sbQH = [sb(ar2, i * 8192, BF16, T) for i in range(2)]
    sbKH = [sb(ar2, 16384 + i * 8192, BF16, T) for i in range(2)]
    sbVT = [sb(ar2, 32768 + i * 8192, BF16, NT * 128) for i in range(2)]
    sbWO = sb(ar2, 49152, BF16, 8 * 1024)
    sbWD = [sb(ar2, i * 32768, BF16, 16 * 1024) for i in range(2)]
    sbHR = [sb(arS, 13632 + i * 1024, BF16, 512) for i in range(2)]

    sbSTG = sb(ar3, 0, BF16, 4 * 1024)
    sbKVB = sb(ar3, 0, BF16, BW * 128)
    sbRKV = sb(ar3, 16384, BF16, T)
    sbACC = sb(ar3, 0, F32, 8 * 1024)
    sbCz = sb(ar3, 0, F32, 1024)
    sbCsq = sb(ar3, 4096, F32, 1024)
    sbCx = sb(ar3, 8192, F32, 1024)
    sbCa = sb(ar3, 12288, BF16, 1024)
    sbCx1 = sb(ar3, 14336, BF16, 1024)
    sbCxf = sb(ar3, 16384, BF16, 1024)

    sbX1Q = sb(ar4, 0, BF16, 8 * 1024)
    sbHE = sb(ar4, 16384, BF16, 16 * 512)
    sbPP = sb(ar4, 0, BF16, 512)
    sbP2 = sb(ar4, 1024, BF16, 8 * 128)
    sbP2T = sb(ar4, 3072, BF16, 8 * 64)
    sbBVR = sb(ar4, 4096, BF16, 8 * 64)
    sbFE = sb(ar4, 5120, BF16, 64)
    sbRT = sb(ar4, 5248, BF16, 64)
    sbRS = sb(ar4, 5376, F32, 64, rows=64)
    sbRE = sb(ar4, 5632, F32, 64, rows=64)
    sbST1 = sb(ar4, 5888, F32, 16)
    sbST2 = sb(ar4, 5952, F32, 16)
    sbST3 = sb(ar4, 6016, F32, 16)
    sbFE32 = sb(ar4, 6144, F32, 64)
    sbWX = sb(ar4, 8192, BF16, 8 * 512)

    sbID = sb(arS, 0, BF16, 128)
    sbSORT = sb(arS, 256, BF16, SHH * NB)
    sbGW = sb(arS, 1792, BF16, 64)
    sbBO = sb(arS, 1920, BF16, 1024, rows=1)
    sbONE = sb(arS, 3968, BF16, 128, rows=1)
    sbEB = sb(arS, 4224, F32, NE)
    sbW2 = sb(arS, 4480, F32, 8 * NE)
    sbGS = sb(arS, 4736, F32, 8 * NE)
    sbGB = sb(arS, 4992, F32, 8 * NE)
    sbTOP = sb(arS, 5248, F32, 8)
    sbLN = sb(arS, 5312, F32, 16)
    sbEPSC = sb(arS, 5376, F32, 1)
    sbMKL = sb(ar4, 3072, BF16, 256)
    sbMKS = sb(ar4, 3584, BF16, 512)
    sbNEGR = sb(ar4, 4608, BF16, 64, rows=1)
    sbO = sb(arS, 5440, F32, 1024)
    sbX1T = sb(arS, 9536, F32, 1024)
    sbEPQ = [sb(ar4, 16384 + i * 4096, F32, 1024) for i in range(2)]

    pbank = [nc.alloc_psum_tensor(f"pb{i}", [128, 512], F32).ap() for i in range(6)]
    ptr = [nc.alloc_psum_tensor(f"pt{i}", [128, 1024], BF16).ap() for i in range(2)]

    import contextlib
    estack = contextlib.ExitStack()
    semnames = [
        "gld", "gldA", "sld", "sst", "gst",
        "stg0", "stg1", "stg2", "stg3",
        "mmA", "evA",
        "mmB", "evB", "scB", "hld0", "hld1",
        "mmC", "evC", "scC", "xld", "x1t", "x1f", "gld2",
        "mmD", "evD", "scD", "wld0", "wld1", "pcl0", "pcs0", "pcl1", "pcs1", "xq", "xt2",
        "outd",
    ]
    S = {nm: Sem(estack.enter_context(nc.semaphore(nm))) for nm in semnames}

    prog = Prog()

    for c in range(8):
        prog.dma("gpsimd", sbXF[:, c * T:(c + 1) * T], xT[c * 128:(c + 1) * 128, :], S["gldA"])
    for c in range(8):
        prog.dma("gpsimd", sbWQ[:, c * 1024:(c + 1) * 1024], wq[c * 128:(c + 1) * 128, :], S["gldA"])
    setup_gA = S["gldA"].v
    for c in range(8):
        prog.dma("gpsimd", sbWKV[:, c * 2048:(c + 1) * 2048], wkv[c * 128:(c + 1) * 128, :], S["gld"])
    for hh in range(SHH):
        prog.dma("gpsimd", sbSORT[:, hh * NB:(hh + 1) * NB], sortw[hh], S["gld"])
    for c in range(8):
        prog.dma("gpsimd", sbGW[:, c * 8:(c + 1) * 8], gwT[c * 128:(c + 1) * 128, :], S["gld"])
    prog.dma("gpsimd", sbBO[:, :], bo.rearrange("(a b) -> a b", a=1), S["gld"])
    setup_g = S["gld"].v
    prog.dma("sync", sbID[:, :], ident_h, S["sld"])
    prog.dma("sync", sbONE[:, :], ones_h, S["sld"])
    prog.dma("sync", sbEB[:, :], ebias, S["sld"])
    prog.dma("sync", sbMKL[:, :], maskl_h, S["sld"])
    prog.dma("sync", sbMKS[:, :], masks_h, S["sld"])
    prog.dma("sync", sbNEGR[:, :], negrow_h, S["sld"])
    prog.dma("sync", sbEPSC[:, :], epscol_h, S["sld"])
    setup_s = S["sld"].v
    prog.wait("tensor", S["sld"], setup_s)
    prog.wait("vector", S["sld"], setup_s)
    prog.wait("scalar", S["sld"], setup_s)
    for c in range(8):
        prog.dma("gpsimd", sbWO[:, c * 1024:(c + 1) * 1024], wo[c * 128:(c + 1) * 128, :], S["gld2"])
    setup_g2 = S["gld2"].v

    sbPC = [sb(ar4, 16384 + i * 4096, BF16, 2048) for i in range(2)]
    pcl = [S["pcl0"], S["pcl1"]]
    pcs = [S["pcs0"], S["pcs1"]]
    pc_jobs = []
    for e_ in range(NE):
        for ch in range(8):
            pc_jobs.append((wuT[e_, ch * 128:(ch + 1) * 128, :],
                            scr_wub[e_, ch * 128:(ch + 1) * 128, :]))
    for e_ in range(NE):
        for ch in range(16):
            pc_jobs.append((wdT[e_, ch * 128:(ch + 1) * 128, :],
                            scr_wdb[e_, ch * 128:(ch + 1) * 128, :]))
    for j, (src, dst) in enumerate(pc_jobs):
        slot = j % 2
        ncols = src.shape[1]
        w = [(pcs[slot], pcs[slot].v)] if j >= 2 else []
        prog.dma("gpsimd", sbPC[slot][:, 0:ncols], src, pcl[slot], waits=w)
        prog.dma("gpsimd", dst, sbPC[slot][:, 0:ncols], pcs[slot],
                 waits=[(pcl[slot], pcl[slot].v)])
    precast_done = [(pcs[0], pcs[0].v), (pcs[1], pcs[1].v)]

    mmA, evA = S["mmA"], S["evA"]
    stgsem = [S["stg0"], S["stg1"], S["stg2"], S["stg3"]]
    unit = [0]

    def proj_unit(mk_mms, evacs, stores, qonly=False):
        k = unit[0]
        bank = pbank[k % 2]
        slot = stgsem[k % 4]
        gate = (S["gldA"], setup_gA) if qonly else (S["gld"], setup_g)
        prog.op("tensor", (lambda e, bank=bank: mk_mms(e, bank)),
                waits=[gate, (S["gldA"], setup_gA), (evA, k - 1)], inc=mmA)
        mw = mmA.v

        evw = [(mmA, mw), (slot, slot.v)]

        def ev(e, bank=bank, evacs=evacs):
            ins = None
            for fn in evacs:
                ins = fn(e, bank)
            return ins
        prog.op("vector", ev, waits=evw, inc=evA)
        ew = evA.v
        for dst, src in stores:
            prog.dma("sync", dst, src, slot, waits=[(evA, ew)])
        unit[0] += 1

    def stg(k, r0, nr, cols=512):
        s0 = (k % 4) * 1024
        return sbSTG[r0:r0 + nr, s0:s0 + cols]

    for pc in range(8):
        for sl in range(8):
            k = unit[0]
            def mk(e, bank, pc=pc, sl=sl):
                ins = None
                for dc in range(8):
                    ins = e.matmul(bank[:, :],
                                   sbWQ[:, dc * 1024 + pc * 128: dc * 1024 + pc * 128 + 128],
                                   sbXF[:, dc * T + sl * 512: dc * T + sl * 512 + 512],
                                   start=(dc == 0), stop=(dc == 7))
                return ins
            def ev0(e, bank, k=k):
                return e.tensor_copy(out=stg(k, 0, 128), in_=bank[:, :])
            proj_unit(mk, [ev0],
                      [(scr_q[pc, :, sl * 512:(sl + 1) * 512], stg(k, 0, 128))],
                      qonly=True)

    for pc in range(8):
        for sl in range(8):
            k = unit[0]
            def mk(e, bank, pc=pc, sl=sl):
                ins = None
                for dc in range(8):
                    ins = e.matmul(bank[:, :],
                                   sbWKV[:, dc * 2048 + pc * 128: dc * 2048 + pc * 128 + 128],
                                   sbXF[:, dc * T + sl * 512: dc * T + sl * 512 + 512],
                                   start=(dc == 0), stop=(dc == 7))
                return ins
            def ev0(e, bank, k=k):
                return e.tensor_copy(out=stg(k, 0, 128), in_=bank[:, :])
            if pc < 2:
                proj_unit(mk, [ev0],
                          [(scr_kl[pc, :, sl * 512:(sl + 1) * 512], stg(k, 0, 128))])
            else:
                hs0, hs1 = 2 * pc - 4, 2 * pc - 3
                full = stg(k, 0, 128)
                proj_unit(mk, [ev0],
                          [(scr_kv[hs0, 0:64, sl * 512:(sl + 1) * 512], full[0:64, :]),
                           (scr_kv[hs1, 0:64, sl * 512:(sl + 1) * 512], full[64:128, :])])

    for vc in range(2, 8):
        for sl in range(8):
            k = unit[0]
            def mk(e, bank, vc=vc, sl=sl):
                ins = None
                for dc in range(8):
                    ins = e.matmul(bank[:, :],
                                   sbWKV[:, dc * 2048 + 1024 + vc * 128: dc * 2048 + 1024 + vc * 128 + 128],
                                   sbXF[:, dc * T + sl * 512: dc * T + sl * 512 + 512],
                                   start=(dc == 0), stop=(dc == 7))
                return ins
            def ev0(e, bank, k=k):
                return e.tensor_copy(out=stg(k, 0, 128), in_=bank[:, :])
            hs0, hs1 = 2 * vc - 4, 2 * vc - 3
            full = stg(k, 0, 128)
            proj_unit(mk, [ev0],
                      [(scr_kv[hs0, 64:128, sl * 512:(sl + 1) * 512], full[0:64, :]),
                       (scr_kv[hs1, 64:128, sl * 512:(sl + 1) * 512], full[64:128, :])])

    for tt in range(NT):
        for vh in range(2):
            k = unit[0]
            def mk(e, bank, tt=tt, vh=vh):
                ins = None
                for dc in range(8):
                    ins = e.matmul(bank[:, :],
                                   sbXF[:, dc * T + tt * 128: dc * T + tt * 128 + 128],
                                   sbWKV[:, dc * 2048 + 1024 + vh * 512: dc * 2048 + 1024 + vh * 512 + 512],
                                   start=(dc == 0), stop=(dc == 7))
                return ins
            def ev0(e, bank, k=k):
                return e.tensor_copy(out=stg(k, 0, 128), in_=bank[:, :])
            stores = []
            for pp in range(4):
                p = 4 * vh + pp
                stores.append((scr_vt[p, tt], stg(k, 0, 128)[:, pp * 128:(pp + 1) * 128]))
            proj_unit(mk, [ev0], stores)

    phaseA_done = [(s, s.v) for s in stgsem]

    mmB, scB, evB = S["mmB"], S["scB"], S["evB"]
    hld = [S["hld0"], S["hld1"]]
    slot_free_at = [0, 0]

    def load_local_pair(p):
        s = p % 2
        w = list(phaseA_done) if p == 0 else []
        if slot_free_at[s] > 0:
            w.append((mmB, slot_free_at[s]))
        prog.dma("sync", sbQH[s][:, :], scr_q[p], hld[s], waits=w)
        prog.dma("sync", sbKH[s][:, :], scr_kl[p], hld[s], waits=w)
        for tc in range(4):
            prog.dma("sync",
                     sbVT[s].rearrange("p (t c) -> p t c", c=128)[:, tc * 8:(tc + 1) * 8, :],
                     scr_vt[p, tc * 8:(tc + 1) * 8].transpose([1, 0, 2]),
                     hld[s], waits=w)
        return hld[s].v

    def load_sink_head(hs):
        s = hs % 2
        w = []
        if slot_free_at[s] > 0:
            w.append((mmB, slot_free_at[s]))
        h = hs + 4
        pair = h // 2
        rb = (h % 2) * 64
        prog.dma("sync", sbQH[s][0:64, :], scr_q[pair][rb:rb + 64, :], hld[s], waits=w)
        prog.dma("sync", sbKH[s][:, :], scr_kv[hs], hld[s], waits=w)
        for tc in range(4):
            prog.dma("sync",
                     sbVT[s].rearrange("p (t c) -> p t c", c=128)[:, tc * 8:(tc + 1) * 8, :],
                     scr_vt[pair, tc * 8:(tc + 1) * 8].transpose([1, 0, 2]),
                     hld[s], waits=w)
        return hld[s].v

    lu = [0]
    l_tr_mm = []
    l_av_mm = []
    l_pt_ev = []
    l_av_ev = []
    l_exp = []

    def local_head_tile(s, h, tt, ldv):
        u = lu[0]
        rb = (h % 2) * 64
        hv = (h % 2) * 64
        sbank = pbank[u % 2]
        obank = pbank[2 + u % 2]
        pslot = (u % 2) * 256
        wslot = (u % 2) * 1024

        if tt == 0:
            c0, c1, p0 = 0, 192, 64
        elif tt == NT - 1:
            c0, c1, p0 = tt * 128 - 64, tt * 128 + 128, 0
        else:
            c0, c1, p0 = tt * 128 - 64, tt * 128 + 192, 0

        def mk_s(e, s=s, rb=rb, tt=tt, c0=c0, c1=c1, p0=p0, sbank=sbank):
            ins = e.matmul(sbank[:, p0:p0 + (c1 - c0)],
                           sbQH[s][rb:rb + 64, tt * 128:tt * 128 + 128],
                           sbKH[s][rb:rb + 64, c0:c1],
                           start=True, stop=True)
            if tt == 0:
                ins = e.matmul(sbank[:, 0:64], sbONE[:, :], sbNEGR[:, :],
                               start=True, stop=True)
            if tt == NT - 1:
                ins = e.matmul(sbank[:, 192:256], sbONE[:, :], sbNEGR[:, :],
                               start=True, stop=True)
            return ins
        sw = [(hld[s], ldv)]
        if u >= 2:
            sw.append((scB, l_exp[u - 2]))
        prog.op("tensor", mk_s, waits=sw, inc=mmB)
        smm = mmB.v

        prog.op("vector",
                lambda e, sbank=sbank:
                    e.tensor_tensor(sbank[:, 0:256], sbank[:, 0:256], sbMKL[:, :],
                                    ALU.add),
                waits=[(mmB, smm)], inc=evB)
        mzd = evB.v

        expw = [(evB, mzd)]
        if u >= 2:
            expw.append((mmB, l_tr_mm[u - 2]))

        prog.op("scalar",
                lambda e, sbank=sbank, pslot=pslot:
                    e.activation(out=sbPP[:, pslot:pslot + 256], in_=sbank[:, 0:256],
                                 func=AF.Exp, accum_out=sbST1[:, 0:1]),
                waits=expw, inc=scB)
        l_exp.append(scB.v)
        sexp = scB.v

        prog.op("vector",
                lambda e: e.reciprocal(sbST2[:, 0:1], sbST1[:, 0:1]),
                waits=[(scB, sexp)], inc=evB)
        maskd = evB.v

        pb0 = (u % 2) * 512
        trw = [(evB, maskd)]
        if u >= 2:
            trw.append((evB, l_pt_ev[u - 2]))

        def mk_tr(e, pslot=pslot, tt=tt, pb0=pb0):
            ins = None
            if tt > 0:
                ins = e.transpose(ptr[0][64:128, pb0:pb0 + 128], sbPP[:, pslot:pslot + 64],
                                  sbID[:, :])
            ins = e.transpose(ptr[0][:, pb0 + 128:pb0 + 256], sbPP[:, pslot + 64:pslot + 192],
                              sbID[:, :])
            if tt < NT - 1:
                ins = e.transpose(ptr[0][0:64, pb0 + 256:pb0 + 384], sbPP[:, pslot + 192:pslot + 256],
                                  sbID[:, :])
            return ins
        prog.op("tensor", mk_tr, waits=trw, inc=mmB)
        l_tr_mm.append(mmB.v)
        trd = mmB.v

        pw = [(mmB, trd)]
        if u >= 2:
            pw.append((mmB, l_av_mm[u - 2]))
        def ev_pt(e, wslot=wslot, pb0=pb0, tt=tt):
            if tt > 0:
                e.tensor_copy(out=sbWX[64:128, wslot:wslot + 128],
                              in_=ptr[0][64:128, pb0:pb0 + 128])
            ins = e.tensor_copy(out=sbWX[:, wslot + 128:wslot + 256],
                                in_=ptr[0][:, pb0 + 128:pb0 + 256])
            if tt < NT - 1:
                ins = e.tensor_copy(out=sbWX[0:64, wslot + 256:wslot + 384],
                                    in_=ptr[0][0:64, pb0 + 256:pb0 + 384])
            return ins
        prog.op("vector", ev_pt, waits=pw, inc=evB)
        l_pt_ev.append(evB.v)
        ptd = evB.v

        avw = [(evB, ptd)]
        if u >= 2:
            avw.append((evB, l_av_ev[u - 2]))

        def mk_av(e, s=s, hv=hv, tt=tt, wslot=wslot, obank=obank):
            first = True
            ins = None
            if tt > 0:
                ins = e.matmul(obank[:, 0:64], sbWX[64:128, wslot:wslot + 128],
                               sbVT[s][64:128, (tt - 1) * 128 + hv:(tt - 1) * 128 + hv + 64],
                               start=first, stop=False)
                first = False
            ins = e.matmul(obank[:, 0:64], sbWX[:, wslot + 128:wslot + 256],
                           sbVT[s][:, tt * 128 + hv:tt * 128 + hv + 64],
                           start=first, stop=(tt == NT - 1))
            if tt < NT - 1:
                ins = e.matmul(obank[:, 0:64], sbWX[0:64, wslot + 256:wslot + 384],
                               sbVT[s][0:64, (tt + 1) * 128 + hv:(tt + 1) * 128 + hv + 64],
                               start=False, stop=True)
            return ins
        prog.op("tensor", mk_av, waits=avw, inc=mmB)
        l_av_mm.append(mmB.v)
        avd = mmB.v

        prog.op("vector",
                lambda e, obank=obank, tt=tt, h=h:
                    e.tensor_scalar(out=sbAO[:, tt * 1024 + h * 64:tt * 1024 + h * 64 + 64],
                                    in0=obank[:, 0:64], scalar1=sbST2[:, 0:1],
                                    scalar2=None, op0=ALU.mult),
                waits=[(mmB, avd)], inc=evB)
        l_av_ev.append(evB.v)
        lu[0] += 1


    if "Bl" in parts:
        for p in range(2):
            ldv = load_local_pair(p)
            for hh in range(2):
                h = 2 * p + hh
                for tt in range(NT):
                    local_head_tile(p % 2, h, tt, ldv)
            slot_free_at[p % 2] = mmB.v

    sbFEb = sb(ar4, 6400, BF16, 64)
    sbRTb = sb(ar4, 6528, BF16, 64)
    sbRSb = sb(ar4, 6656, F32, 64, rows=64)
    sbREb = sb(ar4, 6912, F32, 64, rows=64)
    sbFE32b = sb(ar4, 7168, F32, 64)
    sbST1b = sb(ar4, 7424, F32, 16)
    sbST2b = sb(ar4, 7488, F32, 16)
    sbST3b = sb(ar4, 7552, F32, 16)
    sbRKV2 = sb(ar3, 24576, BF16, T // 2)
    sbRKV2 = sb(ar3, 24576, BF16, T)

    CTXS = [
        dict(FE=sbFE, FE32=sbFE32, RT=sbRT, RS=sbRS, RE=sbRE,
             ST1=sbST1, ST2=sbST2, ST3=sbST3, RKV=sbRKV, rr=0,
             sbank=pbank[0], obank=pbank[2], rbank=pbank[4], ptrX=ptr[0],
             po=0, vo=0),
        dict(FE=sbFEb, FE32=sbFE32b, RT=sbRTb, RS=sbRSb, RE=sbREb,
             ST1=sbST1b, ST2=sbST2b, ST3=sbST3b, RKV=sbRKV2, rr=64,
             sbank=pbank[1], obank=pbank[3], rbank=pbank[5], ptrX=ptr[1],
             po=512, vo=256),
    ]

    def sink_stages(hs, ldv):
        pi = hs % 2
        C = CTXS[pi]
        s = pi
        h = hs + 4
        rr = C["rr"]
        FE, FE32, RT, RS, RE = C["FE"], C["FE32"], C["RT"], C["RS"], C["RE"]
        ST1, ST2, ST3, RKV = C["ST1"], C["ST2"], C["ST3"], C["RKV"]
        sbank, obank, rbank, ptrX = C["sbank"], C["obank"], C["rbank"], C["ptrX"]
        po, vo = C["po"], C["vo"]

        def mk_f(e, s=s, FE32=FE32, FE=FE):
            e.tensor_reduce(out=FE32[0:64, 0:64],
                            in_=sbQH[s][0:64, :].rearrange("p (v t) -> p v t", v=NB),
                            axis=AX.X, op=ALU.add, opt_input=False, opt_output=False)
            e.tensor_reduce(out=FE32[64:128, 0:64],
                            in_=sbKH[s][0:64, :].rearrange("p (v t) -> p v t", v=NB),
                            axis=AX.X, op=ALU.add, opt_input=False, opt_output=False)
            e.drain()
            return e.tensor_copy(out=FE[:, 0:64], in_=FE32[:, 0:64])
        prog.op("vector", mk_f, waits=[(hld[s], ldv)], inc=evB)
        fd = evB.v
        yield

        prog.op("tensor",
                lambda e, hs=hs, FE=FE, rbank=rbank:
                    e.matmul(rbank[0:64, 0:64], FE[:, 0:64],
                             sbSORT[:, hs * 64:hs * 64 + 64], start=True, stop=True),
                waits=[(evB, fd)], inc=mmB)
        rmm = mmB.v
        yield

        def mk_rel(e, RS=RS, ST1=ST1, rbank=rbank):
            e.tensor_scalar_max(RS[0:64, :], rbank[0:64, 0:64], 0.0)
            e.drain()
            return e.reduce_max(out=ST1[0:64, 1:2], in_=RS[0:64, :], axis=AX.X,
                                negate=True, opt_input=False, opt_output=False)
        prog.op("vector", mk_rel, waits=[(mmB, rmm)], inc=evB)
        reld = evB.v
        yield

        prog.op("scalar",
                lambda e, RE=RE, RS=RS, ST1=ST1:
                    e.activation(out=RE[0:64, :], in_=RS[0:64, :], func=AF.Exp,
                                 bias=ST1[0:64, 1:2], accum_out=ST1[0:64, 3:4]),
                waits=[(evB, reld)], inc=scB)
        red = scB.v
        yield

        def mk_rn(e, FE=FE, RE=RE, ST1=ST1):
            e.reciprocal(ST1[0:64, 4:5], ST1[0:64, 3:4])
            e.drain()
            return e.tensor_scalar(out=FE[0:64, 0:64], in0=RE[0:64, :],
                                   scalar1=ST1[0:64, 4:5], scalar2=None, op0=ALU.mult)
        prog.op("vector", mk_rn, waits=[(scB, red)], inc=evB)
        rnd = evB.v
        yield

        prog.op("tensor",
                lambda e, FE=FE, ptrX=ptrX, rr=rr:
                    e.transpose(ptrX[rr:rr + 64, 0:64], FE[0:64, 0:64],
                                sbID[0:64, 0:64]),
                waits=[(evB, rnd)], inc=mmB)
        rtt = mmB.v
        prog.op("vector",
                lambda e, RT=RT, ptrX=ptrX, rr=rr:
                    e.tensor_copy(out=RT[rr:rr + 64, :], in_=ptrX[rr:rr + 64, 0:64]),
                waits=[(mmB, rtt)], inc=evB)
        rtd = evB.v
        yield

        if sink_stop < 2:
            return
        kv_ev = rtd
        r_ev_prev = 0
        for g in range(8):
            def mk_kvb(e, s=s, g=g, ptrX=ptrX, rr=rr):
                ins = None
                for j in range(8):
                    tp = g * 8 + j
                    ins = e.transpose(ptrX[rr:rr + 64, j * 128:(j + 1) * 128],
                                      sbKH[s].rearrange("p (v t) -> p t v", v=NB)[:, tp, :],
                                      sbID[:, :])
                return ins
            prog.op("tensor", mk_kvb, waits=[(evB, kv_ev)], inc=mmB)
            kv_mm = mmB.v
            prog.op("vector",
                    lambda e, g=g, ptrX=ptrX, rr=rr:
                        e.tensor_copy(out=sbKVB[rr:rr + 64, g * 1024:(g + 1) * 1024],
                                      in_=ptrX[rr:rr + 64, 0:1024]),
                    waits=[(mmB, kv_mm)], inc=evB)
            kv_ev = evB.v
            yield

            def mk_r(e, g=g, rbank=rbank, rr=rr, RT=RT):
                ins = None
                for j in range(8):
                    tp = g * 8 + j
                    ins = e.matmul(rbank[:, j * 64:(j + 1) * 64],
                                   sbKVB[rr:rr + 64, tp * 128:(tp + 1) * 128],
                                   RT[rr:rr + 64, :], start=True, stop=True)
                return ins
            rw = [(evB, kv_ev)]
            if r_ev_prev > 0:
                rw.append((evB, r_ev_prev))
            prog.op("tensor", mk_r, waits=rw, inc=mmB)
            r_mm = mmB.v

            def ev_r(e, g=g, rbank=rbank, RKV=RKV):
                src = rbank[:, 0:512].rearrange("p (j u) -> p j u", j=8)
                dst = RKV.rearrange("p (u t) -> p t u", u=NB)[:, g * 8:(g + 1) * 8, :]
                return e.tensor_copy(out=dst, in_=src)
            prog.op("vector", ev_r, waits=[(mmB, r_mm)], inc=evB)
            r_ev_prev = evB.v
            yield
        rkvd = evB.v

        if sink_stop < 4:
            return
        P2T = sbWX
        p2to = pi * 2048
        prev_exp = 0
        prev_tr = 0
        prev_et = 0
        prev_av = 0
        prev_ev = 0
        for b in range(16):
            t0 = 2 * b
            def mk_s2(e, s=s, b=b, t0=t0, sbank=sbank, RKV=RKV):
                ins = None
                for half in range(2):
                    tl = t0 + half
                    q_l = sbQH[s][0:64, tl * 128:(tl + 1) * 128]
                    ins = e.matmul(sbank[:, half * 256:half * 256 + 128],
                                   q_l, sbKH[s][0:64, tl * 128:(tl + 1) * 128],
                                   start=True, stop=True)
                    ins = e.matmul(sbank[:, half * 256 + 128:half * 256 + 256],
                                   q_l, RKV[0:64, tl * 128:(tl + 1) * 128],
                                   start=True, stop=True)
                return ins
            sw2 = [(evB, rkvd)]
            if prev_exp > 0:
                sw2.append((scB, prev_exp))
            prog.op("tensor", mk_s2, waits=sw2, inc=mmB)
            s2mm = mmB.v
            yield

            prog.op("vector",
                    lambda e, sbank=sbank:
                        e.tensor_tensor(sbank[:, 0:512], sbank[:, 0:512], sbMKS[:, :],
                                        ALU.add),
                    waits=[(mmB, s2mm)], inc=evB)
            mz2d = evB.v

            xw2 = [(evB, mz2d)]
            if prev_tr > 0:
                xw2.append((mmB, prev_tr))

            def mk_ex2(e, sbank=sbank, po=po, b=b, ST2=ST2):
                e.activation(out=sbP2[:, po:po + 256], in_=sbank[:, 0:256],
                             func=AF.Exp, scale=0.125,
                             accum_out=ST2[:, (b % 2) * 2:(b % 2) * 2 + 1])
                return e.activation(out=sbP2[:, po + 256:po + 512], in_=sbank[:, 256:512],
                                    func=AF.Exp, scale=0.125,
                                    accum_out=ST2[:, (b % 2) * 2 + 1:(b % 2) * 2 + 2])
            prog.op("scalar", mk_ex2, waits=xw2, inc=scB)
            prev_exp = scB.v
            s2exp = scB.v
            yield

            prog.op("vector",
                    lambda e, b=b, ST2=ST2, ST3=ST3:
                        e.reciprocal(ST3[:, (b % 2) * 2:(b % 2) * 2 + 2],
                                     ST2[:, (b % 2) * 2:(b % 2) * 2 + 2]),
                    waits=[(scB, s2exp)], inc=evB)
            sumd = evB.v
            yield

            def mk_t2(e, b=b, po=po, ptrX=ptrX, RKV=RKV):
                ins = None
                for blk in range(4):
                    ins = e.transpose(ptrX[:, blk * 128:(blk + 1) * 128],
                                      sbP2[:, po + blk * 128:po + (blk + 1) * 128],
                                      sbID[:, :])
                for j in range(4):
                    u = 4 * b + j
                    rbase = (u % 2) * 64
                    half = j // 2
                    ins = e.transpose(ptrX[rbase:rbase + 64, 512 + half * 64:512 + (half + 1) * 64],
                                      RKV[64:128, u * 64:(u + 1) * 64],
                                      sbID[64:128, 64:128])
                return ins
            t2w = [(evB, sumd)]
            if prev_et > 0:
                t2w.append((evB, prev_et))
            prog.op("tensor", mk_t2, waits=t2w, inc=mmB)
            prev_tr = mmB.v
            t2d = mmB.v
            yield

            def ev_t2(e, ptrX=ptrX, p2to=p2to):
                e.tensor_copy(out=P2T[:, p2to:p2to + 512], in_=ptrX[:, 0:512])
                return e.tensor_copy(out=P2T[:, p2to + 512:p2to + 640],
                                     in_=ptrX[:, 512:640])
            ew2 = [(mmB, t2d)]
            if prev_av > 0:
                ew2.append((mmB, prev_av))
            prog.op("vector", ev_t2, waits=ew2, inc=evB)
            prev_et = evB.v
            t2e = evB.v
            yield

            def mk_v2(e, s=s, t0=t0, obank=obank, p2to=p2to, h=h):
                hv = (h % 2) * 64
                ins = None
                for half in range(2):
                    tl = t0 + half
                    oc = half * 64
                    ins = e.matmul(obank[:, oc:oc + 64],
                                   P2T[:, p2to + half * 256:p2to + half * 256 + 128],
                                   sbVT[s][:, tl * 128 + hv:tl * 128 + hv + 64],
                                   start=True, stop=False)
                    ins = e.matmul(obank[:, oc:oc + 64],
                                   P2T[:, p2to + half * 256 + 128:p2to + half * 256 + 256],
                                   P2T[:, p2to + 512 + half * 64:p2to + 512 + (half + 1) * 64],
                                   start=False, stop=True)
                return ins
            avw = [(evB, t2e)]
            if prev_ev > 0:
                avw.append((evB, prev_ev))
            prog.op("tensor", mk_v2, waits=avw, inc=mmB)
            prev_av = mmB.v
            v2d = mmB.v
            yield

            def ev_v2(e, b=b, t0=t0, h=h, obank=obank, ST3=ST3):
                ins = None
                for half in range(2):
                    tl = t0 + half
                    col = tl * 1024 + h * 64
                    ins = e.tensor_scalar(
                        out=sbAO[:, col:col + 64],
                        in0=obank[:, half * 64:half * 64 + 64],
                        scalar1=ST3[:, (b % 2) * 2 + half:(b % 2) * 2 + half + 1],
                        scalar2=None, op0=ALU.mult)
                return ins
            prog.op("vector", ev_v2, waits=[(mmB, v2d)], inc=evB)
            prev_ev = evB.v
            yield

    if "Bs" in parts:
        for hp in range(SHH // 2):
            h0, h1 = 2 * hp, 2 * hp + 1
            ld0 = load_sink_head(h0)
            ld1 = load_sink_head(h1)
            g0 = sink_stages(h0, ld0)
            g1 = sink_stages(h1, ld1)
            alive = [g0, g1]
            while alive:
                for g in list(alive):
                    try:
                        next(g)
                    except StopIteration:
                        alive.remove(g)
            slot_free_at[0] = mmB.v
            slot_free_at[1] = mmB.v

    phaseB_ev = evB.v

    dbg_spill_done = 0
    if dbg_ao is not None:
        for tt in range(NT):
            prog.dma("sync", dbg_ao[tt * 128:(tt + 1) * 128, :],
                     sbAO[:, tt * 1024:(tt + 1) * 1024], S["gst"],
                     waits=[(evB, phaseB_ev)] if tt == 0 else ())
        dbg_spill_done = S["gst"].v

    mmC, evC, scC = S["mmC"], S["evC"], S["scC"]
    xld, x1t, x1f = S["xld"], S["x1t"], S["x1f"]

    if "C" in parts:
        c_ev2 = []
        c_ev5 = []
        c_spl = []
        for tc in range(NT):
            lw = [(evB, phaseB_ev)] if tc == 0 else [(evC, c_ev2[tc - 1])]
            prog.dma("sync", sbCx[:, :], x_in[tc * 128:(tc + 1) * 128, :], xld,
                     waits=lw)
            xv = xld.v

            tw = [(evB, phaseB_ev)] if tc == 0 else [(evC, c_ev5[tc - 1])]

            def mk_c1(e, tc=tc):
                ins = None
                for c in range(8):
                    ins = e.transpose(ptr[0][:, c * 128:(c + 1) * 128],
                                      sbAO[:, tc * 1024 + c * 128:tc * 1024 + (c + 1) * 128],
                                      sbID[:, :])
                return ins
            prog.op("tensor", mk_c1, waits=tw, inc=mmC)
            c1d = mmC.v

            prog.op("vector", lambda e: e.tensor_copy(out=sbCa[:, :], in_=ptr[0][:, :1024]),
                    waits=[(mmC, c1d)], inc=evC)
            cad = evC.v

            pw = [(evC, cad), (S["gld2"], setup_g2)]
            if tc >= 1:
                pw.append((evC, c_ev2[tc - 1]))

            def mk_c2(e):
                ins = None
                for dh2 in range(2):
                    for c in range(8):
                        ins = e.matmul(pbank[dh2][:, :],
                                       sbCa[:, c * 128:(c + 1) * 128],
                                       sbWO[:, c * 1024 + dh2 * 512:c * 1024 + dh2 * 512 + 512],
                                       start=(c == 0), stop=False)
                    ins = e.matmul(pbank[dh2][:, :], sbONE[:, :],
                                   sbBO[0:1, dh2 * 512:(dh2 + 1) * 512],
                                   start=False, stop=True)
                return ins
            prog.op("tensor", mk_c2, waits=pw, inc=mmC)
            c2d = mmC.v

            def mk_c3(e):
                e.tensor_tensor(sbCz[:, 0:512], pbank[0][:, :], sbCx[:, 0:512], ALU.add)
                e.tensor_tensor(sbCz[:, 512:1024], pbank[1][:, :], sbCx[:, 512:1024], ALU.add)
                e.drain()
                return e.tensor_reduce(out=sbLN[:, 0:1], in_=sbCz[:, :], axis=AX.X,
                                       op=ALU.add, opt_input=False, opt_output=False)
            prog.op("vector", mk_c3, waits=[(mmC, c2d), (xld, xv)], inc=evC)
            c_ev2.append(evC.v)
            c3d = evC.v

            sqw = [(evC, c3d)]
            if tc >= 1:
                sqw.append((x1t, c_spl[tc - 1][0]))
            prog.op("scalar",
                    lambda e: e.activation(out=sbCsq[:, :], in_=sbCz[:, :], func=AF.Square,
                                           accum_out=sbLN[:, 1:2]),
                    waits=sqw, inc=scC)
            csq = scC.v

            def mk_c4(e, tc=tc):
                e.tensor_scalar_mul(sbLN[:, 2:3], sbLN[:, 0:1], 1.0 / 1024.0)
                e.tensor_scalar_mul(sbLN[:, 3:4], sbLN[:, 1:2], 1.0 / 1024.0)
                e.drain()
                e.scalar_tensor_tensor(out=sbLN[:, 4:5], in0=sbLN[:, 2:3],
                                       scalar=sbLN[:, 2:3], in1=sbLN[:, 3:4],
                                       op0=ALU.mult, op1=ALU.subtract)
                e.drain()
                return e.scalar_tensor_tensor(out=sbLN[:, 5:6], in0=sbLN[:, 4:5],
                                              scalar=-1.0, in1=sbEPSC[:, :],
                                              op0=ALU.mult, op1=ALU.add)
            prog.op("vector", mk_c4, waits=[(scC, csq)], inc=evC)
            c4d = evC.v

            prog.op("scalar",
                    lambda e: e.activation(out=sbLN[:, 6:7], in_=sbLN[:, 5:6],
                                           func=AF.Sqrt),
                    waits=[(evC, c4d)], inc=scC)
            cstd = scC.v

            def mk_c5(e):
                e.reciprocal(sbLN[:, 8:9], sbLN[:, 6:7])
                e.drain()
                e.tensor_scalar(out=sbCsq[:, :], in0=sbCz[:, :], scalar1=sbLN[:, 2:3],
                                scalar2=sbLN[:, 8:9], op0=ALU.subtract, op1=ALU.mult)
                return e.tensor_scalar(out=sbCx1[:, :], in0=sbCz[:, :], scalar1=sbLN[:, 2:3],
                                       scalar2=sbLN[:, 8:9], op0=ALU.subtract, op1=ALU.mult)
            prog.op("vector", mk_c5, waits=[(scC, cstd)], inc=evC)
            c5d = evC.v

            prog.dma("sync", scr_x1t[tc * 128:(tc + 1) * 128, :], sbCsq[:, :], x1t,
                     waits=[(evC, c5d)])

            def mk_c6(e):
                ins = None
                for c in range(8):
                    ins = e.transpose(ptr[0][:, c * 128:(c + 1) * 128],
                                      sbCx1[:, c * 128:(c + 1) * 128], sbID[:, :])
                return ins
            prog.op("tensor", mk_c6, waits=[(evC, c5d)], inc=mmC)
            c6d = mmC.v

            evw2 = [(mmC, c6d)]
            if tc >= 1:
                evw2.append((x1f, c_spl[tc - 1][1]))
            prog.op("vector", lambda e: e.tensor_copy(out=sbCxf[:, :], in_=ptr[0][:, :1024]),
                    waits=evw2, inc=evC)
            c_ev5.append(evC.v)
            c5e = evC.v

            for c in range(8):
                prog.dma("sync", scr_x1f[c, :, tc * 128:(tc + 1) * 128],
                         sbCxf[:, c * 128:(c + 1) * 128], x1f,
                         waits=[(evC, c5e)] if c == 0 else ())
            c_spl.append((x1t.v, x1f.v))

        phaseC_done = [(x1t, x1t.v), (x1f, x1f.v)]


    if "D" in parts:
        mmD, evD, scD = S["mmD"], S["evD"], S["scD"]
        wld = [S["wld0"], S["wld1"]]
        xq, xt2, outd_ = S["xq"], S["xt2"], S["outd"]
        x1t_total = S["x1t"].v
        wld_free_at = [0, 0]
        up_ev = []
        up_sc = []
        dn_ev = []
        prev_sig = 0
        q_x1q_free = 0
        pending_epilogue = []
        epi_sq_guard = [0]

        def load_expert_weights(e_, first):
            sl = e_ % 2
            w = []
            if first:
                w = [(S["pcs0"], precast_done[0][1]), (S["pcs1"], precast_done[1][1])]
                if dbg_spill_done:
                    w.append((S["gst"], dbg_spill_done))
            if wld_free_at[sl] > 0:
                w.append((mmD, wld_free_at[sl]))
            for c in range(8):
                prog.dma("gpsimd", sbWU[sl][:, c * HID:(c + 1) * HID],
                         scr_wub[e_, c * 128:(c + 1) * 128, :], wld[sl], waits=w)
            for ch in range(16):
                prog.dma("gpsimd", sbWD[sl][:, ch * 1024:(ch + 1) * 1024],
                         scr_wdb[e_, ch * 128:(ch + 1) * 128, :], wld[sl], waits=w)
            return wld[sl].v

        for tq in range(4):
            xw = list(phaseC_done) if tq == 0 else []
            if q_x1q_free > 0:
                xw.append((mmD, q_x1q_free))
            for c in range(8):
                prog.dma("gpsimd", sbX1Q[:, c * 1024:(c + 1) * 1024],
                         scr_x1f[c, :, tq * 1024:(tq + 1) * 1024], xq, waits=xw)
            xqv = xq.v

            def mk_g(e):
                ins_ = None
                for ti in range(8):
                    for c in range(8):
                        ins_ = e.matmul(pbank[4][:, ti * 8:(ti + 1) * 8],
                                        sbX1Q[:, c * 1024 + ti * 128:c * 1024 + (ti + 1) * 128],
                                        sbGW[:, c * 8:(c + 1) * 8],
                                        start=(c == 0), stop=(c == 7))
                return ins_
            gw_ = [(xq, xqv)]
            if prev_sig > 0:
                gw_.append((scD, prev_sig))
            prog.op("tensor", mk_g, waits=gw_, inc=mmD)
            gmm = mmD.v

            prog.op("scalar",
                    lambda e: e.activation(out=sbGS[:, :], in_=pbank[4][:, 0:64],
                                           func=AF.Sigmoid),
                    waits=[(mmD, gmm)], inc=scD)
            prev_sig = scD.v

            def mk_w2(e):
                bias_b = sbEB[:, 0:8].unsqueeze(1).broadcast_to([128, 8, 8])
                e.tensor_tensor(sbGB[:, :].rearrange("p (t e) -> p t e", t=8),
                                sbGS[:, :].rearrange("p (t e) -> p t e", t=8),
                                bias_b, ALU.add)
                e.drain()
                ins_ = None
                for ti in range(8):
                    gb = sbGB[:, ti * 8:(ti + 1) * 8]
                    e.reduce_max(out=sbTOP[:, 0:1], in_=gb, axis=AX.X,
                                 opt_input=False, opt_output=False)
                    e.drain()
                    e.tensor_scalar(out=sbW2[:, ti * 8:(ti + 1) * 8], in0=gb,
                                    scalar1=sbTOP[:, 0:1], scalar2=None, op0=ALU.is_ge)
                    e.drain()
                    e.scalar_tensor_tensor(out=sbW2[:, ti * 8:(ti + 1) * 8],
                                           in0=sbW2[:, ti * 8:(ti + 1) * 8],
                                           scalar=-1.0e30, in1=gb,
                                           op0=ALU.mult, op1=ALU.add)
                    e.drain()
                    e.reduce_max(out=sbTOP[:, 1:2], in_=sbW2[:, ti * 8:(ti + 1) * 8],
                                 axis=AX.X, opt_input=False, opt_output=False)
                    e.drain()
                    e.tensor_scalar(out=sbW2[:, ti * 8:(ti + 1) * 8], in0=gb,
                                    scalar1=sbTOP[:, 1:2], scalar2=None, op0=ALU.is_ge)
                    e.drain()
                    e.tensor_tensor(sbW2[:, ti * 8:(ti + 1) * 8],
                                    sbW2[:, ti * 8:(ti + 1) * 8],
                                    sbGS[:, ti * 8:(ti + 1) * 8], ALU.mult)
                    e.drain()
                    ins_ = e.tensor_tensor(sbW2[:, ti * 8:(ti + 1) * 8],
                                           sbW2[:, ti * 8:(ti + 1) * 8],
                                           sbW2[:, ti * 8:(ti + 1) * 8], ALU.mult)
                return ins_
            w2w_ = [(scD, prev_sig)]
            if dbg_w2 is not None and S["gst"].v > 0:
                w2w_.append((S["gst"], S["gst"].v))
            prog.op("vector", mk_w2, waits=w2w_, inc=evD)
            w2d = evD.v

            if pending_epilogue:
                epi_sq_guard[0] = pending_epilogue.pop()()

            if dbg_w2 is not None:
                for ti in range(8):
                    prog.dma("sync", dbg_w2[tq * 1024 + ti * 128:tq * 1024 + (ti + 1) * 128, :],
                             sbW2[:, ti * 8:(ti + 1) * 8], S["gst"],
                             waits=[(evD, w2d)])

            wld_vals = {}
            wld_vals[0] = load_expert_weights(0, first=(tq == 0))
            wld_vals[1] = load_expert_weights(1, first=(tq == 0))

            for e_ in range(NE):
                sl = e_ % 2
                ldv = wld_vals[e_]
                for sl2 in range(2):
                    for jc in range(16):
                        bank = pbank[jc % 2]
                        def mk_up(e, sl=sl, sl2=sl2, jc=jc, bank=bank):
                            ins_ = None
                            for dc in range(8):
                                ins_ = e.matmul(bank[:, :],
                                                sbWU[sl][:, dc * HID + jc * 128:dc * HID + jc * 128 + 128],
                                                sbX1Q[:, dc * 1024 + sl2 * 512:dc * 1024 + sl2 * 512 + 512],
                                                start=(dc == 0), stop=(dc == 7))
                            return ins_
                        uw = [(wld[sl], ldv), (xq, xqv)]
                        if len(up_sc) >= 2:
                            uw.append((scD, up_sc[-2]))
                        prog.op("tensor", mk_up, waits=uw, inc=mmD)
                        umm = mmD.v
                        rw = [(mmD, umm)]
                        if len(up_ev) >= 2:
                            rw.append((evD, up_ev[-2]))
                        prog.op("scalar",
                                lambda e, jc=jc, bank=bank:
                                    e.activation(out=sbHR[jc % 2][:, :], in_=bank[:, :],
                                                 func=AF.Relu),
                                waits=rw, inc=scD)
                        up_sc.append(scD.v)
                        if e_ == 0 and sl2 == 0:
                            sqg = [(scD, up_sc[-1])]
                            if jc == 0 and epi_sq_guard[0] != 0:
                                sqg.append((outd_, epi_sq_guard[0][1]))
                            prog.op("scalar",
                                    lambda e, jc=jc:
                                        e.activation(out=sbHE[:, jc * 512:(jc + 1) * 512],
                                                     in_=sbHR[jc % 2][:, :],
                                                     func=AF.Square),
                                    waits=sqg, inc=scD)
                            up_sc[-1] = scD.v
                        else:
                            sqg = [(scD, up_sc[-1])]
                            if e_ == 0 and epi_sq_guard[0] != 0:
                                sqg.append((scD, epi_sq_guard[0][0]))
                                sqg.append((outd_, epi_sq_guard[0][1]))
                            prog.op("vector",
                                    lambda e, jc=jc:
                                        e.tensor_tensor(sbHE[:, jc * 512:(jc + 1) * 512],
                                                        sbHR[jc % 2][:, :], sbHR[jc % 2][:, :],
                                                        ALU.mult),
                                    waits=sqg, inc=evD)
                            up_ev.append(evD.v)
                    h_ready_w = (scD, scD.v) if (e_ == 0 and sl2 == 0) else (evD, evD.v)
                    h_ready = evD.v
                    for cg in range(4):
                        for dh2 in range(2):
                            bank = pbank[2 + (cg * 2 + dh2) % 2]
                            def mk_dn(e, sl=sl, cg=cg, dh2=dh2, bank=bank):
                                ins_ = None
                                for jc in range(16):
                                    ins_ = e.matmul(bank[:, :],
                                                    sbHE[:, jc * 512 + cg * 128:jc * 512 + cg * 128 + 128],
                                                    sbWD[sl][:, jc * 1024 + dh2 * 512:jc * 1024 + dh2 * 512 + 512],
                                                    start=(jc == 0), stop=(jc == 15))
                                return ins_
                            dw = [h_ready_w, (wld[sl], ldv)]
                            if len(dn_ev) >= 2:
                                dw.append((evD, dn_ev[-2]))
                            prog.op("tensor", mk_dn, waits=dw, inc=mmD)
                            dmm = mmD.v
                            ti = sl2 * 4 + cg
                            def mk_acc(e, bank=bank, ti=ti, e_=e_, dh2=dh2):
                                dst = sbACC[:, ti * 1024 + dh2 * 512:ti * 1024 + dh2 * 512 + 512]
                                if e_ == 0:
                                    return e.tensor_scalar(out=dst, in0=bank[:, :],
                                                           scalar1=sbW2[:, ti * 8:ti * 8 + 1],
                                                           scalar2=None, op0=ALU.mult)
                                return e.scalar_tensor_tensor(out=dst, in0=bank[:, :],
                                                              scalar=sbW2[:, ti * 8 + e_:ti * 8 + e_ + 1],
                                                              in1=dst, op0=ALU.mult, op1=ALU.add)
                            prog.op("vector", mk_acc, waits=[(mmD, dmm), (evD, w2d)], inc=evD)
                            dn_ev.append(evD.v)
                if e_ + 2 < NE:
                    wld_free_at[sl] = mmD.v
                    wld_vals[e_ + 2] = load_expert_weights(e_ + 2, first=False)
                else:
                    wld_free_at[sl] = mmD.v
            q_x1q_free = mmD.v

            def plan_epilogue(tq=tq):
                slots = [sbX1T, sbO]
                last_sq = [0, 0]
                last_ap = [0, 0]
                for pg in range(4):
                    zd, sqd_, stdd, sqr2 = [0, 0], [0, 0], [0, 0], [0, 0]
                    for j in range(2):
                        ti = pg * 2 + j
                        trow = tq * 1024 + ti * 128
                        lw = [(outd_, outd_.v)]
                        if last_ap[j] > 0:
                            lw.append((evD, last_ap[j]))
                        if tq == 0 and ti == 0:
                            lw.append((S["x1t"], x1t_total))
                        prog.dma("sync", slots[j][:, :], scr_x1t[trow:trow + 128, :],
                                 xt2, waits=lw)
                    xtv = xt2.v
                    for j in range(2):
                        ti = pg * 2 + j
                        prog.op("vector",
                                lambda e, ti=ti, j=j:
                                    e.tensor_tensor(slots[j][:, :],
                                                    sbACC[:, ti * 1024:(ti + 1) * 1024],
                                                    slots[j][:, :], ALU.add),
                                waits=[(xt2, xtv)] + ([(evD, last_ap[j])] if last_ap[j] else []),
                                inc=evD)
                        zd[j] = evD.v
                    sum_v = [0, 0]
                    for j in range(2):
                        prog.op("vector",
                                lambda e, j=j:
                                    e.tensor_reduce(out=sbLN[:, j * 8:j * 8 + 1],
                                                    in_=slots[j][:, :], axis=AX.X,
                                                    op=ALU.add, opt_input=False,
                                                    opt_output=False),
                                waits=[(evD, zd[j])], inc=evD)
                        sum_v[j] = evD.v
                        sqw = [(evD, zd[j])]
                        sqw.append((outd_, outd_.v))
                        prog.op("scalar",
                                lambda e, j=j:
                                    e.activation(out=sbEPQ[j][:, :], in_=slots[j][:, :],
                                                 func=AF.Square,
                                                 accum_out=sbLN[:, j * 8 + 1:j * 8 + 2]),
                                waits=sqw, inc=scD)
                        sqd_[j] = scD.v
                        last_sq[j] = scD.v
                    for j in range(2):
                        def mk_st(e, j=j):
                            e.tensor_scalar_mul(sbLN[:, j * 8 + 2:j * 8 + 3],
                                                sbLN[:, j * 8:j * 8 + 1], 1.0 / 1024.0)
                            e.tensor_scalar_mul(sbLN[:, j * 8 + 3:j * 8 + 4],
                                                sbLN[:, j * 8 + 1:j * 8 + 2], 1.0 / 1024.0)
                            e.drain()
                            e.scalar_tensor_tensor(out=sbLN[:, j * 8 + 4:j * 8 + 5],
                                                   in0=sbLN[:, j * 8 + 2:j * 8 + 3],
                                                   scalar=sbLN[:, j * 8 + 2:j * 8 + 3],
                                                   in1=sbLN[:, j * 8 + 3:j * 8 + 4],
                                                   op0=ALU.mult, op1=ALU.subtract)
                            e.drain()
                            return e.scalar_tensor_tensor(out=sbLN[:, j * 8 + 5:j * 8 + 6],
                                                          in0=sbLN[:, j * 8 + 4:j * 8 + 5],
                                                          scalar=-1.0, in1=sbEPSC[:, :],
                                                          op0=ALU.mult, op1=ALU.add)
                        prog.op("vector", mk_st, waits=[(scD, sqd_[j]), (evD, sum_v[j])],
                                inc=evD)
                        stdd[j] = evD.v
                    for j in range(2):
                        prog.op("scalar",
                                lambda e, j=j:
                                    e.activation(out=sbLN[:, j * 8 + 6:j * 8 + 7],
                                                 in_=sbLN[:, j * 8 + 5:j * 8 + 6],
                                                 func=AF.Sqrt),
                                waits=[(evD, stdd[j])], inc=scD)
                        sqr2[j] = scD.v
                    for j in range(2):
                        ti = pg * 2 + j
                        trow = tq * 1024 + ti * 128
                        def mk_ap(e, j=j):
                            e.reciprocal(sbLN[:, j * 8 + 7:j * 8 + 8],
                                         sbLN[:, j * 8 + 6:j * 8 + 7])
                            e.drain()
                            return e.tensor_scalar(out=sbEPQ[j][:, :], in0=slots[j][:, :],
                                                   scalar1=sbLN[:, j * 8 + 2:j * 8 + 3],
                                                   scalar2=sbLN[:, j * 8 + 7:j * 8 + 8],
                                                   op0=ALU.subtract, op1=ALU.mult)
                        prog.op("vector", mk_ap, waits=[(scD, sqr2[j])], inc=evD)
                        apd = evD.v
                        last_ap[j] = apd
                        prog.dma("sync", out[trow:trow + 128, :], sbEPQ[j][:, :], outd_,
                                 waits=[(evD, apd)])
                return (max(last_sq), outd_.v)
            pending_epilogue.append(plan_epilogue)

        if pending_epilogue:
            pending_epilogue.pop()()

    outd = S["outd"]
    if "D" not in parts:
        src = scr_x1t if "C" in parts else x_in
        for tt in range(NT):
            lw = [(outd, outd.v)]
            if tt == 0:
                lw.extend(phaseA_done)
                if "C" in parts:
                    lw.extend(phaseC_done)
            prog.dma("sync", sbO[:, :], src[tt * 128:(tt + 1) * 128, :], S["sld"], waits=lw)
            prog.dma("sync", out[tt * 128:(tt + 1) * 128, :], sbO[:, :], outd,
                     waits=[(S["sld"], S["sld"].v)])
    prog.wait("sync", outd, outd.v)

    with nc.Block() as block:
        for eng in ENGINES:
            ops = prog.ops[eng]
            if not ops:
                continue
            def body(e, ops=ops):
                for fn in ops:
                    fn(e)
            getattr(block, eng)(body)
    estack.close()
    return nc


_NC_CACHE = {}


def _get_nc(debug=()):
    key = tuple(sorted(debug))
    if key not in _NC_CACHE:
        _NC_CACHE[key] = build_nc(debug=key)
    return _NC_CACHE[key]


def _shard_inputs(inputs):
    x = np.ascontiguousarray(inputs["L0_attn_x"])
    wq = np.ascontiguousarray(inputs["L0_attn_w_q"])
    wkv = np.ascontiguousarray(inputs["L0_attn_w_kv"])
    wo = np.ascontiguousarray(inputs["L0_attn_w_out"])
    bo = np.ascontiguousarray(inputs["L0_attn_b_out"])
    sortw = np.ascontiguousarray(inputs["L0_attn_sort_w"]).reshape(SHH, 2 * DHH, NB)
    gw = np.ascontiguousarray(inputs["L0_ffn_gate_w"])
    wu = np.ascontiguousarray(inputs["L0_ffn_w_up"])
    wd = np.ascontiguousarray(inputs["L0_ffn_w_down"])
    eb = np.ascontiguousarray(inputs["L0_ffn_expert_bias"])

    gwT = np.ascontiguousarray(gw.T)
    wuT = np.ascontiguousarray(wu.transpose(0, 2, 1))
    wdT = np.ascontiguousarray(wd.transpose(0, 2, 1))
    ebr = np.ascontiguousarray(np.broadcast_to(eb[None, :], (128, NE)))

    in_maps = []
    for c in range(8):
        xc = np.ascontiguousarray(x[c])
        in_maps.append({
            "xT": np.ascontiguousarray(xc.T),
            "x_in": xc,
            "wq": wq, "wkv": wkv, "wo": wo, "bo": bo,
            "sortw": sortw, "gwT": gwT, "ebias": ebr,
            "wuT": wuT, "wdT": wdT,
        })
    return in_maps


def kernel(**inputs):
    nc = _get_nc()
    in_maps = _shard_inputs(inputs)
    res = run_bass_kernel_spmd(nc, in_maps, core_ids=list(range(8)))
    return np.stack([r["out"] for r in res.results], axis=0)


def run_debug(inputs, debug, parts=("A", "Bl", "Bs", "C", "D"), cores=8, sink_stop=99):
    nc = build_nc(debug=tuple(debug), parts=parts, sink_stop=sink_stop)
    in_maps = _shard_inputs(inputs)[:cores]
    res = run_bass_kernel_spmd(nc, in_maps, core_ids=list(range(cores)))
    return res
